# version 01
# baseline (speedup 1.0000x reference)
import dataclasses
from contextlib import ExitStack
import numpy as np
import ml_dtypes
import concourse.bass as bass
import concourse.mybir as mybir
from concourse.bass_utils import run_bass_kernel_spmd

F32 = mybir.dt.float32
BF16 = mybir.dt.bfloat16
AF = mybir.ActivationFunctionType
AX = mybir.AxisListType
OP = mybir.AluOpType

D = 1024
S = 16
L = 16384
E = 8
II = 4096
NCORE = 8
T = 2048
TC = 128
J = L // TC
JOWN0 = J - T // TC
EPS = 1e-5
LOG_LO = float(np.log(np.float64(1e-8)))
LOG_HI = float(np.log1p(np.float64(-1e-8)))
DTL, DTH = 1e-4, 1.0
NEG_BIG = -1.0e30


class Sem:
    def __init__(self, nc, name):
        self.h = nc.alloc_semaphore(name)
        self.n = 0


class Prog:
    ENGINES = ("tensor", "vector", "scalar", "gpsimd", "sync")

    def __init__(self):
        self.items = {e: [] for e in self.ENGINES}

    def emit(self, eng, fn, waits=(), incs=()):
        waits = [(s.h, n) for (s, n) in waits if n > 0]
        snap = []
        for (s, k) in incs:
            s.n += k
            snap.append((s.h, k))

        def closure(engine, _w=waits, _f=fn, _i=snap):
            for (h, n) in _w:
                engine.wait_ge(h, n)
            ins = _f(engine)
            if _i:
                ins.then_inc(*_i[0])
                for (h, k) in _i[1:]:
                    engine.nop().then_inc(h, k)
        self.items[eng].append(closure)

    def t(self, fn, **kw): self.emit("tensor", fn, **kw)
    def v(self, fn, **kw): self.emit("vector", fn, **kw)
    def s(self, fn, **kw): self.emit("scalar", fn, **kw)
    def g(self, fn, **kw): self.emit("gpsimd", fn, **kw)
    def sp(self, fn, **kw): self.emit("sync", fn, **kw)

    def run(self, nc):
        with nc.Block() as block:
            for e in self.ENGINES:
                lst = self.items[e]
                if not lst:
                    continue
                getattr(block, e)(lambda eng, lst=lst: [f(eng) for f in lst])


def dram_ap(ap0, offset, pattern):
    return dataclasses.replace(ap0, offset=offset, ap=pattern)


def build(debug=False, max_phase=9):
    nc = bass.Bass("TRN2", target_bir_lowering=False, detect_race_conditions=False)

    din = {}
    def di(name, shape, dt=F32):
        din[name] = nc.dram_tensor(name, list(shape), dt, kind="ExternalInput").ap()
    di("xt", (D, L))
    di("mask_cm", (128, J))
    di("w_in_t", (D, 2 * D))
    di("w_in_upn", (D, D))
    di("w_out_t", (D, D))
    di("w_b_t", (D, S))
    di("w_c_t", (S, D))
    di("gate_t", (D, E))
    di("wg_t", (E, D, II))
    di("wu_t", (E, D, II))
    di("wd_t", (E, II, D))
    di("g_fat", (128, 8)); di("b_fat", (128, 8)); di("d_fat", (128, 8))
    di("logdt_fat", (128, 8)); di("bout_fat", (128, 8)); di("bin_fat", (128, 16))
    di("binup_col", (128, 8)); di("lnb_col", (128, 8))
    di("alog_fat", (128, 128))

    outT_d = nc.dram_tensor("outT", [D, T], F32, kind="ExternalOutput").ap()
    dbg = {}
    def do_(name, shape, dt=F32):
        if debug:
            dbg[name] = nc.dram_tensor(name, list(shape), dt, kind="ExternalOutput").ap()
    do_("dbg_bcm", (128, S, J), BF16)
    do_("dbg_mu", (128, J)); do_("dbg_rs1", (128, J)); do_("dbg_q2", (128, J))
    do_("dbg_xm", (S, T))
    do_("dbg_xnt", (128, 8, T), BF16)
    do_("dbg_comb8", (E, T), BF16)
    do_("dbg_yt", (128, 8, T), BF16)
    do_("dbg_logafat", (128, 128))
    do_("dbg_statw", (128, 8, 18), BF16)
    do_("dbg_rows", (1, 64))
    do_("dbg_cbm", (128, S)); do_("dbg_negsw", (128, S))
    do_("dbg_pcm", (128, S, 128))
    do_("dbg_sown", (128, 8, S, 17))

    ident_bf_d = nc.inline_tensor(np.eye(128, dtype=ml_dtypes.bfloat16), name="c_ident_bf").ap()
    ident_f32_d = nc.inline_tensor(np.eye(128, dtype=np.float32), name="c_ident_f32").ap()
    jrev_np = np.zeros((128, 128), dtype=ml_dtypes.bfloat16)
    for c in range(128):
        jrev_np[127 - c, c] = 1.0
    jrev_d = nc.inline_tensor(jrev_np, name="c_jrev").ap()
    ones_r_b_d = nc.inline_tensor(np.ones((1, 128), dtype=ml_dtypes.bfloat16), name="c_ones_r_b").ap()
    ones_r_f_d = nc.inline_tensor(np.ones((1, 128), dtype=np.float32), name="c_ones_r_f").ap()
    ones_c_b_d = nc.inline_tensor(np.ones((128, 1), dtype=ml_dtypes.bfloat16), name="c_ones_c_b").ap()
    ones_c_f_d = nc.inline_tensor(np.ones((128, 1), dtype=np.float32), name="c_ones_c_f").ap()
    iota_p1_d = nc.inline_tensor(np.broadcast_to(np.arange(1, 129, dtype=np.float32),
                                                 (128, 128)).copy(), name="c_iota_p1").ap()
    iota_r2_d = nc.inline_tensor(np.broadcast_to(127.0 - np.arange(128, dtype=np.float32),
                                                 (128, 128)).copy(), name="c_iota_r2").ap()
    sel_np = np.zeros((E, E * 128), dtype=ml_dtypes.bfloat16)
    for e in range(E):
        sel_np[e, e * 128:(e + 1) * 128] = 1.0
    sel_d = nc.inline_tensor(sel_np, name="c_sel").ap()

    kpad2_d = nc.dram_tensor("kpad2_d", [S, 128, 255], BF16, kind="Internal")
    xm_d = nc.dram_tensor("xm_d", [S, T], F32, kind="Internal")
    rsq_d = nc.dram_tensor("rsq_d", [3, T], F32, kind="Internal")

    names = ("dg", "ds", "mm", "ev", "vv", "mm2", "vv2", "ev2", "st",
             "wgu", "wgufr", "wd", "wdfr", "pa", "pb", "hb")
    sm = {k: Sem(nc, k) for k in names}
    dg, ds, mm, ev, vv = sm["dg"], sm["ds"], sm["mm"], sm["ev"], sm["vv"]
    mm2, vv2, ev2, st = sm["mm2"], sm["vv2"], sm["ev2"], sm["st"]

    ctx = nc.ctx
    def sb(name, shape, dt=F32):
        return ctx.enter_context(nc.sbuf_tensor(name, list(shape), dt))

    statW = sb("statW", (128, 8, 18), BF16)
    wct_sb = sb("wct_sb", (S, D), BF16)
    gate_f = sb("gate_f", (128, 8, E), F32)
    gfat = sb("gfat", (128, 8)); bfat_ = sb("bfat", (128, 8))
    dfat = sb("dfat", (128, 8))
    boutfat = sb("boutfat", (128, 8)); binfat = sb("binfat", (128, 16))
    logafat = sb("logafat", (128, 128))
    a128fat = sb("a128fat", (128, 128))
    dtm_bc = sb("dtm_bc", (128, 1))
    cbm_bc = sb("cbm_bc", (128, S))
    negsw_bc = sb("negsw_bc", (128, S))
    sg_bc = sb("sg_bc", (128, E))
    identb = sb("identb", (128, 128), BF16)
    identf = sb("identf", (128, 128), F32)
    jrevb = sb("jrevb", (128, 128), BF16)
    ones_r_b = sb("ones_r_b", (1, 128), BF16)
    ones_r_f = sb("ones_r_f", (1, 128), F32)
    ones_c_b = sb("ones_c_b", (128, 1), BF16)
    ones_c_f = sb("ones_c_f", (128, 1), F32)
    iota_p1 = sb("iota_p1", (128, 128), F32)
    iota_r2 = sb("iota_r2", (128, 128), F32)
    selb = sb("selb", (E, E * 128), BF16)
    mask_sb = sb("mask_sb", (128, J), F32)
    B_cm = sb("B_cm", (128, S, J), BF16)
    x_meanTb = sb("x_meanTb", (S, T), BF16)
    comb8 = sb("comb8", (E, T), BF16)
    rs1_cm = sb("rs1_cm", (128, J))
    xnq_cm = sb("xnq_cm", (128, J))
    lnbcol = sb("lnbcol", (128, 8), BF16)

    _ph = ExitStack()
    winupn = _ph.enter_context(nc.sbuf_tensor("winupn", [128, 8, D], BF16))
    wbt = _ph.enter_context(nc.sbuf_tensor("wbt", [128, 8, S], BF16))
    binup = _ph.enter_context(nc.sbuf_tensor("binup", [128, 8], BF16))
    wbbraw = _ph.enter_context(nc.sbuf_tensor("wbbraw", [128, 8, S], BF16))
    p0rows = _ph.enter_context(nc.sbuf_tensor("p0rows", [1, 64], F32))
    alogfat = _ph.enter_context(nc.sbuf_tensor("alogfat", [128, 128], F32))
    ldtfat = _ph.enter_context(nc.sbuf_tensor("ldtfat", [128, 8], F32))
    dt_fat = _ph.enter_context(nc.sbuf_tensor("dt_fat", [128, 8], F32))
    dt8 = _ph.enter_context(nc.sbuf_tensor("dt8", [128, 1], F32))
    ps0 = _ph.enter_context(nc.psum_tensor("ps0", [128, 512], F32))
    ps1 = _ph.enter_context(nc.psum_tensor("ps1", [128, 512], F32))
    ps2 = _ph.enter_context(nc.psum_tensor("ps2", [128, 512], F32))
    if True:
        p = Prog()
        def ld_s(dst, src):
            p.sp(lambda e, d=dst, s=src: e.dma_start(out=d, in_=s), incs=[(ds, 16)])
        def ld_g(dst, src):
            p.g(lambda e, d=dst, s=src: e.dma_start(out=d, in_=s), incs=[(dg, 16)])
        ld_s(gfat[:], din["g_fat"]); ld_s(bfat_[:], din["b_fat"])
        ld_s(dfat[:], din["d_fat"]); ld_s(ldtfat[:], din["logdt_fat"])
        ld_s(boutfat[:], din["bout_fat"]); ld_s(binfat[:], din["bin_fat"])
        ld_s(alogfat[:], din["alog_fat"]); ld_s(mask_sb[:], din["mask_cm"])
        ld_s(identf[:], ident_f32_d); ld_s(ones_r_f[:], ones_r_f_d)
        ld_s(ones_c_f[:], ones_c_f_d)
        ld_s(iota_p1[:], iota_p1_d); ld_s(iota_r2[:], iota_r2_d)
        ld_s(gate_f[:], din["gate_t"].rearrange("(dk p) e -> p dk e", p=128))
        ld_s(identb[:], ident_bf_d); ld_s(jrevb[:], jrev_d)
        ld_s(ones_r_b[:], ones_r_b_d); ld_s(ones_c_b[:], ones_c_b_d)
        ld_s(selb[:], sel_d)
        n_lds = ds.n
        ld_g(wct_sb[:], din["w_c_t"])
        ld_g(winupn[:], din["w_in_upn"].rearrange("(dok p) di -> p dok di", p=128))
        ld_g(wbt[:], din["w_b_t"].rearrange("(dok p) s -> p dok s", p=128))
        ld_g(binup[:], din["binup_col"])
        ld_g(lnbcol[:], din["lnb_col"])
        n_ldg = dg.n

        p.v(lambda e: e.memset(p0rows[:], 0.0))
        p.v(lambda e: e.memset(statW[:, :, 0:1], 1.0), waits=[(ds, n_lds)])
        p.v(lambda e: e.memset(statW[:, :, 17:18], 0.0))
        p.s(lambda e: e.activation(dt_fat[:], ldtfat[:], AF.Exp),
            waits=[(ds, n_lds)], incs=[(ev, 1)])
        ev_dt = ev.n
        p.v(lambda e: e.tensor_scalar(dt_fat[:], dt_fat[:], DTL, DTH, OP.max, OP.min),
            waits=[(ev, ev_dt)], incs=[(vv, 1)])
        vv_dt = vv.n
        p.s(lambda e: e.activation(logafat[:], alogfat[:], AF.Exp), incs=[(ev, 1)])
        ev_la = ev.n
        for dhi in range(8):
            p.v(lambda e, k=dhi: e.tensor_scalar(logafat[:, k * 16:(k + 1) * 16],
                                                 logafat[:, k * 16:(k + 1) * 16],
                                                 dt_fat[:, k:k + 1], -1.0, OP.mult, OP.mult),
                waits=[(ev, ev_la), (vv, vv_dt)] if dhi == 0 else [])
        p.v(lambda e: e.nop(), incs=[(vv, 1)])
        p.v(lambda e: e.tensor_scalar(logafat[:], logafat[:], LOG_LO, LOG_HI, OP.max, OP.min),
            waits=[(vv, vv.n)], incs=[(vv, 1)])
        vv_lf = vv.n
        p.s(lambda e: e.activation(a128fat[:], logafat[:], AF.Exp, scale=128.0),
            waits=[(vv, vv_lf)])
        p.v(lambda e: e.reduce_sum(dt8[:], dt_fat[:], axis=AX.X), incs=[(vv, 1)])
        vv_dt8 = vv.n
        p.t(lambda e: e.matmul(ps0[0:1, 0:1], ones_c_f[:], dt8[:], start=True, stop=True),
            waits=[(vv, vv_dt8)], incs=[(mm, 1)])
        mm_dtm = mm.n
        p.v(lambda e: e.tensor_scalar(p0rows[0:1, 0:1], ps0[0:1, 0:1], 1.0 / D, None, OP.mult),
            waits=[(mm, mm_dtm)], incs=[(vv, 1)])
        vv_dtm = vv.n
        p.t(lambda e: e.matmul(ps1[:, 0:1], ones_r_f[:], p0rows[0:1, 0:1], start=True, stop=True),
            waits=[(vv, vv_dtm)], incs=[(mm, 1)])
        mm_dbc = mm.n
        p.v(lambda e: e.tensor_copy(dtm_bc[:], ps1[:, 0:1]), waits=[(mm, mm_dbc)], incs=[(vv, 1)])

        vv_wbb = {}
        for dik in range(8):
            def mmf(e, dik=dik):
                for dok in range(8):
                    ins = e.matmul(ps2[:, 0:S], winupn[:, dok, dik * 128:(dik + 1) * 128],
                                   wbt[:, dok, :], start=(dok == 0), stop=(dok == 7))
                return ins
            wt = [(dg, n_ldg)] if dik == 0 else [(vv, vv_wbb[dik - 1])]
            p.t(mmf, waits=wt, incs=[(mm, 1)])
            mm_w = mm.n
            p.v(lambda e, dik=dik: e.tensor_copy(wbbraw[:, dik, :], ps2[:, 0:S]),
                waits=[(mm, mm_w)])
            p.v(lambda e, dik=dik: e.tensor_scalar(statW[:, dik, 1:17], ps2[:, 0:S],
                                                   gfat[:, dik:dik + 1], None, OP.mult),
                incs=[(vv, 1)])
            vv_wbb[dik] = vv.n
        def wbf(e):
            for dik in range(8):
                ins = e.matmul(ps0[0:1, 4:20], lnbcol[:, dik:dik + 1], wbbraw[:, dik, :],
                               start=(dik == 0), stop=(dik == 7))
            return ins
        p.t(wbf, waits=[(vv, vv_wbb[7])], incs=[(mm, 1)])
        def cbf(e):
            for dok in range(8):
                ins = e.matmul(ps1[0:1, 4:20], binup[:, dok:dok + 1], wbt[:, dok, :],
                               start=(dok == 0), stop=(dok == 7))
            return ins
        p.t(cbf, incs=[(mm, 1)])
        def swf(e):
            for dik in range(8):
                ins = e.matmul(ps2[0:1, 0:S], ones_c_b[:], statW[:, dik, 1:17],
                               start=(dik == 0), stop=(dik == 7))
            return ins
        p.t(swf, incs=[(mm, 1)])
        def sgf(e):
            for dk in range(8):
                ins = e.matmul(ps0[0:1, 24:32], ones_c_f[:], gate_f[:, dk, :],
                               start=(dk == 0), stop=(dk == 7))
            return ins
        p.t(sgf, incs=[(mm, 1)])
        mm_p0 = mm.n
        p.v(lambda e: e.tensor_copy(p0rows[0:1, 4:20], ps0[0:1, 4:20]), waits=[(mm, mm_p0)],
            incs=[(vv, 1)])
        p.v(lambda e: e.tensor_add(p0rows[0:1, 4:20], p0rows[0:1, 4:20], ps1[0:1, 4:20]),
            waits=[(vv, vv.n)], incs=[(vv, 1)])
        p.v(lambda e: e.tensor_scalar(p0rows[0:1, 20:36], ps2[0:1, 0:S], -1.0, None, OP.mult))
        p.v(lambda e: e.tensor_copy(p0rows[0:1, 40:48], ps0[0:1, 24:32]), waits=[(vv, vv.n)],
            incs=[(vv, 1)])
        vv_rows = vv.n
        p.t(lambda e: e.matmul(ps0[:, 32:48], ones_r_f[:], p0rows[0:1, 4:20], start=True, stop=True),
            waits=[(vv, vv_rows)])
        p.t(lambda e: e.matmul(ps1[:, 32:48], ones_r_f[:], p0rows[0:1, 20:36], start=True, stop=True))
        p.t(lambda e: e.matmul(ps2[:, 32:40], ones_r_f[:], p0rows[0:1, 40:48], start=True, stop=True),
            incs=[(mm, 1)])
        mm_bc = mm.n
        p.v(lambda e: e.tensor_scalar(cbm_bc[:], ps0[:, 32:48], dtm_bc[:, 0:1], None, OP.mult),
            waits=[(mm, mm_bc)])
        p.v(lambda e: e.tensor_copy(negsw_bc[:], ps1[:, 32:48]))
        p.v(lambda e: e.tensor_copy(sg_bc[:], ps2[:, 32:40]), incs=[(vv, 1)])
        if debug:
            vfin = vv.n
            p.sp(lambda e: e.dma_start(out=dbg["dbg_logafat"], in_=logafat[:]),
                 waits=[(vfin_s, vfin) for vfin_s in (vv,)], incs=[(ds, 16)])
            p.sp(lambda e: e.dma_start(out=dbg["dbg_statw"], in_=statW[:]), incs=[(ds, 16)])
            p.sp(lambda e: e.dma_start(out=dbg["dbg_cbm"], in_=cbm_bc[:]), incs=[(ds, 16)])
            p.sp(lambda e: e.dma_start(out=dbg["dbg_negsw"], in_=negsw_bc[:]), incs=[(ds, 16)])
            p.sp(lambda e: e.dma_start(out=dbg["dbg_rows"], in_=p0rows[:]), incs=[(ds, 16)])
            p.sp(lambda e: e.nop(), waits=[(ds, ds.n)])
        p.run(nc)
        _ph.close()

    if max_phase < 1:
        return nc, din, outT_d, dbg

    _ph = ExitStack()
    xbf = _ph.enter_context(nc.sbuf_tensor("xbf", [128, 2, 8, 512], BF16))
    xsq = _ph.enter_context(nc.sbuf_tensor("xsq", [128, 2, 8, 512], BF16))
    P_cm = _ph.enter_context(nc.sbuf_tensor("P_cm", [128, S, 128], F32))
    sumx_cm = _ph.enter_context(nc.sbuf_tensor("sumx_cm", [128, 128], F32))
    sumsq_cm = _ph.enter_context(nc.sbuf_tensor("sumsq_cm", [128, 128], F32))
    scrA = _ph.enter_context(nc.sbuf_tensor("scrA", [128, 128], F32))
    scrB = _ph.enter_context(nc.sbuf_tensor("scrB", [128, 128], F32))
    mu_cm = _ph.enter_context(nc.sbuf_tensor("mu_cm", [128, J], F32))
    u_cm = _ph.enter_context(nc.sbuf_tensor("u_cm", [128, J], F32))
    q2_cm = _ph.enter_context(nc.sbuf_tensor("q2_cm", [128, J], F32))
    rs2_cm = _ph.enter_context(nc.sbuf_tensor("rs2_cm", [128, J], F32))
    t2f = _ph.enter_context(nc.sbuf_tensor("t2f", [16, 3, 128], F32))
    pP0 = _ph.enter_context(nc.psum_tensor("pP0", [128, 512], F32))
    pP1 = _ph.enter_context(nc.psum_tensor("pP1", [128, 512], F32))
    pQ0 = _ph.enter_context(nc.psum_tensor("pQ0", [128, 512], F32))
    pQ1 = _ph.enter_context(nc.psum_tensor("pQ1", [128, 512], F32))
    pT = _ph.enter_context(nc.psum_tensor("pT", [128, 512], F32))
    if True:
        p = Prog()
        pP = (pP0, pP1); pQ = (pQ0, pQ1)
        vv_grp = {}; ev_chunk = {}; mm_chunk = {}
        for tc in range(32):
            slot = tc % 2
            wt = []
            if tc >= 2:
                wt = [(mm, mm_chunk[tc - 2]), (ev, ev_chunk[tc - 2])]
            src = din["xt"][:, tc * 512:(tc + 1) * 512].rearrange("(dk p) t -> p dk t", p=128)
            p.g(lambda e, d=xbf[:, slot], s=src: e.dma_start(out=d, in_=s),
                waits=wt, incs=[(dg, 16)])
            dg_c = dg.n
            wt_s = [(dg, dg_c)]
            if tc >= 2:
                wt_s.append((mm, mm_chunk[tc - 2]))
            p.s(lambda e, sl=slot: e.activation(
                    xsq[:, sl].rearrange("p a b -> p (a b)"),
                    xbf[:, sl].rearrange("p a b -> p (a b)"), AF.Square),
                waits=wt_s, incs=[(ev, 1)])
            ev_chunk[tc] = ev.n
            for jj in range(4):
                gidx = tc * 4 + jj
                psl = gidx % 2
                wt_t = [(dg, dg_c), (ev, ev_chunk[tc])]
                if gidx >= 2:
                    wt_t.append((vv, vv_grp[gidx - 2]))
                def mP(e, sl=slot, jj=jj, psl=psl):
                    for dk in range(8):
                        ins = e.matmul(pP[psl][:, 0:18], xbf[:, sl, dk, jj * 128:(jj + 1) * 128],
                                       statW[:, dk, :], start=(dk == 0), stop=(dk == 7))
                    return ins
                p.t(mP, waits=wt_t)
                def mQ(e, sl=slot, jj=jj, psl=psl):
                    for dk in range(8):
                        ins = e.matmul(pQ[psl][:, 0:1], xsq[:, sl, dk, jj * 128:(jj + 1) * 128],
                                       ones_c_b[:], start=(dk == 0), stop=(dk == 7))
                    return ins
                p.t(mQ, incs=[(mm, 1)])
                mm_g = mm.n
                jg = gidx
                p.v(lambda e, psl=psl, jg=jg: e.tensor_copy(P_cm[:, :, jg], pP[psl][:, 1:17]),
                    waits=[(mm, mm_g)])
                p.v(lambda e, psl=psl, jg=jg: e.tensor_copy(sumx_cm[:, jg:jg + 1], pP[psl][:, 0:1]))
                p.v(lambda e, psl=psl, jg=jg: e.tensor_copy(sumsq_cm[:, jg:jg + 1], pQ[psl][:, 0:1]),
                    incs=[(vv, 1)])
                vv_grp[gidx] = vv.n
            mm_chunk[tc] = mm.n
        p.v(lambda e: e.tensor_scalar(mu_cm[:], sumx_cm[:], 1.0 / D, None, OP.mult),
            waits=[(vv, vv.n)])
        p.v(lambda e: e.tensor_scalar(scrA[:], sumsq_cm[:], 1.0 / D, None, OP.mult),
            incs=[(vv, 1)])
        vv_a = vv.n
        p.s(lambda e: e.activation(scrB[:], mu_cm[:], AF.Square), waits=[(vv, vv_a)], incs=[(ev, 1)])
        ev_a = ev.n
        p.v(lambda e: e.tensor_sub(scrA[:], scrA[:], scrB[:]), waits=[(ev, ev_a)])
        p.v(lambda e: e.tensor_scalar(scrB[:], scrA[:], EPS, None, OP.add), incs=[(vv, 1)])
        vv_b = vv.n
        p.s(lambda e: e.activation(scrB[:], scrB[:], AF.Ln), waits=[(vv, vv_b)])
        p.s(lambda e: e.activation(rs1_cm[:], scrB[:], AF.Exp, scale=-0.5), incs=[(ev, 1)])
        ev_b = ev.n
        p.v(lambda e: e.tensor_scalar(scrB[:], scrA[:], EPS, None, OP.add), waits=[(ev, ev_b)])
        p.v(lambda e: e.reciprocal(scrB[:], scrB[:]))
        p.v(lambda e: e.tensor_mul(scrB[:], scrA[:], scrB[:]))
        p.v(lambda e: e.tensor_scalar(scrB[:], scrB[:], EPS, None, OP.add), incs=[(vv, 1)])
        vv_c2 = vv.n
        p.s(lambda e: e.activation(scrB[:], scrB[:], AF.Ln), waits=[(vv, vv_c2)])
        p.s(lambda e: e.activation(rs2_cm[:], scrB[:], AF.Exp, scale=-0.5), incs=[(ev, 1)])
        ev_c = ev.n
        p.v(lambda e: e.scalar_tensor_tensor(q2_cm[:], rs1_cm[:], dtm_bc[:, 0:1], rs2_cm[:],
                                             OP.mult, OP.mult), waits=[(ev, ev_c)])
        p.v(lambda e: e.tensor_mul(xnq_cm[:], mu_cm[:], rs1_cm[:]), incs=[(vv, 1)])
        vv_fin = vv.n
        for s_ in range(S):
            p.v(lambda e, s_=s_: e.scalar_tensor_tensor(scrA[:], mu_cm[:], negsw_bc[:, s_:s_ + 1],
                                                        P_cm[:, s_, :], OP.mult, OP.add),
                waits=[(vv, vv_fin)] if s_ == 0 else [])
            p.v(lambda e: e.tensor_mul(scrA[:], scrA[:], q2_cm[:]))
            p.v(lambda e, s_=s_: e.tensor_scalar(scrA[:], scrA[:], cbm_bc[:, s_:s_ + 1], None, OP.add))
            p.v(lambda e, s_=s_: e.tensor_mul(B_cm[:, s_, :], scrA[:], mask_sb[:]),
                incs=[(vv, 1)])
        vv_bcm = vv.n
        p.t(lambda e: e.transpose(pT[0:16, 0:128], rs1_cm[:, JOWN0:J], identf[:]),
            waits=[(vv, vv_fin)], incs=[(mm, 1)])
        mm_t1 = mm.n
        p.v(lambda e: e.tensor_copy(t2f[:, 0, :], pT[0:16, 0:128]), waits=[(mm, mm_t1)],
            incs=[(vv, 1)])
        vv_t1 = vv.n
        p.t(lambda e: e.transpose(pT[0:16, 128:256], xnq_cm[:, JOWN0:J], identf[:]),
            waits=[(vv, vv_t1)], incs=[(mm, 1)])
        mm_t2 = mm.n
        p.v(lambda e: e.tensor_copy(t2f[:, 1, :], pT[0:16, 128:256]), waits=[(mm, mm_t2)],
            incs=[(vv, 1)])
        vv_t2 = vv.n
        p.sp(lambda e: e.dma_start(out=dram_ap(rsq_d.ap(), 0, [[128, 16], [1, 128]]),
                                   in_=t2f[:, 0, :]),
             waits=[(vv, vv_t2)], incs=[(ds, 16)])
        p.sp(lambda e: e.dma_start(out=dram_ap(rsq_d.ap(), T, [[128, 16], [1, 128]]),
                                   in_=t2f[:, 1, :]), incs=[(ds, 16)])
        p.t(lambda e: e.transpose(pT[0:16, 256:384], rs2_cm[:, JOWN0:J], identf[:]),
            waits=[(vv, vv_t2)], incs=[(mm, 1)])
        mm_t3 = mm.n
        p.v(lambda e: e.tensor_copy(t2f[:, 2, :], pT[0:16, 256:384]), waits=[(mm, mm_t3)],
            incs=[(vv, 1)])
        vv_t3 = vv.n
        p.sp(lambda e: e.dma_start(out=dram_ap(rsq_d.ap(), 2 * T, [[128, 16], [1, 128]]),
                                   in_=t2f[:, 2, :]),
             waits=[(vv, vv_t3)], incs=[(ds, 16)])
        if debug:
            p.sp(lambda e: e.dma_start(out=dbg["dbg_bcm"], in_=B_cm[:]),
                 waits=[(vv, vv_bcm)], incs=[(ds, 16)])
            p.sp(lambda e: e.dma_start(out=dbg["dbg_mu"], in_=mu_cm[:]), incs=[(ds, 16)])
            p.sp(lambda e: e.dma_start(out=dbg["dbg_rs1"], in_=rs1_cm[:]), incs=[(ds, 16)])
            p.sp(lambda e: e.dma_start(out=dbg["dbg_q2"], in_=q2_cm[:]), incs=[(ds, 16)])
            p.sp(lambda e: e.dma_start(out=dbg["dbg_pcm"], in_=P_cm[:]), incs=[(ds, 16)])
        p.sp(lambda e: e.nop(), waits=[(ds, ds.n)])
        p.run(nc)
        _ph.close()

    if max_phase < 2:
        return nc, din, outT_d, dbg

    _ph = ExitStack()
    apowc = _ph.enter_context(nc.sbuf_tensor("apowc", [128, S, 8, 128], BF16))
    arev_s = _ph.enter_context(nc.sbuf_tensor("arev_s", [128, 2, 8, 128], BF16))
    loc_s = _ph.enter_context(nc.sbuf_tensor("loc_s", [128, 2, 8, 128], BF16))
    scr2 = _ph.enter_context(nc.sbuf_tensor("scr2", [128, 2, 128], F32))
    gsc = _ph.enter_context(nc.sbuf_tensor("gsc", [128, 2, 128], BF16))
    S_own = _ph.enter_context(nc.sbuf_tensor("S_own", [128, 8, S, 17], F32))
    S_bf = _ph.enter_context(nc.sbuf_tensor("S_bf", [128, 8, S, 16], BF16))
    x_meanT = _ph.enter_context(nc.sbuf_tensor("x_meanT", [S, T], F32))
    onetile = _ph.enter_context(nc.sbuf_tensor("onetile", [128, 128], F32))
    disc = _ph.enter_context(nc.sbuf_tensor("disc", [128, 112], F32))
    rep = _ph.enter_context(nc.sbuf_tensor("rep", [128, 128], F32))
    kpadf = _ph.enter_context(nc.sbuf_tensor("kpadf", [1, 2, 255], F32))
    kpadb = _ph.enter_context(nc.sbuf_tensor("kpadb", [1, 2, 255], BF16))
    k2sb = _ph.enter_context(nc.sbuf_tensor("k2sb", [128, 2, 255], BF16))
    hsb = _ph.enter_context(nc.sbuf_tensor("hsb", [128, 2, 128], BF16))
    msb = _ph.enter_context(nc.sbuf_tensor("msb", [128, 2, 128], BF16))
    esb = _ph.enter_context(nc.sbuf_tensor("esb", [128, 2, 16], BF16))
    t2b = _ph.enter_context(nc.sbuf_tensor("t2b", [16, 16, 128], F32))
    pA = _ph.enter_context(nc.psum_tensor("pA", [128, 512], F32))
    pB = _ph.enter_context(nc.psum_tensor("pB", [128, 512], F32))
    pC = _ph.enter_context(nc.psum_tensor("pC", [128, 512], F32))
    pD = _ph.enter_context(nc.psum_tensor("pD", [128, 512], F32))
    pE = _ph.enter_context(nc.psum_tensor("pE", [128, 256], BF16))
    if True:
        p = Prog()
        p.v(lambda e: e.memset(onetile[:], 1.0))
        ev2_hist = {}
        for s_ in range(S):
            for dk in range(8):
                idx = s_ * 8 + dk
                sl = idx % 2
                wt = [(ev2, ev2_hist[idx - 2])] if idx >= 2 else []
                p.v(lambda e, s_=s_, dk=dk, sl=sl: e.tensor_scalar(
                        scr2[:, sl, :], iota_p1[:], logafat[:, dk * 16 + s_:dk * 16 + s_ + 1],
                        None, OP.mult),
                    waits=wt, incs=[(vv2, 1)])
                p.s(lambda e, s_=s_, dk=dk, sl=sl: e.activation(
                        apowc[:, s_, dk, :], scr2[:, sl, :], AF.Exp),
                    waits=[(vv2, vv2.n)], incs=[(ev2, 1)])
                ev2_hist[idx] = ev2.n
        ev2_apow = ev2.n
        mm_loc_end = {}
        st_kp = {}
        vv_arev_hist = {}
        for s_ in range(S):
            sl = s_ % 2
            for dk in range(8):
                gidx = s_ * 8 + dk
                gsl = gidx % 2
                wt_g = [(ev, ev.n)] if gidx >= 2 else [(ev2, ev2_apow)]
                p.v(lambda e, s_=s_, dk=dk, gsl=gsl: e.tensor_scalar(
                        scr2[:, gsl, :], iota_r2[:], logafat[:, dk * 16 + s_:dk * 16 + s_ + 1],
                        None, OP.mult),
                    waits=wt_g, incs=[(vv, 1)])
                vv_g = vv.n
                wt_e = [(vv, vv_g)]
                if gidx >= 2:
                    wt_e.append((mm, mm.n))
                p.s(lambda e, gsl=gsl: e.activation(gsc[:, gsl, :], scr2[:, gsl, :], AF.Exp),
                    waits=wt_e, incs=[(ev, 1)])
                ev_g = ev.n
                wt_t = [(ev, ev_g)]
                if gidx >= 1:
                    wt_t.append((vv, vv_arev_hist[gidx - 1]))
                p.t(lambda e, gsl=gsl: e.transpose(pE[:, 0:128], gsc[:, gsl, :], identb[:]),
                    waits=wt_t, incs=[(mm, 1)])
                mm_g = mm.n
                p.v(lambda e, sl=sl, dk=dk: e.tensor_copy(arev_s[:, sl, dk, :], pE[:, 0:128]),
                    waits=[(mm, mm_g)], incs=[(vv, 1)])
                vv_arev_hist[gidx] = vv.n
            vv_arev = vv.n
            for dk in range(8):
                wtl = [(vv, vv_arev)]
                if dk >= 1:
                    wtl.append((vv, vv.n))
                p.t(lambda e, sl=sl, dk=dk, s_=s_: e.matmul(
                        pB[:, (dk % 2) * 128:((dk % 2) + 1) * 128],
                        arev_s[:, sl, dk, :], B_cm[:, s_, :], start=True, stop=True),
                    waits=wtl, incs=[(mm, 1)])
                mm_l = mm.n
                p.v(lambda e, sl=sl, dk=dk: e.tensor_copy(
                        loc_s[:, sl, dk, :], pB[:, (dk % 2) * 128:((dk % 2) + 1) * 128]),
                    waits=[(mm, mm_l)], incs=[(vv, 1)])
            mm_loc_end[s_] = mm.n
            for dk in range(8):
                col = dk * 16 + s_
                p.v(lambda e, col=col: e.tensor_scalar(rep[:], onetile[:],
                                                       a128fat[:, col:col + 1], None, OP.mult),
                    incs=[(vv, 1)])
                p.v(lambda e, sl=sl, dk=dk: e.tensor_tensor_scan(
                        disc[:], rep[:, 0:112], loc_s[:, sl, dk, 0:112], 0.0, OP.mult, OP.add),
                    waits=[(vv, vv.n)], incs=[(vv, 1)])
                p.v(lambda e, dk=dk, s_=s_: e.tensor_copy(S_own[:, dk, s_, 0:1], disc[:, 111:112]),
                    waits=[(vv, vv.n)], incs=[(vv, 1)])
                p.v(lambda e, sl=sl, dk=dk, s_=s_: e.tensor_tensor_scan(
                        S_own[:, dk, s_, 1:17], rep[:, 0:16], loc_s[:, sl, dk, 112:128],
                        S_own[:, dk, s_, 0:1], OP.mult, OP.add),
                    waits=[(vv, vv.n)], incs=[(vv, 1)])
            vv_scan = vv.n
            def mK(e, s_=s_):
                for dk in range(8):
                    ins = e.matmul(pC[0:1, 0:128], ones_c_b[:], apowc[:, s_, dk, :],
                                   start=(dk == 0), stop=(dk == 7))
                return ins
            wt_mk = [(ev2, ev2_apow)]
            if s_ >= 1:
                wt_mk.append((vv2, vv2_kprev))
            p.t(mK, waits=wt_mk, incs=[(mm2, 1)])
            mm2_k = mm2.n
            ksl = s_ % 2
            wt_kp = [(mm2, mm2_k)]
            if s_ >= 2:
                wt_kp.append((st, st_kp[s_ - 2]))
            p.v(lambda e, ksl=ksl: e.memset(kpadf[:, ksl, :], 0.0), waits=wt_kp,
                incs=[(vv2, 1)])
            p.v(lambda e, ksl=ksl: e.memset(kpadf[0:1, ksl, 127:128], 1.0),
                waits=[(vv2, vv2.n)], incs=[(vv2, 1)])
            p.v(lambda e, ksl=ksl: e.tensor_scalar(kpadf[0:1, ksl, 128:255], pC[0:1, 0:127],
                                                   1.0 / D, None, OP.mult),
                waits=[(vv2, vv2.n)], incs=[(vv2, 1)])
            p.v(lambda e, ksl=ksl: e.tensor_copy(kpadb[:, ksl, :], kpadf[:, ksl, :]),
                waits=[(vv2, vv2.n)], incs=[(vv2, 1)])
            vv2_k = vv2.n
            p.t(lambda e, ksl=ksl: e.matmul(pD[:, 0:255], ones_r_b[:], kpadb[0:1, ksl, :],
                                            start=True, stop=True),
                waits=[(vv2, vv2_k)], incs=[(mm2, 1)])
            mm2_k2 = mm2.n
            p.v(lambda e, ksl=ksl: e.tensor_copy(k2sb[:, ksl, :], pD[:, 0:255]),
                waits=[(mm2, mm2_k2)], incs=[(vv2, 1)])
            vv2_k2 = vv2.n
            p.sp(lambda e, s_=s_, ksl=ksl: e.dma_start(
                    out=kpad2_d.ap()[s_], in_=k2sb[:, ksl, :]),
                 waits=[(vv2, vv2_k2)], incs=[(st, 16)])
            st_kp[s_] = st.n
            vv2_kprev = vv2.n
        st_kpad = st.n
        p.v(lambda e: e.tensor_scalar(S_bf[:], S_own[:, :, :, 0:16], 1.0 / D, None, OP.mult),
            waits=[(vv, vv.n)], incs=[(vv, 1)])
        vv_sbf = vv.n
        ev_loop1 = ev.n
        for s_ in range(S):
            sl = s_ % 2
            wt_h = [(st, st_kpad)]
            if s_ >= 2:
                wt_h.append((mm2, mm2.n))
            p.sp(lambda e, s_=s_, sl=sl: e.dma_start(
                    out=hsb[:, sl, :],
                    in_=dram_ap(kpad2_d.ap(), s_ * 128 * 255, [[256, 128], [1, 128]])),
                 waits=wt_h, incs=[(ds, 16)])
            ds_h = ds.n
            wt_j = [(ds, ds_h)]
            if s_ >= 1:
                wt_j.append((vv2, vv2.n))
            p.t(lambda e, sl=sl: e.matmul(pA[:, 0:128], jrevb[:], hsb[:, sl, :],
                                          start=True, stop=True),
                waits=wt_j, incs=[(mm2, 1)])
            mm2_j = mm2.n
            p.v(lambda e, sl=sl: e.tensor_copy(msb[:, sl, :], pA[:, 0:128]),
                waits=[(mm2, mm2_j)], incs=[(vv2, 1)])
            vv2_m = vv2.n
            def mconv(e, s_=s_, sl=sl):
                e.matmul(pB[:, 256:272], msb[:, sl, :], B_cm[:, s_, 112:128],
                         start=True, stop=False)
                for dk in range(8):
                    ins = e.matmul(pB[:, 256:272], apowc[:, s_, dk, :], S_bf[:, dk, s_, :],
                                   start=False, stop=(dk == 7))
                return ins
            p.t(mconv, waits=[(vv2, vv2_m), (vv, vv_sbf)], incs=[(mm2, 1)])
            mm2_c = mm2.n
            p.v(lambda e, sl=sl: e.tensor_copy(esb[:, sl, :], pB[:, 256:272]),
                waits=[(mm2, mm2_c)], incs=[(vv2, 1)])
            vv2_e = vv2.n
            p.t(lambda e, sl=sl: e.transpose(pE[0:16, 0:128], esb[:, sl, :], identb[:]),
                waits=[(vv2, vv2_e)], incs=[(mm2, 1)])
            mm2_t = mm2.n
            p.v(lambda e, s_=s_: e.tensor_copy(t2b[:, s_, :], pE[0:16, 0:128]),
                waits=[(mm2, mm2_t)], incs=[(vv2, 1)])
            vv2_t = vv2.n
            p.sp(lambda e, s_=s_: e.dma_start(
                    out=dram_ap(xm_d.ap(), s_ * T, [[128, 16], [1, 128]]),
                    in_=t2b[:, s_, :]),
                 waits=[(vv2, vv2_t)], incs=[(st, 16)])
        st_xm = st.n
        p.sp(lambda e: e.dma_start(out=x_meanT[:], in_=xm_d.ap()),
             waits=[(st, st_xm)], incs=[(ds, 16)])
        ds_xm = ds.n
        p.v(lambda e: e.tensor_copy(x_meanTb[:], x_meanT[:]), waits=[(ds, ds_xm)], incs=[(vv, 1)])
        if debug:
            p.sp(lambda e: e.dma_start(out=dbg["dbg_xm"], in_=x_meanT[:]),
                 waits=[(ds, ds_xm)], incs=[(ds, 16)])
            p.sp(lambda e: e.dma_start(out=dbg["dbg_sown"], in_=S_own[:]), incs=[(ds, 16)])
        p.sp(lambda e: e.nop(), waits=[(ds, ds.n)])
        p.run(nc)
        _ph.close()

    if max_phase < 3:
        return nc, din, outT_d, dbg

    xnT = sb("xnT", (128, 8, T), BF16)
    outT = sb("outT_sb", (128, 8, T), F32)
    yT_guard = nc.sbuf_tensor("yT", [128, 8, T], BF16)
    yT = yT_guard.__enter__()

    _ph = ExitStack()
    xf32 = _ph.enter_context(nc.sbuf_tensor("xf32", [128, 2, 8, 256], F32))
    rsq_sb = _ph.enter_context(nc.sbuf_tensor("rsq_sb", [1, 2, 3, 256], F32))
    rs2bc = _ph.enter_context(nc.sbuf_tensor("rs2bc", [128, T], BF16))
    ccol = _ph.enter_context(nc.sbuf_tensor("ccol", [128, 4], F32))
    zt1 = _ph.enter_context(nc.sbuf_tensor("zt1", [128, 2, 512], BF16))
    lgt = _ph.enter_context(nc.sbuf_tensor("lgt", [128, 8], F32))
    lgt2 = _ph.enter_context(nc.sbuf_tensor("lgt2", [128, 8], F32))
    m1c = _ph.enter_context(nc.sbuf_tensor("m1c", [128, 4], F32))
    combt = _ph.enter_context(nc.sbuf_tensor("combt", [128, 8], BF16))
    xnscr = _ph.enter_context(nc.sbuf_tensor("xnscr", [128, 2, 256], F32))
    wpb = _ph.enter_context(nc.sbuf_tensor("wpb", [128, 2, 8, 128], BF16))
    wzb = _ph.enter_context(nc.sbuf_tensor("wzb", [128, 2, 8, 128], BF16))
    zscr = _ph.enter_context(nc.sbuf_tensor("zscr", [128, 2, 512], BF16))
    yscr = _ph.enter_context(nc.sbuf_tensor("yscr", [128, 2, 512], F32))
    qA = _ph.enter_context(nc.psum_tensor("qA", [128, 512], F32))
    qB = _ph.enter_context(nc.psum_tensor("qB", [128, 512], F32))
    qC = _ph.enter_context(nc.psum_tensor("qC", [128, 512], F32))
    qD = _ph.enter_context(nc.psum_tensor("qD", [128, 512], F32))
    qE = _ph.enter_context(nc.psum_tensor("qE", [128, 512], F32))
    qF = _ph.enter_context(nc.psum_tensor("qF", [128, 512], F32))
    qG = _ph.enter_context(nc.psum_tensor("qG", [128, 256], BF16))
    if True:
        p = Prog()
        vv_tc = {}; mm_tc = {}; vv_rs2bc = {}
        for tc in range(8):
            sl = tc % 2
            wt = []
            if tc >= 2:
                wt = [(vv, vv_tc[tc - 2]), (mm, mm_tc[tc - 2])]
            src = din["xt"][:, L - T + tc * 256: L - T + (tc + 1) * 256].rearrange(
                "(dk p) t -> p dk t", p=128)
            p.sp(lambda e, d=xf32[:, sl], s=src: e.dma_start(out=d, in_=s),
                 waits=wt, incs=[(ds, 16)])
            for r_ in range(3):
                p.sp(lambda e, sl=sl, r_=r_, tc=tc: e.dma_start(
                        out=rsq_sb[:, sl, r_, :],
                        in_=rsq_d.ap()[r_:r_ + 1, tc * 256:(tc + 1) * 256]),
                     incs=[(ds, 16)])
            ds_x = ds.n
            p.t(lambda e, sl=sl: e.matmul(qA[:, 0:256], ones_r_f[:], rsq_sb[0:1, sl, 0, :],
                                          start=True, stop=True),
                waits=[(ds, ds_x), (vv, vv_tc[tc - 1] if tc >= 1 else 0)])
            p.t(lambda e, sl=sl: e.matmul(qA[:, 256:512], ones_r_f[:], rsq_sb[0:1, sl, 1, :],
                                          start=True, stop=True), incs=[(mm, 1)])
            mm_b = mm.n
            p.t(lambda e, sl=sl: e.matmul(qC[:, 0:256], ones_r_f[:], rsq_sb[0:1, sl, 2, :],
                                          start=True, stop=True),
                waits=[(vv, vv_rs2bc[tc - 1]) if tc >= 1 else (vv, 0)], incs=[(mm, 1)])
            mm_r2 = mm.n
            p.v(lambda e, tc=tc: e.tensor_copy(rs2bc[:, tc * 256:(tc + 1) * 256], qC[:, 0:256]),
                waits=[(mm, mm_r2)], incs=[(vv, 1)])
            vv_rs2bc[tc] = vv.n
            for dk in range(8):
                p.v(lambda e, sl=sl, dk=dk: e.tensor_mul(xnscr[:, sl, :], xf32[:, sl, dk, :],
                                                         qA[:, 0:256]),
                    waits=[(mm, mm_b)] if dk == 0 else [])
                p.v(lambda e, sl=sl, dk=dk, tc=tc: e.tensor_sub(
                        xnT[:, dk, tc * 256:(tc + 1) * 256], xnscr[:, sl, :], qA[:, 256:512]))
                p.v(lambda e, sl=sl, dk=dk, tc=tc: e.tensor_add(
                        outT[:, dk, tc * 256:(tc + 1) * 256], xf32[:, sl, dk, :],
                        xnT[:, dk, tc * 256:(tc + 1) * 256]),
                    incs=[(vv, 1)])
            for h in range(2):
                tt = tc * 2 + h
                jcol = JOWN0 + tt
                def mg(e, sl=sl, h=h):
                    for dk in range(8):
                        ins = e.matmul(qB[:, 0:8], xf32[:, sl, dk, h * 128:(h + 1) * 128],
                                       gate_f[:, dk, :], start=(dk == 0), stop=(dk == 7))
                    return ins
                p.t(mg, waits=[(vv, vv.n)], incs=[(mm, 1)])
                mm_gx = mm.n
                def vc(fn, extra=()):
                    p.v(fn, waits=[(vv, vv.n)] + list(extra), incs=[(vv, 1)])
                p.v(lambda e, jcol=jcol: e.tensor_scalar(lgt2[:], sg_bc[:],
                                                         xnq_cm[:, jcol:jcol + 1], None, OP.mult),
                    waits=[(mm, mm_gx)], incs=[(vv, 1)])
                vc(lambda e, jcol=jcol: e.scalar_tensor_tensor(
                        lgt[:], qB[:, 0:8], rs1_cm[:, jcol:jcol + 1], lgt2[:],
                        OP.mult, OP.subtract))
                vc(lambda e: e.reduce_max(m1c[:, 0:1], lgt[:], axis=AX.X))
                vc(lambda e: e.tensor_scalar(lgt2[:], lgt[:], m1c[:, 0:1], None, OP.is_ge))
                vc(lambda e: e.scalar_tensor_tensor(lgt2[:], lgt2[:], NEG_BIG, lgt[:],
                                                    OP.mult, OP.add))
                vc(lambda e: e.reduce_max(m1c[:, 1:2], lgt2[:], axis=AX.X))
                vc(lambda e: e.tensor_scalar(m1c[:, 2:3], m1c[:, 0:1], -1.0, None, OP.mult))
                vv_r = vv.n
                p.s(lambda e: e.activation(lgt2[:], lgt[:], AF.Exp, bias=m1c[:, 2:3]),
                    waits=[(vv, vv_r)], incs=[(ev, 1)])
                p.s(lambda e: e.activation(m1c[:, 3:4], m1c[:, 1:2], AF.Exp, bias=m1c[:, 2:3]),
                    waits=[(ev, ev.n)], incs=[(ev, 1)])
                ev_r = ev.n
                p.v(lambda e: e.tensor_scalar(m1c[:, 3:4], m1c[:, 3:4], 1.0, None, OP.add),
                    waits=[(ev, ev_r)], incs=[(vv, 1)])
                vc(lambda e: e.reciprocal(m1c[:, 3:4], m1c[:, 3:4]))
                vc(lambda e: e.tensor_scalar(lgt[:], lgt[:], m1c[:, 1:2], None, OP.is_ge))
                vc(lambda e: e.scalar_tensor_tensor(lgt2[:], lgt2[:], m1c[:, 3:4], lgt[:],
                                                    OP.mult, OP.mult))
                vc(lambda e: e.tensor_copy(combt[:], lgt2[:]))
                vv_cb = vv.n
                p.t(lambda e: e.transpose(qG[0:8, 0:128], combt[:], identb[:]),
                    waits=[(vv, vv_cb)], incs=[(mm, 1)])
                mm_tr = mm.n
                p.v(lambda e, tt=tt: e.tensor_copy(comb8[:, tt * 128:(tt + 1) * 128],
                                                   qG[0:8, 0:128]),
                    waits=[(mm, mm_tr)], incs=[(vv, 1)])
            vv_tc[tc] = vv.n
            mm_tc[tc] = mm.n
        vv_p3a = vv.n
        if debug:
            p.sp(lambda e: e.dma_start(out=dbg["dbg_xnt"], in_=xnT[:]),
                 waits=[(vv, vv_p3a)], incs=[(ds, 16)])
            p.sp(lambda e: e.dma_start(out=dbg["dbg_comb8"], in_=comb8[:]), incs=[(ds, 16)])
        pP3 = (qC, qD); pZ3 = (qE, qF); pY3 = (qA, qB)
        qH = qG
        vv_rs2 = vv_p3a
        mm_chi = {}; vv_y = {}
        for chi in range(8):
            wsl = chi % 2
            wt_w = []
            if chi >= 2:
                wt_w = [(mm, mm_chi[chi - 2])]
            srcp = din["w_in_t"][:, chi * 128:(chi + 1) * 128].rearrange(
                "(dk p) c -> p dk c", p=128)
            srcz = din["w_in_t"][:, D + chi * 128: D + (chi + 1) * 128].rearrange(
                "(dk p) c -> p dk c", p=128)
            p.g(lambda e, d=wpb[:, wsl], s=srcp: e.dma_start(out=d, in_=s),
                waits=wt_w, incs=[(dg, 16)])
            p.g(lambda e, d=wzb[:, wsl], s=srcz: e.dma_start(out=d, in_=s), incs=[(dg, 16)])
            dg_w = dg.n
            wt_cb2 = [(dg, dg_w)]
            if chi == 0:
                wt_cb2.append((vv, vv_rs2))
            if chi >= 2:
                wt_cb2.append((vv, vv_y[(chi - 2) * 4 + 3]))
            def mcb2(e, wsl=wsl):
                for dk in range(8):
                    e.matmul(qB[:, 0:1], wpb[:, wsl, dk, :], lnbcol[:, dk:dk + 1],
                             start=(dk == 0), stop=(dk == 7))
                for dk in range(8):
                    ins = e.matmul(qB[:, 1:2], wzb[:, wsl, dk, :], lnbcol[:, dk:dk + 1],
                                   start=(dk == 0), stop=(dk == 7))
                return ins
            p.t(mcb2, waits=wt_cb2, incs=[(mm, 1)])
            mm_cb2 = mm.n
            p.v(lambda e, chi=chi: e.tensor_scalar(ccol[:, 0:1], qB[:, 0:1],
                                                   binfat[:, chi:chi + 1], None, OP.add),
                waits=[(mm, mm_cb2)], incs=[(vv, 1)])
            p.v(lambda e, chi=chi: e.tensor_scalar(ccol[:, 1:2], qB[:, 1:2],
                                                   binfat[:, 8 + chi:9 + chi], None, OP.add),
                waits=[(vv, vv.n)], incs=[(vv, 1)])
            for dk in range(8):
                p.v(lambda e, wsl=wsl, dk=dk: e.tensor_scalar(
                        wpb[:, wsl, dk, :], wpb[:, wsl, dk, :], gfat[:, dk:dk + 1],
                        None, OP.mult))
                p.v(lambda e, wsl=wsl, dk=dk: e.tensor_scalar(
                        wzb[:, wsl, dk, :], wzb[:, wsl, dk, :], gfat[:, dk:dk + 1],
                        None, OP.mult))
            p.v(lambda e: e.nop(), incs=[(vv, 1)])
            vv_scaled = vv.n
            for tc in range(4):
                gi = chi * 4 + tc
                psl = gi % 2
                wt_t = [(vv, vv_scaled)]
                if gi >= 2:
                    wt_t.append((vv, vv_y[gi - 2]))
                def mxz(e, wsl=wsl, tc=tc, psl=psl, chi=chi):
                    for dk in range(8):
                        e.matmul(pP3[psl][:, 0:512], wpb[:, wsl, dk, :],
                                 xnT[:, dk, tc * 512:(tc + 1) * 512],
                                 start=(dk == 0), stop=(dk == 7))
                    for dk in range(8):
                        e.matmul(pZ3[psl][:, 0:512], wzb[:, wsl, dk, :],
                                 xnT[:, dk, tc * 512:(tc + 1) * 512],
                                 start=(dk == 0), stop=(dk == 7))
                    return e.matmul(pY3[psl][:, 0:512], wct_sb[:, chi * 128:(chi + 1) * 128],
                                    x_meanTb[:, tc * 512:(tc + 1) * 512],
                                    start=True, stop=True)
                p.t(mxz, waits=wt_t, incs=[(mm, 1)])
                mm_xz = mm.n
                p.v(lambda e, psl=psl, tc=tc: e.tensor_mul(
                        zt1[:, psl, :], pZ3[psl][:, 0:512], rs2bc[:, tc * 512:(tc + 1) * 512]),
                    waits=[(mm, mm_xz)], incs=[(vv, 1)])
                p.s(lambda e, psl=psl: e.activation(zscr[:, psl, :], zt1[:, psl, :], AF.Silu,
                                                    bias=ccol[:, 1:2]),
                    waits=[(vv, vv.n)], incs=[(ev, 1)])
                ev_z = ev.n
                p.v(lambda e, psl=psl, tc=tc: e.tensor_mul(
                        yscr[:, psl, :], pP3[psl][:, 0:512], rs2bc[:, tc * 512:(tc + 1) * 512]))
                p.v(lambda e, psl=psl: e.tensor_scalar(
                        yscr[:, psl, :], yscr[:, psl, :], ccol[:, 0:1], None, OP.add))
                p.v(lambda e, psl=psl, chi=chi: e.scalar_tensor_tensor(
                        yscr[:, psl, :], yscr[:, psl, :], dfat[:, chi:chi + 1],
                        pY3[psl][:, 0:512], OP.mult, OP.add))
                p.v(lambda e, psl=psl, chi=chi, tc=tc: e.tensor_mul(
                        yT[:, chi, tc * 512:(tc + 1) * 512], yscr[:, psl, :], zscr[:, psl, :]),
                    waits=[(ev, ev_z)], incs=[(vv, 1)])
                vv_y[gi] = vv.n
            mm_chi[chi] = mm.n
        if debug:
            p.sp(lambda e: e.dma_start(out=dbg["dbg_yt"], in_=yT[:]),
                 waits=[(vv, vv.n)], incs=[(ds, 16)])
        p.sp(lambda e: e.nop(), waits=[(ds, ds.n)])
        p.run(nc)
        _ph.close()

    if max_phase < 4:
        return nc, din, outT_d, dbg

    _ph = ExitStack()
    woutb = _ph.enter_context(nc.sbuf_tensor("woutb", [128, 8, D], BF16))
    rA = _ph.enter_context(nc.psum_tensor("rA", [128, 512], F32))
    rB = _ph.enter_context(nc.psum_tensor("rB", [128, 512], F32))
    if True:
        p = Prog()
        p.g(lambda e: e.dma_start(out=woutb[:],
                                  in_=din["w_out_t"].rearrange("(dik p) do -> p dik do", p=128)),
            incs=[(dg, 16)])
        dg_w = dg.n
        rP = (rA, rB)
        vv_o = {}
        for dot in range(8):
            for tc in range(4):
                gi = dot * 4 + tc
                psl = gi % 2
                wt_t = [(dg, dg_w)]
                if gi >= 2:
                    wt_t.append((vv, vv_o[gi - 2]))
                def mwo(e, dot=dot, tc=tc, psl=psl):
                    for dik in range(8):
                        ins = e.matmul(rP[psl][:, 0:512], woutb[:, dik, dot * 128:(dot + 1) * 128],
                                       yT[:, dik, tc * 512:(tc + 1) * 512],
                                       start=(dik == 0), stop=(dik == 7))
                    return ins
                p.t(mwo, waits=wt_t, incs=[(mm, 1)])
                mm_o = mm.n
                p.v(lambda e, dot=dot, tc=tc, psl=psl: e.scalar_tensor_tensor(
                        outT[:, dot, tc * 512:(tc + 1) * 512], rP[psl][:, 0:512],
                        boutfat[:, dot:dot + 1], outT[:, dot, tc * 512:(tc + 1) * 512],
                        OP.add, OP.add),
                    waits=[(mm, mm_o)], incs=[(vv, 1)])
                vv_o[gi] = vv.n
        p.run(nc)
        _ph.close()

    yT_guard.__exit__(None, None, None)

    if max_phase < 5:
        return nc, din, outT_d, dbg

    _ph = ExitStack()
    hT = _ph.enter_context(nc.sbuf_tensor("hT", [128, 8, T], BF16))
    wgb = _ph.enter_context(nc.sbuf_tensor("wgb", [128, 2, 8, 512], BF16))
    wub = _ph.enter_context(nc.sbuf_tensor("wub", [128, 2, 8, 512], BF16))
    wdb = _ph.enter_context(nc.sbuf_tensor("wdb", [128, 2, 8, 128], BF16))
    cbc = _ph.enter_context(nc.sbuf_tensor("cbc", [128, 4, 512], BF16))
    sgscr = _ph.enter_context(nc.sbuf_tensor("sgscr", [128, 2, 512], BF16))
    uscr = _ph.enter_context(nc.sbuf_tensor("uscr", [128, 2, 512], BF16))
    mg0 = _ph.enter_context(nc.psum_tensor("mg0", [128, 512], F32))
    mg1 = _ph.enter_context(nc.psum_tensor("mg1", [128, 512], F32))
    mu0 = _ph.enter_context(nc.psum_tensor("mu0", [128, 512], F32))
    mu1 = _ph.enter_context(nc.psum_tensor("mu1", [128, 512], F32))
    mo0 = _ph.enter_context(nc.psum_tensor("mo0", [128, 512], F32))
    mo1 = _ph.enter_context(nc.psum_tensor("mo1", [128, 512], F32))
    mcb = _ph.enter_context(nc.psum_tensor("mcb", [128, 512], F32))
    if True:
        p = Prog()
        pg = (mg0, mg1); pu = (mu0, mu1); po = (mo0, mo1)
        wgu, wgufr, wd, wdfr = sm["wgu"], sm["wgufr"], sm["wd"], sm["wdfr"]
        pa, pb, hb = sm["pa"], sm["pb"], sm["hb"]
        gi5 = 0
        gb5 = 0
        gh = 0
        gw = 0
        evA = {}; vvA = {}; vvB = {}
        for e_ in range(E):
            for tcb in range(4):
                wt_cb = [(pb, pb.n)] if (e_ > 0 or tcb > 0) else []
                p.t(lambda e, e_=e_, tcb=tcb: e.matmul(
                        mcb[:, 0:512], selb[:, e_ * 128:(e_ + 1) * 128],
                        comb8[:, tcb * 512:(tcb + 1) * 512], start=True, stop=True),
                    waits=wt_cb, incs=[(pa, 1)])
                p.v(lambda e, tcb=tcb: e.tensor_copy(cbc[:, tcb, :], mcb[:, 0:512]),
                    waits=[(pa, pa.n)], incs=[(pb, 1)])
            pb_e = pb.n
            for q in range(4):
                eq = e_ * 4 + q
                for h in range(2):
                    wt_dma = [(wgufr, gh - 1)] if gh >= 2 else []
                    srcg = din["wg_t"][e_][:, q * 1024 + h * 512: q * 1024 + (h + 1) * 512] \
                        .rearrange("(dk p) i -> p dk i", p=128)
                    srcu = din["wu_t"][e_][:, q * 1024 + h * 512: q * 1024 + (h + 1) * 512] \
                        .rearrange("(dk p) i -> p dk i", p=128)
                    p.g(lambda e, d=wgb[:, gh % 2], s=srcg: e.dma_start(out=d, in_=s),
                        waits=wt_dma, incs=[(wgu, 16)])
                    p.g(lambda e, d=wub[:, gh % 2], s=srcu: e.dma_start(out=d, in_=s),
                        incs=[(wgu, 16)])
                    wgu_h = wgu.n
                    for icw in range(4):
                        ic = h * 4 + icw
                        for tc in range(4):
                            psl = gi5 % 2
                            wt_t = [(wgu, wgu_h)]
                            if e_ == 0 and q == 0 and h == 0 and icw == 0 and tc == 0:
                                wt_t.append((pb, pb_e))
                            if gi5 >= 2:
                                wt_t.append((ev2, evA[gi5 - 2]))
                                wt_t.append((vv2, vvA[gi5 - 2]))
                            def mA(e, wsl=gh % 2, icw=icw, tc=tc, psl=psl):
                                for dk in range(8):
                                    e.matmul(pg[psl][:, 0:512],
                                             wgb[:, wsl, dk, icw * 128:(icw + 1) * 128],
                                             xnT[:, dk, tc * 512:(tc + 1) * 512],
                                             start=(dk == 0), stop=(dk == 7))
                                for dk in range(8):
                                    ins = e.matmul(pu[psl][:, 0:512],
                                                   wub[:, wsl, dk, icw * 128:(icw + 1) * 128],
                                                   xnT[:, dk, tc * 512:(tc + 1) * 512],
                                                   start=(dk == 0), stop=(dk == 7))
                                return ins
                            last_of_half = (icw == 3 and tc == 3)
                            incs_t = [(mm2, 1)] + ([(wgufr, 1)] if last_of_half else [])
                            p.t(mA, waits=wt_t, incs=incs_t)
                            mm2_a = mm2.n
                            wt_sg = [(mm2, mm2_a)]
                            if gi5 >= 2:
                                wt_sg.append((vv2, vvA[gi5 - 2]))
                            p.s(lambda e, psl=psl: e.activation(sgscr[:, psl, :],
                                                                pg[psl][:, 0:512], AF.Silu),
                                waits=wt_sg, incs=[(ev2, 1)])
                            evA[gi5] = ev2.n
                            wt_u = [(mm2, mm2_a)]
                            if eq >= 1:
                                wt_u.append((hb, eq))
                            p.v(lambda e, psl=psl, tc=tc: e.tensor_mul(
                                    uscr[:, psl, :], pu[psl][:, 0:512], cbc[:, tc, :]),
                                waits=wt_u)
                            p.v(lambda e, psl=psl, ic=ic, tc=tc: e.tensor_mul(
                                    hT[:, ic, tc * 512:(tc + 1) * 512],
                                    sgscr[:, psl, :], uscr[:, psl, :]),
                                waits=[(ev2, evA[gi5])], incs=[(vv2, 1)])
                            vvA[gi5] = vv2.n
                            gi5 += 1
                    gh += 1
                vv2_heq = vv2.n
                for dmt in range(8):
                    wt_dma = [(wdfr, gw - 1)] if gw >= 2 else []
                    srcd = din["wd_t"][e_][q * 1024:(q + 1) * 1024, dmt * 128:(dmt + 1) * 128] \
                        .rearrange("(ik p) d -> p ik d", p=128)
                    p.g(lambda e, d=wdb[:, gw % 2], s=srcd: e.dma_start(out=d, in_=s),
                        waits=wt_dma, incs=[(wd, 16)])
                    wd_c = wd.n
                    for tc in range(4):
                        psl = gb5 % 2
                        wt_t = [(wd, wd_c)]
                        if tc == 0 and dmt == 0:
                            wt_t.append((vv2, vv2_heq))
                        if gb5 >= 2:
                            wt_t.append((vv2, vvB[gb5 - 2]))
                        def mB(e, wsl=gw % 2, tc=tc, psl=psl):
                            for ik in range(8):
                                ins = e.matmul(po[psl][:, 0:512], wdb[:, wsl, ik, :],
                                               hT[:, ik, tc * 512:(tc + 1) * 512],
                                               start=(ik == 0), stop=(ik == 7))
                            return ins
                        last_of_chunk = (tc == 3)
                        last_of_eq = (dmt == 7 and tc == 3)
                        incs_t = [(mm2, 1)] + ([(wdfr, 1)] if last_of_chunk else []) \
                            + ([(hb, 1)] if last_of_eq else [])
                        p.t(mB, waits=wt_t, incs=incs_t)
                        mm2_b = mm2.n
                        p.v(lambda e, dmt=dmt, tc=tc, psl=psl: e.tensor_add(
                                outT[:, dmt, tc * 512:(tc + 1) * 512],
                                outT[:, dmt, tc * 512:(tc + 1) * 512], po[psl][:, 0:512]),
                            waits=[(mm2, mm2_b)], incs=[(vv2, 1)])
                        vvB[gb5] = vv2.n
                        gb5 += 1
                    gw += 1
        p.run(nc)
        _ph.close()

    p = Prog()
    p.sp(lambda e: e.dma_start(out=outT_d.rearrange("(dk p) t -> p dk t", p=128),
                               in_=outT[:]),
         incs=[(st, 16)])
    p.sp(lambda e: e.nop(), waits=[(st, st.n)])
    p.run(nc)
    return nc, din, outT_d, dbg


def _stage_inputs(inputs):
    f = np.float32
    x = np.asarray(inputs["L0_attn_x"], dtype=f)
    xT = np.ascontiguousarray(x[0].T)
    W_in = np.asarray(inputs["L0_attn_W_in"], f)
    W_out = np.asarray(inputs["L0_attn_W_out"], f)
    W_B = np.asarray(inputs["L0_attn_W_B"], f)
    W_C = np.asarray(inputs["L0_attn_W_C"], f)
    A_log = np.asarray(inputs["L0_attn_A_log"], f)
    D_skip = np.asarray(inputs["L0_attn_D_skip"], f)
    log_dt = np.asarray(inputs["L0_attn_log_dt"], f)
    b_in = np.asarray(inputs["L0_attn_b_in"], f)
    b_out = np.asarray(inputs["L0_attn_b_out"], f)
    ln_g = np.asarray(inputs["L0_attn_ln_g"], f)
    ln_b = np.asarray(inputs["L0_attn_ln_b"], f)
    gate_w = np.asarray(inputs["L0_ffn_gate_w"], f)
    wg = np.asarray(inputs["L0_ffn_wg"], f)
    wu = np.asarray(inputs["L0_ffn_wu"], f)
    wd = np.asarray(inputs["L0_ffn_wd"], f)

    shared = {
        "w_in_t": np.ascontiguousarray(W_in.T),
        "w_in_upn": np.ascontiguousarray(W_in[0:D, :]),
        "w_out_t": np.ascontiguousarray(W_out.T),
        "w_b_t": np.ascontiguousarray(W_B.T),
        "w_c_t": np.ascontiguousarray(W_C.T),
        "gate_t": np.ascontiguousarray(gate_w.T),
        "wg_t": np.ascontiguousarray(wg.transpose(0, 2, 1)),
        "wu_t": np.ascontiguousarray(wu.transpose(0, 2, 1)),
        "wd_t": np.ascontiguousarray(wd.transpose(0, 2, 1)),
        "g_fat": np.ascontiguousarray(ln_g.reshape(8, 128).T),
        "b_fat": np.ascontiguousarray(ln_b.reshape(8, 128).T),
        "d_fat": np.ascontiguousarray(D_skip.reshape(8, 128).T),
        "logdt_fat": np.ascontiguousarray(log_dt.reshape(8, 128).T),
        "bout_fat": np.ascontiguousarray(b_out.reshape(8, 128).T),
        "bin_fat": np.ascontiguousarray(b_in.reshape(16, 128).T),
        "binup_col": np.ascontiguousarray(b_in[0:D].reshape(8, 128).T),
        "lnb_col": np.ascontiguousarray(ln_b.reshape(8, 128).T),
        "alog_fat": np.ascontiguousarray(
            A_log.reshape(8, 128, S).transpose(1, 0, 2).reshape(128, 128)),
    }
    in_maps = []
    for c in range(NCORE):
        pad = (NCORE - 1 - c) * T
        xt_c = np.zeros((D, L), f)
        if pad:
            xt_c[:, pad:] = xT[:, :L - pad]
        else:
            xt_c[:] = xT
        mask = np.zeros((128, J), f)
        mask[:, JOWN0 - 16 * c:] = 1.0
        in_maps.append({**shared, "xt": xt_c, "mask_cm": mask})
    return in_maps


def kernel(**inputs):
    in_maps = _stage_inputs(inputs)
    nc, din, outT_d, dbg = build(debug=False)
    res = run_bass_kernel_spmd(nc, in_maps, core_ids=list(range(NCORE)))
    outs = [np.asarray(res.results[c]["outT"]) for c in range(NCORE)]
    full = np.concatenate(outs, axis=1)
    return np.ascontiguousarray(full.T).reshape(1, L, D).astype(np.float32)


# version 02
# speedup vs baseline: 1.2060x; 1.2060x over previous
import dataclasses
from contextlib import ExitStack
import numpy as np
import ml_dtypes
import concourse.bass as bass
import concourse.mybir as mybir
from concourse.bass_utils import run_bass_kernel_spmd

F32 = mybir.dt.float32
BF16 = mybir.dt.bfloat16
AF = mybir.ActivationFunctionType
AX = mybir.AxisListType
OP = mybir.AluOpType

D = 1024
S = 16
L = 16384
E = 8
II = 4096
NCORE = 8
T = 2048
TC = 128
J = L // TC
JOWN0 = J - T // TC
EPS = 1e-5
LOG_LO = float(np.log(np.float64(1e-8)))
LOG_HI = float(np.log1p(np.float64(-1e-8)))
DTL, DTH = 1e-4, 1.0
NEG_BIG = -1.0e30


class Sem:
    def __init__(self, nc, name):
        self.h = nc.alloc_semaphore(name)
        self.n = 0


class Prog:
    ENGINES = ("tensor", "vector", "scalar", "gpsimd", "sync")

    def __init__(self):
        self.items = {e: [] for e in self.ENGINES}

    def emit(self, eng, fn, waits=(), incs=()):
        waits = [(s.h, n) for (s, n) in waits if n > 0]
        snap = []
        for (s, k) in incs:
            s.n += k
            snap.append((s.h, k))

        def closure(engine, _w=waits, _f=fn, _i=snap):
            for (h, n) in _w:
                engine.wait_ge(h, n)
            ins = _f(engine)
            if _i:
                ins.then_inc(*_i[0])
                for (h, k) in _i[1:]:
                    engine.nop().then_inc(h, k)
        self.items[eng].append(closure)

    def t(self, fn, **kw): self.emit("tensor", fn, **kw)
    def v(self, fn, **kw): self.emit("vector", fn, **kw)
    def s(self, fn, **kw): self.emit("scalar", fn, **kw)
    def g(self, fn, **kw): self.emit("gpsimd", fn, **kw)
    def sp(self, fn, **kw): self.emit("sync", fn, **kw)

    def run(self, nc):
        with nc.Block() as block:
            for e in self.ENGINES:
                lst = self.items[e]
                if not lst:
                    continue
                getattr(block, e)(lambda eng, lst=lst: [f(eng) for f in lst])


def dram_ap(ap0, offset, pattern):
    return dataclasses.replace(ap0, offset=offset, ap=pattern)


def build(debug=False, max_phase=9):
    nc = bass.Bass("TRN2", target_bir_lowering=False, detect_race_conditions=False)

    din = {}
    def di(name, shape, dt=F32):
        din[name] = nc.dram_tensor(name, list(shape), dt, kind="ExternalInput").ap()
    di("xt", (D, L))
    di("mask_cm", (128, J))
    di("w_in_t", (D, 2 * D))
    di("w_in_upn", (D, D))
    di("w_out_t", (D, D))
    di("w_b_t", (D, S))
    di("w_c_t", (S, D))
    di("gate_t", (D, E))
    di("wg_t", (E, D, II))
    di("wu_t", (E, D, II))
    di("wd_t", (E, II, D))
    di("g_fat", (128, 8)); di("b_fat", (128, 8)); di("d_fat", (128, 8))
    di("logdt_fat", (128, 8)); di("bout_fat", (128, 8)); di("bin_fat", (128, 16))
    di("binup_col", (128, 8)); di("lnb_col", (128, 8))
    di("alog_fat", (128, 128))

    outT_d = nc.dram_tensor("outT", [D, T], F32, kind="ExternalOutput").ap()
    dbg = {}
    def do_(name, shape, dt=F32):
        if debug:
            dbg[name] = nc.dram_tensor(name, list(shape), dt, kind="ExternalOutput").ap()
    do_("dbg_bcm", (128, S, J), BF16)
    do_("dbg_mu", (128, J)); do_("dbg_rs1", (128, J)); do_("dbg_q2", (128, J))
    do_("dbg_xm", (S, T))
    do_("dbg_xnt", (128, 8, T), BF16)
    do_("dbg_comb8", (E, T), BF16)
    do_("dbg_yt", (128, 8, T), BF16)
    do_("dbg_logafat", (128, 128))
    do_("dbg_statw", (128, 8, 18), BF16)
    do_("dbg_rows", (1, 64))
    do_("dbg_cbm", (128, S)); do_("dbg_negsw", (128, S))
    do_("dbg_pcm", (128, S, 128))
    do_("dbg_sown", (128, 8, S, 16))

    ident_bf_d = nc.inline_tensor(np.eye(128, dtype=ml_dtypes.bfloat16), name="c_ident_bf").ap()
    ident_f32_d = nc.inline_tensor(np.eye(128, dtype=np.float32), name="c_ident_f32").ap()
    jrev_np = np.zeros((128, 128), dtype=ml_dtypes.bfloat16)
    for c in range(128):
        jrev_np[127 - c, c] = 1.0
    jrev_d = nc.inline_tensor(jrev_np, name="c_jrev").ap()
    ones_r_b_d = nc.inline_tensor(np.ones((1, 128), dtype=ml_dtypes.bfloat16), name="c_ones_r_b").ap()
    ones_r_f_d = nc.inline_tensor(np.ones((1, 128), dtype=np.float32), name="c_ones_r_f").ap()
    ones_c_b_d = nc.inline_tensor(np.ones((128, 1), dtype=ml_dtypes.bfloat16), name="c_ones_c_b").ap()
    ones_c_f_d = nc.inline_tensor(np.ones((128, 1), dtype=np.float32), name="c_ones_c_f").ap()
    iota_p1_d = nc.inline_tensor(np.broadcast_to(np.arange(1, 129, dtype=np.float32),
                                                 (128, 128)).copy(), name="c_iota_p1").ap()
    iota_r2_d = nc.inline_tensor(np.broadcast_to(127.0 - np.arange(128, dtype=np.float32),
                                                 (128, 128)).copy(), name="c_iota_r2").ap()
    sel_np = np.zeros((E, E * 128), dtype=ml_dtypes.bfloat16)
    for e in range(E):
        sel_np[e, e * 128:(e + 1) * 128] = 1.0
    sel_d = nc.inline_tensor(sel_np, name="c_sel").ap()

    kpad2_d = nc.dram_tensor("kpad2_d", [S, 128, 255], BF16, kind="Internal")
    xm_d = nc.dram_tensor("xm_d", [S, T], F32, kind="Internal")
    rsq_d = nc.dram_tensor("rsq_d", [3, T], F32, kind="Internal")

    names = ("dg", "ds", "mm", "ev", "vv", "mm2", "vv2", "ev2", "st",
             "wgu", "wgufr", "wd", "wdfr", "pa", "pb", "hb")
    sm = {k: Sem(nc, k) for k in names}
    dg, ds, mm, ev, vv = sm["dg"], sm["ds"], sm["mm"], sm["ev"], sm["vv"]
    mm2, vv2, ev2, st = sm["mm2"], sm["vv2"], sm["ev2"], sm["st"]

    ctx = nc.ctx
    def sb(name, shape, dt=F32):
        return ctx.enter_context(nc.sbuf_tensor(name, list(shape), dt))

    statW = sb("statW", (128, 8, 18), BF16)
    wct_sb = sb("wct_sb", (S, D), BF16)
    gate_f = sb("gate_f", (128, 8, E), F32)
    gfat = sb("gfat", (128, 8)); bfat_ = sb("bfat", (128, 8))
    dfat = sb("dfat", (128, 8))
    boutfat = sb("boutfat", (128, 8)); binfat = sb("binfat", (128, 16))
    logafat = sb("logafat", (128, 128))
    a128fat = sb("a128fat", (128, 128))
    dtm_bc = sb("dtm_bc", (128, 1))
    cbm_bc = sb("cbm_bc", (128, S))
    negsw_bc = sb("negsw_bc", (128, S))
    sg_bc = sb("sg_bc", (128, E))
    identb = sb("identb", (128, 128), BF16)
    identf = sb("identf", (128, 128), F32)
    jrevb = sb("jrevb", (128, 128), BF16)
    ones_r_b = sb("ones_r_b", (1, 128), BF16)
    ones_r_f = sb("ones_r_f", (1, 128), F32)
    ones_c_b = sb("ones_c_b", (128, 1), BF16)
    ones_c_f = sb("ones_c_f", (128, 1), F32)
    iota_p1 = sb("iota_p1", (128, 128), F32)
    iota_r2 = sb("iota_r2", (128, 128), F32)
    selb = sb("selb", (E, E * 128), BF16)
    mask_sb = sb("mask_sb", (128, J), F32)
    B_cm = sb("B_cm", (128, S, J), BF16)
    x_meanTb = sb("x_meanTb", (S, T), BF16)
    comb8 = sb("comb8", (E, T), BF16)
    rs1_cm = sb("rs1_cm", (128, J))
    xnq_cm = sb("xnq_cm", (128, J))
    lnbcol = sb("lnbcol", (128, 8), BF16)

    _ph = ExitStack()
    winupn = _ph.enter_context(nc.sbuf_tensor("winupn", [128, 8, D], BF16))
    wbt = _ph.enter_context(nc.sbuf_tensor("wbt", [128, 8, S], BF16))
    binup = _ph.enter_context(nc.sbuf_tensor("binup", [128, 8], BF16))
    wbbraw = _ph.enter_context(nc.sbuf_tensor("wbbraw", [128, 8, S], BF16))
    p0rows = _ph.enter_context(nc.sbuf_tensor("p0rows", [1, 64], F32))
    alogfat = _ph.enter_context(nc.sbuf_tensor("alogfat", [128, 128], F32))
    ldtfat = _ph.enter_context(nc.sbuf_tensor("ldtfat", [128, 8], F32))
    dt_fat = _ph.enter_context(nc.sbuf_tensor("dt_fat", [128, 8], F32))
    dt8 = _ph.enter_context(nc.sbuf_tensor("dt8", [128, 1], F32))
    ps0 = _ph.enter_context(nc.psum_tensor("ps0", [128, 512], F32))
    ps1 = _ph.enter_context(nc.psum_tensor("ps1", [128, 512], F32))
    ps2 = _ph.enter_context(nc.psum_tensor("ps2", [128, 512], F32))
    if True:
        p = Prog()
        def ld_s(dst, src):
            p.sp(lambda e, d=dst, s=src: e.dma_start(out=d, in_=s), incs=[(ds, 16)])
        def ld_g(dst, src):
            p.g(lambda e, d=dst, s=src: e.dma_start(out=d, in_=s), incs=[(dg, 16)])
        ld_s(gfat[:], din["g_fat"]); ld_s(bfat_[:], din["b_fat"])
        ld_s(dfat[:], din["d_fat"]); ld_s(ldtfat[:], din["logdt_fat"])
        ld_s(boutfat[:], din["bout_fat"]); ld_s(binfat[:], din["bin_fat"])
        ld_s(alogfat[:], din["alog_fat"]); ld_s(mask_sb[:], din["mask_cm"])
        ld_s(identf[:], ident_f32_d); ld_s(ones_r_f[:], ones_r_f_d)
        ld_s(ones_c_f[:], ones_c_f_d)
        ld_s(iota_p1[:], iota_p1_d); ld_s(iota_r2[:], iota_r2_d)
        ld_s(gate_f[:], din["gate_t"].rearrange("(dk p) e -> p dk e", p=128))
        ld_s(identb[:], ident_bf_d); ld_s(jrevb[:], jrev_d)
        ld_s(ones_r_b[:], ones_r_b_d); ld_s(ones_c_b[:], ones_c_b_d)
        ld_s(selb[:], sel_d)
        n_lds = ds.n
        ld_g(wct_sb[:], din["w_c_t"])
        ld_g(winupn[:], din["w_in_upn"].rearrange("(dok p) di -> p dok di", p=128))
        ld_g(wbt[:], din["w_b_t"].rearrange("(dok p) s -> p dok s", p=128))
        ld_g(binup[:], din["binup_col"])
        ld_g(lnbcol[:], din["lnb_col"])
        n_ldg = dg.n

        p.v(lambda e: e.memset(p0rows[:], 0.0))
        p.v(lambda e: e.memset(statW[:, :, 0:1], 1.0), waits=[(ds, n_lds)])
        p.v(lambda e: e.memset(statW[:, :, 17:18], 0.0))
        p.s(lambda e: e.activation(dt_fat[:], ldtfat[:], AF.Exp),
            waits=[(ds, n_lds)], incs=[(ev, 1)])
        ev_dt = ev.n
        p.v(lambda e: e.tensor_scalar(dt_fat[:], dt_fat[:], DTL, DTH, OP.max, OP.min),
            waits=[(ev, ev_dt)], incs=[(vv, 1)])
        vv_dt = vv.n
        p.s(lambda e: e.activation(logafat[:], alogfat[:], AF.Exp), incs=[(ev, 1)])
        ev_la = ev.n
        for dhi in range(8):
            p.v(lambda e, k=dhi: e.tensor_scalar(logafat[:, k * 16:(k + 1) * 16],
                                                 logafat[:, k * 16:(k + 1) * 16],
                                                 dt_fat[:, k:k + 1], -1.0, OP.mult, OP.mult),
                waits=[(ev, ev_la), (vv, vv_dt)] if dhi == 0 else [])
        p.v(lambda e: e.nop(), incs=[(vv, 1)])
        p.v(lambda e: e.tensor_scalar(logafat[:], logafat[:], LOG_LO, LOG_HI, OP.max, OP.min),
            waits=[(vv, vv.n)], incs=[(vv, 1)])
        vv_lf = vv.n
        p.s(lambda e: e.activation(a128fat[:], logafat[:], AF.Exp, scale=128.0),
            waits=[(vv, vv_lf)])
        p.v(lambda e: e.reduce_sum(dt8[:], dt_fat[:], axis=AX.X), incs=[(vv, 1)])
        vv_dt8 = vv.n
        p.t(lambda e: e.matmul(ps0[0:1, 0:1], ones_c_f[:], dt8[:], start=True, stop=True),
            waits=[(vv, vv_dt8)], incs=[(mm, 1)])
        mm_dtm = mm.n
        p.v(lambda e: e.tensor_scalar(p0rows[0:1, 0:1], ps0[0:1, 0:1], 1.0 / D, None, OP.mult),
            waits=[(mm, mm_dtm)], incs=[(vv, 1)])
        vv_dtm = vv.n
        p.t(lambda e: e.matmul(ps1[:, 0:1], ones_r_f[:], p0rows[0:1, 0:1], start=True, stop=True),
            waits=[(vv, vv_dtm)], incs=[(mm, 1)])
        mm_dbc = mm.n
        p.v(lambda e: e.tensor_copy(dtm_bc[:], ps1[:, 0:1]), waits=[(mm, mm_dbc)], incs=[(vv, 1)])

        vv_wbb = {}
        for dik in range(8):
            def mmf(e, dik=dik):
                for dok in range(8):
                    ins = e.matmul(ps2[:, 0:S], winupn[:, dok, dik * 128:(dik + 1) * 128],
                                   wbt[:, dok, :], start=(dok == 0), stop=(dok == 7))
                return ins
            wt = [(dg, n_ldg)] if dik == 0 else [(vv, vv_wbb[dik - 1])]
            p.t(mmf, waits=wt, incs=[(mm, 1)])
            mm_w = mm.n
            p.v(lambda e, dik=dik: e.tensor_copy(wbbraw[:, dik, :], ps2[:, 0:S]),
                waits=[(mm, mm_w)])
            p.v(lambda e, dik=dik: e.tensor_scalar(statW[:, dik, 1:17], ps2[:, 0:S],
                                                   gfat[:, dik:dik + 1], None, OP.mult),
                incs=[(vv, 1)])
            vv_wbb[dik] = vv.n
        def wbf(e):
            for dik in range(8):
                ins = e.matmul(ps0[0:1, 4:20], lnbcol[:, dik:dik + 1], wbbraw[:, dik, :],
                               start=(dik == 0), stop=(dik == 7))
            return ins
        p.t(wbf, waits=[(vv, vv_wbb[7])], incs=[(mm, 1)])
        def cbf(e):
            for dok in range(8):
                ins = e.matmul(ps1[0:1, 4:20], binup[:, dok:dok + 1], wbt[:, dok, :],
                               start=(dok == 0), stop=(dok == 7))
            return ins
        p.t(cbf, incs=[(mm, 1)])
        def swf(e):
            for dik in range(8):
                ins = e.matmul(ps2[0:1, 0:S], ones_c_b[:], statW[:, dik, 1:17],
                               start=(dik == 0), stop=(dik == 7))
            return ins
        p.t(swf, incs=[(mm, 1)])
        def sgf(e):
            for dk in range(8):
                ins = e.matmul(ps0[0:1, 24:32], ones_c_f[:], gate_f[:, dk, :],
                               start=(dk == 0), stop=(dk == 7))
            return ins
        p.t(sgf, incs=[(mm, 1)])
        mm_p0 = mm.n
        p.v(lambda e: e.tensor_copy(p0rows[0:1, 4:20], ps0[0:1, 4:20]), waits=[(mm, mm_p0)],
            incs=[(vv, 1)])
        p.v(lambda e: e.tensor_add(p0rows[0:1, 4:20], p0rows[0:1, 4:20], ps1[0:1, 4:20]),
            waits=[(vv, vv.n)], incs=[(vv, 1)])
        p.v(lambda e: e.tensor_scalar(p0rows[0:1, 20:36], ps2[0:1, 0:S], -1.0, None, OP.mult))
        p.v(lambda e: e.tensor_copy(p0rows[0:1, 40:48], ps0[0:1, 24:32]), waits=[(vv, vv.n)],
            incs=[(vv, 1)])
        vv_rows = vv.n
        p.t(lambda e: e.matmul(ps0[:, 32:48], ones_r_f[:], p0rows[0:1, 4:20], start=True, stop=True),
            waits=[(vv, vv_rows)])
        p.t(lambda e: e.matmul(ps1[:, 32:48], ones_r_f[:], p0rows[0:1, 20:36], start=True, stop=True))
        p.t(lambda e: e.matmul(ps2[:, 32:40], ones_r_f[:], p0rows[0:1, 40:48], start=True, stop=True),
            incs=[(mm, 1)])
        mm_bc = mm.n
        p.v(lambda e: e.tensor_scalar(cbm_bc[:], ps0[:, 32:48], dtm_bc[:, 0:1], None, OP.mult),
            waits=[(mm, mm_bc)])
        p.v(lambda e: e.tensor_copy(negsw_bc[:], ps1[:, 32:48]))
        p.v(lambda e: e.tensor_copy(sg_bc[:], ps2[:, 32:40]), incs=[(vv, 1)])
        if debug:
            vfin = vv.n
            p.sp(lambda e: e.dma_start(out=dbg["dbg_logafat"], in_=logafat[:]),
                 waits=[(vfin_s, vfin) for vfin_s in (vv,)], incs=[(ds, 16)])
            p.sp(lambda e: e.dma_start(out=dbg["dbg_statw"], in_=statW[:]), incs=[(ds, 16)])
            p.sp(lambda e: e.dma_start(out=dbg["dbg_cbm"], in_=cbm_bc[:]), incs=[(ds, 16)])
            p.sp(lambda e: e.dma_start(out=dbg["dbg_negsw"], in_=negsw_bc[:]), incs=[(ds, 16)])
            p.sp(lambda e: e.dma_start(out=dbg["dbg_rows"], in_=p0rows[:]), incs=[(ds, 16)])
            p.sp(lambda e: e.nop(), waits=[(ds, ds.n)])
        p.run(nc)
        _ph.close()

    if max_phase < 1:
        return nc, din, outT_d, dbg

    _ph = ExitStack()
    xbf = _ph.enter_context(nc.sbuf_tensor("xbf", [128, 2, 8, 512], BF16))
    xsq = _ph.enter_context(nc.sbuf_tensor("xsq", [128, 2, 8, 512], BF16))
    P_cm = _ph.enter_context(nc.sbuf_tensor("P_cm", [128, S, 128], F32))
    sumx_cm = _ph.enter_context(nc.sbuf_tensor("sumx_cm", [128, 128], F32))
    sumsq_cm = _ph.enter_context(nc.sbuf_tensor("sumsq_cm", [128, 128], F32))
    scrA = _ph.enter_context(nc.sbuf_tensor("scrA", [128, 128], F32))
    scrB = _ph.enter_context(nc.sbuf_tensor("scrB", [128, 128], F32))
    mu_cm = _ph.enter_context(nc.sbuf_tensor("mu_cm", [128, J], F32))
    u_cm = _ph.enter_context(nc.sbuf_tensor("u_cm", [128, J], F32))
    q2_cm = _ph.enter_context(nc.sbuf_tensor("q2_cm", [128, J], F32))
    rs2_cm = _ph.enter_context(nc.sbuf_tensor("rs2_cm", [128, J], F32))
    t2f = _ph.enter_context(nc.sbuf_tensor("t2f", [16, 3, 128], F32))
    pP0 = _ph.enter_context(nc.psum_tensor("pP0", [128, 512], F32))
    pP1 = _ph.enter_context(nc.psum_tensor("pP1", [128, 512], F32))
    pQ0 = _ph.enter_context(nc.psum_tensor("pQ0", [128, 512], F32))
    pQ1 = _ph.enter_context(nc.psum_tensor("pQ1", [128, 512], F32))
    pT = _ph.enter_context(nc.psum_tensor("pT", [128, 512], F32))
    if True:
        p = Prog()
        pP = (pP0, pP1); pQ = (pQ0, pQ1)
        vv_grp = {}; ev_chunk = {}; mm_chunk = {}
        for tc in range(32):
            slot = tc % 2
            wt = []
            if tc >= 2:
                wt = [(mm, mm_chunk[tc - 2]), (ev, ev_chunk[tc - 2])]
            src = din["xt"][:, tc * 512:(tc + 1) * 512].rearrange("(dk p) t -> p dk t", p=128)
            p.g(lambda e, d=xbf[:, slot], s=src: e.dma_start(out=d, in_=s),
                waits=wt, incs=[(dg, 16)])
            dg_c = dg.n
            wt_s = [(dg, dg_c)]
            if tc >= 2:
                wt_s.append((mm, mm_chunk[tc - 2]))
            p.s(lambda e, sl=slot: e.activation(
                    xsq[:, sl].rearrange("p a b -> p (a b)"),
                    xbf[:, sl].rearrange("p a b -> p (a b)"), AF.Square),
                waits=wt_s, incs=[(ev, 1)])
            ev_chunk[tc] = ev.n
            for jj in range(4):
                gidx = tc * 4 + jj
                psl = gidx % 2
                wt_t = [(dg, dg_c), (ev, ev_chunk[tc])]
                if gidx >= 2:
                    wt_t.append((vv, vv_grp[gidx - 2]))
                def mP(e, sl=slot, jj=jj, psl=psl):
                    for dk in range(8):
                        ins = e.matmul(pP[psl][:, 0:18], xbf[:, sl, dk, jj * 128:(jj + 1) * 128],
                                       statW[:, dk, :], start=(dk == 0), stop=(dk == 7))
                    return ins
                p.t(mP, waits=wt_t)
                def mQ(e, sl=slot, jj=jj, psl=psl):
                    for dk in range(8):
                        ins = e.matmul(pQ[psl][:, 0:1], xsq[:, sl, dk, jj * 128:(jj + 1) * 128],
                                       ones_c_b[:], start=(dk == 0), stop=(dk == 7))
                    return ins
                p.t(mQ, incs=[(mm, 1)])
                mm_g = mm.n
                jg = gidx
                p.v(lambda e, psl=psl, jg=jg: e.tensor_copy(P_cm[:, :, jg], pP[psl][:, 1:17]),
                    waits=[(mm, mm_g)])
                p.v(lambda e, psl=psl, jg=jg: e.tensor_copy(sumx_cm[:, jg:jg + 1], pP[psl][:, 0:1]))
                p.v(lambda e, psl=psl, jg=jg: e.tensor_copy(sumsq_cm[:, jg:jg + 1], pQ[psl][:, 0:1]),
                    incs=[(vv, 1)])
                vv_grp[gidx] = vv.n
            mm_chunk[tc] = mm.n
        p.v(lambda e: e.tensor_scalar(mu_cm[:], sumx_cm[:], 1.0 / D, None, OP.mult),
            waits=[(vv, vv.n)])
        p.v(lambda e: e.tensor_scalar(scrA[:], sumsq_cm[:], 1.0 / D, None, OP.mult),
            incs=[(vv, 1)])
        vv_a = vv.n
        p.s(lambda e: e.activation(scrB[:], mu_cm[:], AF.Square), waits=[(vv, vv_a)], incs=[(ev, 1)])
        ev_a = ev.n
        p.v(lambda e: e.tensor_sub(scrA[:], scrA[:], scrB[:]), waits=[(ev, ev_a)])
        p.v(lambda e: e.tensor_scalar(scrB[:], scrA[:], EPS, None, OP.add), incs=[(vv, 1)])
        vv_b = vv.n
        p.s(lambda e: e.activation(scrB[:], scrB[:], AF.Ln), waits=[(vv, vv_b)])
        p.s(lambda e: e.activation(rs1_cm[:], scrB[:], AF.Exp, scale=-0.5), incs=[(ev, 1)])
        ev_b = ev.n
        p.v(lambda e: e.tensor_scalar(scrB[:], scrA[:], EPS, None, OP.add), waits=[(ev, ev_b)])
        p.v(lambda e: e.reciprocal(scrB[:], scrB[:]))
        p.v(lambda e: e.tensor_mul(scrB[:], scrA[:], scrB[:]))
        p.v(lambda e: e.tensor_scalar(scrB[:], scrB[:], EPS, None, OP.add), incs=[(vv, 1)])
        vv_c2 = vv.n
        p.s(lambda e: e.activation(scrB[:], scrB[:], AF.Ln), waits=[(vv, vv_c2)])
        p.s(lambda e: e.activation(rs2_cm[:], scrB[:], AF.Exp, scale=-0.5), incs=[(ev, 1)])
        ev_c = ev.n
        p.v(lambda e: e.scalar_tensor_tensor(q2_cm[:], rs1_cm[:], dtm_bc[:, 0:1], rs2_cm[:],
                                             OP.mult, OP.mult), waits=[(ev, ev_c)])
        p.v(lambda e: e.tensor_mul(xnq_cm[:], mu_cm[:], rs1_cm[:]), incs=[(vv, 1)])
        vv_fin = vv.n
        for s_ in range(S):
            p.v(lambda e, s_=s_: e.scalar_tensor_tensor(scrA[:], mu_cm[:], negsw_bc[:, s_:s_ + 1],
                                                        P_cm[:, s_, :], OP.mult, OP.add),
                waits=[(vv, vv_fin)] if s_ == 0 else [])
            p.v(lambda e: e.tensor_mul(scrA[:], scrA[:], q2_cm[:]))
            p.v(lambda e, s_=s_: e.tensor_scalar(scrA[:], scrA[:], cbm_bc[:, s_:s_ + 1], None, OP.add))
            p.v(lambda e, s_=s_: e.tensor_mul(B_cm[:, s_, :], scrA[:], mask_sb[:]),
                incs=[(vv, 1)])
        vv_bcm = vv.n
        p.t(lambda e: e.transpose(pT[0:16, 0:128], rs1_cm[:, JOWN0:J], identf[:]),
            waits=[(vv, vv_fin)], incs=[(mm, 1)])
        mm_t1 = mm.n
        p.v(lambda e: e.tensor_copy(t2f[:, 0, :], pT[0:16, 0:128]), waits=[(mm, mm_t1)],
            incs=[(vv, 1)])
        vv_t1 = vv.n
        p.t(lambda e: e.transpose(pT[0:16, 128:256], xnq_cm[:, JOWN0:J], identf[:]),
            waits=[(vv, vv_t1)], incs=[(mm, 1)])
        mm_t2 = mm.n
        p.v(lambda e: e.tensor_copy(t2f[:, 1, :], pT[0:16, 128:256]), waits=[(mm, mm_t2)],
            incs=[(vv, 1)])
        vv_t2 = vv.n
        p.sp(lambda e: e.dma_start(out=dram_ap(rsq_d.ap(), 0, [[128, 16], [1, 128]]),
                                   in_=t2f[:, 0, :]),
             waits=[(vv, vv_t2)], incs=[(ds, 16)])
        p.sp(lambda e: e.dma_start(out=dram_ap(rsq_d.ap(), T, [[128, 16], [1, 128]]),
                                   in_=t2f[:, 1, :]), incs=[(ds, 16)])
        p.t(lambda e: e.transpose(pT[0:16, 256:384], rs2_cm[:, JOWN0:J], identf[:]),
            waits=[(vv, vv_t2)], incs=[(mm, 1)])
        mm_t3 = mm.n
        p.v(lambda e: e.tensor_copy(t2f[:, 2, :], pT[0:16, 256:384]), waits=[(mm, mm_t3)],
            incs=[(vv, 1)])
        vv_t3 = vv.n
        p.sp(lambda e: e.dma_start(out=dram_ap(rsq_d.ap(), 2 * T, [[128, 16], [1, 128]]),
                                   in_=t2f[:, 2, :]),
             waits=[(vv, vv_t3)], incs=[(ds, 16)])
        if debug:
            p.sp(lambda e: e.dma_start(out=dbg["dbg_bcm"], in_=B_cm[:]),
                 waits=[(vv, vv_bcm)], incs=[(ds, 16)])
            p.sp(lambda e: e.dma_start(out=dbg["dbg_mu"], in_=mu_cm[:]), incs=[(ds, 16)])
            p.sp(lambda e: e.dma_start(out=dbg["dbg_rs1"], in_=rs1_cm[:]), incs=[(ds, 16)])
            p.sp(lambda e: e.dma_start(out=dbg["dbg_q2"], in_=q2_cm[:]), incs=[(ds, 16)])
            p.sp(lambda e: e.dma_start(out=dbg["dbg_pcm"], in_=P_cm[:]), incs=[(ds, 16)])
        p.sp(lambda e: e.nop(), waits=[(ds, ds.n)])
        p.run(nc)
        _ph.close()

    if max_phase < 2:
        return nc, din, outT_d, dbg

    _ph = ExitStack()
    apowc = _ph.enter_context(nc.sbuf_tensor("apowc", [128, 128, 128], BF16))
    gsc_c = _ph.enter_context(nc.sbuf_tensor("gsc_c", [128, 2, 16, 128], BF16))
    arev_c = _ph.enter_context(nc.sbuf_tensor("arev_c", [128, 4, 128], BF16))
    loc_all = _ph.enter_context(nc.sbuf_tensor("loc_all", [128, 8, S, 128], BF16))
    scrbig = _ph.enter_context(nc.sbuf_tensor("scrbig", [128, 2, 2048], F32))
    S_own = _ph.enter_context(nc.sbuf_tensor("S_own", [128, 8, S, 16], F32))
    S_bf = _ph.enter_context(nc.sbuf_tensor("S_bf", [128, 8, S, 16], BF16))
    x_meanT = _ph.enter_context(nc.sbuf_tensor("x_meanT", [S, T], F32))
    onetile = _ph.enter_context(nc.sbuf_tensor("onetile", [128, 128], F32))
    rep_v = _ph.enter_context(nc.sbuf_tensor("rep_v", [128, 128], F32))
    rep_g = _ph.enter_context(nc.sbuf_tensor("rep_g", [128, 128], F32))
    sall_v = _ph.enter_context(nc.sbuf_tensor("sall_v", [128, 128], F32))
    sall_g = _ph.enter_context(nc.sbuf_tensor("sall_g", [128, 128], F32))
    kpadf = _ph.enter_context(nc.sbuf_tensor("kpadf", [1, 2, 255], F32))
    kpadb = _ph.enter_context(nc.sbuf_tensor("kpadb", [1, 2, 255], BF16))
    k2sb = _ph.enter_context(nc.sbuf_tensor("k2sb", [128, 2, 255], BF16))
    hsb = _ph.enter_context(nc.sbuf_tensor("hsb", [128, 2, 128], BF16))
    msb = _ph.enter_context(nc.sbuf_tensor("msb", [128, 2, 128], BF16))
    esb = _ph.enter_context(nc.sbuf_tensor("esb", [128, 2, 16], BF16))
    t2b = _ph.enter_context(nc.sbuf_tensor("t2b", [16, 16, 128], F32))
    pA = _ph.enter_context(nc.psum_tensor("pA", [128, 512], F32))
    pB = _ph.enter_context(nc.psum_tensor("pB", [128, 512], F32))
    pF = _ph.enter_context(nc.psum_tensor("pF", [128, 512], F32))
    pC = _ph.enter_context(nc.psum_tensor("pC", [128, 512], F32))
    pD = _ph.enter_context(nc.psum_tensor("pD", [128, 512], F32))
    pE = _ph.enter_context(nc.psum_tensor("pE", [128, 2, 1024], BF16))
    if True:
        p = Prog()
        gp = sm["pa"]
        p.v(lambda e: e.memset(onetile[:], 1.0))
        import dataclasses as _dc
        for q in range(8):
            ssl = q % 2
            la = logafat[:, q * 16:(q + 1) * 16]
            in0 = _dc.replace(la, ap=[la.ap[0], [1, 16], [0, 128]])
            i1 = iota_p1[:, :]
            in1p = _dc.replace(i1, ap=[i1.ap[0], [0, 16], [1, 128]])
            outv = scrbig[:, ssl, :].rearrange("p (a c) -> p a c", a=16)
            wt_q = [(ev2, ev2.n - 1)] if q >= 2 else []
            p.v(lambda e, in0=in0, in1p=in1p, outv=outv: e.tensor_tensor(outv, in0, in1p, OP.mult),
                waits=wt_q, incs=[(vv2, 1)])
            p.s(lambda e, q=q, ssl=ssl: e.activation(
                    apowc[:, q * 16:(q + 1) * 16, :].rearrange("p a c -> p (a c)"),
                    scrbig[:, ssl, :], AF.Exp),
                waits=[(vv2, vv2.n)], incs=[(ev2, 1)])
        ev2_apow = ev2.n
        pLOC = (pB, pF)
        vv2_tr = {}
        vv2_loc = {}
        mm2_tr = {}
        ev2_gsc = {}
        for q in range(8):
            ssl = q % 2
            la = logafat[:, q * 16:(q + 1) * 16]
            in0 = _dc.replace(la, ap=[la.ap[0], [1, 16], [0, 128]])
            i2 = iota_r2[:, :]
            in1r = _dc.replace(i2, ap=[i2.ap[0], [0, 16], [1, 128]])
            outv = scrbig[:, ssl, :].rearrange("p (a c) -> p a c", a=16)
            wt_q = [(ev2, ev2_gsc[q - 2])] if q >= 2 else []
            p.v(lambda e, in0=in0, in1r=in1r, outv=outv: e.tensor_tensor(outv, in0, in1r, OP.mult),
                waits=wt_q, incs=[(vv2, 1)])
            wt_e = [(vv2, vv2.n)]
            if q >= 2:
                wt_e.append((mm2, mm2_tr[q - 2]))
            p.s(lambda e, ssl=ssl: e.activation(
                    gsc_c[:, ssl].rearrange("p a c -> p (a c)"),
                    scrbig[:, ssl, :], AF.Exp),
                waits=wt_e, incs=[(ev2, 1)])
            ev2_gsc[q] = ev2.n
            for cc in range(16):
                col = q * 16 + cc
                tsl = col % 2
                asl = col % 4
                wt_t = [(ev2, ev2_gsc[q])]
                if col >= 2:
                    wt_t.append((vv2, vv2_tr[col - 2]))
                p.t(lambda e, ssl=ssl, cc=cc, tsl=tsl: e.transpose(
                        pE[:, tsl, 0:128], gsc_c[:, ssl, cc, :], identb[:]),
                    waits=wt_t, incs=[(mm2, 1)])
                mm2_t = mm2.n
                wt_a = [(mm2, mm2_t)]
                if col >= 4:
                    wt_a.append((mm2, mm2_locmm[col - 4]))
                p.v(lambda e, asl=asl, tsl=tsl: e.tensor_copy(arev_c[:, asl, :],
                                                              pE[:, tsl, 0:128]),
                    waits=wt_a, incs=[(vv2, 1)])
                vv2_tr[col] = vv2.n
                dk, s_ = col // 16, col % 16
                lsl = col % 2
                wt_l = [(vv2, vv2_tr[col])]
                if col >= 2:
                    wt_l.append((vv2, vv2_loc[col - 2]))
                p.t(lambda e, asl=asl, lsl=lsl, s_=s_: e.matmul(
                        pLOC[lsl][:, 0:128], arev_c[:, asl, :], B_cm[:, s_, :],
                        start=True, stop=True),
                    waits=wt_l, incs=[(mm2, 1)])
                if col == 0:
                    mm2_locmm = {}
                mm2_locmm[col] = mm2.n
                mm2_l = mm2.n
                p.v(lambda e, dk=dk, s_=s_, lsl=lsl: e.tensor_copy(loc_all[:, dk, s_, :],
                                                                   pLOC[lsl][:, 0:128]),
                    waits=[(mm2, mm2_l)], incs=[(vv2, 1)])
                vv2_loc[col] = vv2.n
            mm2_tr[q] = mm2.n
        vv2_locend = vv2.n
        for col in range(128):
            dk, s_ = col // 16, col % 16
            acol = dk * 16 + s_
            if True:
                em, sem_e = p.v, vv
                repb, sallb = (rep_v, sall_v) if col % 2 == 0 else (rep_g, sall_g)
            wt0 = [(vv2, vv2_loc[col])]
            aap = a128fat[:, acol:acol + 1]
            abc = dataclasses.replace(aap, ap=[aap.ap[0], [0, 128]])
            em(lambda e, repb=repb, abc=abc: e.tensor_tensor(
                    repb[:], onetile[:], abc, OP.mult),
               waits=wt0 + [(sem_e, sem_e.n)], incs=[(sem_e, 1)])
            em(lambda e, repb=repb, sallb=sallb, dk=dk, s_=s_: e.tensor_tensor_scan(
                    sallb[:], repb[:], loc_all[:, dk, s_, :], 0.0, OP.mult, OP.add),
               waits=[(sem_e, sem_e.n)], incs=[(sem_e, 1)])
            em(lambda e, sallb=sallb, dk=dk, s_=s_: e.tensor_copy(
                    S_own[:, dk, s_, :], sallb[:, 111:127]),
               waits=[(sem_e, sem_e.n)], incs=[(sem_e, 1)])
        vv_scans = vv.n; gp_scans = gp.n
        p.v(lambda e: e.tensor_scalar(S_bf[:], S_own[:], 1.0 / D, None, OP.mult),
            waits=[(vv, vv_scans), (gp, gp_scans), (vv, vv.n)], incs=[(vv, 1)])
        vv_sbf = vv.n
        st_kp = {}
        for s_ in range(S):
            def mK(e, s_=s_):
                for dk in range(8):
                    ins = e.matmul(pC[0:1, 0:128], ones_c_b[:], apowc[:, dk * 16 + s_, :],
                                   start=(dk == 0), stop=(dk == 7))
                return ins
            wt_mk = [(ev2, ev2_apow)]
            if s_ >= 1:
                wt_mk.append((vv2, vv2_kprev))
            p.t(mK, waits=wt_mk, incs=[(mm2, 1)])
            mm2_k = mm2.n
            ksl = s_ % 2
            wt_kp = [(mm2, mm2_k)]
            if s_ >= 2:
                wt_kp.append((st, st_kp[s_ - 2]))
            p.v(lambda e, ksl=ksl: e.memset(kpadf[:, ksl, :], 0.0), waits=wt_kp,
                incs=[(vv2, 1)])
            p.v(lambda e, ksl=ksl: e.memset(kpadf[0:1, ksl, 127:128], 1.0),
                waits=[(vv2, vv2.n)], incs=[(vv2, 1)])
            p.v(lambda e, ksl=ksl: e.tensor_scalar(kpadf[0:1, ksl, 128:255], pC[0:1, 0:127],
                                                   1.0 / D, None, OP.mult),
                waits=[(vv2, vv2.n)], incs=[(vv2, 1)])
            p.v(lambda e, ksl=ksl: e.tensor_copy(kpadb[:, ksl, :], kpadf[:, ksl, :]),
                waits=[(vv2, vv2.n)], incs=[(vv2, 1)])
            vv2_k = vv2.n
            p.t(lambda e, ksl=ksl: e.matmul(pD[:, 0:255], ones_r_b[:], kpadb[0:1, ksl, :],
                                            start=True, stop=True),
                waits=[(vv2, vv2_k)], incs=[(mm2, 1)])
            mm2_k2 = mm2.n
            p.v(lambda e, ksl=ksl: e.tensor_copy(k2sb[:, ksl, :], pD[:, 0:255]),
                waits=[(mm2, mm2_k2)], incs=[(vv2, 1)])
            vv2_k2 = vv2.n
            p.sp(lambda e, s_=s_, ksl=ksl: e.dma_start(
                    out=kpad2_d.ap()[s_], in_=k2sb[:, ksl, :]),
                 waits=[(vv2, vv2_k2)], incs=[(st, 16)])
            st_kp[s_] = st.n
            vv2_kprev = vv2.n
        st_kpad = st.n
        for s_ in range(S):
            sl = s_ % 2
            wt_h = [(st, st_kpad)]
            if s_ >= 2:
                wt_h.append((mm2, mm2.n))
            p.sp(lambda e, s_=s_, sl=sl: e.dma_start(
                    out=hsb[:, sl, :],
                    in_=dram_ap(kpad2_d.ap(), s_ * 128 * 255, [[256, 128], [1, 128]])),
                 waits=wt_h, incs=[(ds, 16)])
            ds_h = ds.n
            wt_j = [(ds, ds_h)]
            if s_ >= 1:
                wt_j.append((vv2, vv2.n))
            p.t(lambda e, sl=sl: e.matmul(pA[:, 0:128], jrevb[:], hsb[:, sl, :],
                                          start=True, stop=True),
                waits=wt_j, incs=[(mm2, 1)])
            mm2_j = mm2.n
            p.v(lambda e, sl=sl: e.tensor_copy(msb[:, sl, :], pA[:, 0:128]),
                waits=[(mm2, mm2_j)], incs=[(vv2, 1)])
            vv2_m = vv2.n
            def mconv(e, s_=s_, sl=sl):
                e.matmul(pB[:, 256:272], msb[:, sl, :], B_cm[:, s_, 112:128],
                         start=True, stop=False)
                for dk in range(8):
                    ins = e.matmul(pB[:, 256:272], apowc[:, dk * 16 + s_, :], S_bf[:, dk, s_, :],
                                   start=False, stop=(dk == 7))
                return ins
            p.t(mconv, waits=[(vv2, vv2_m), (vv, vv_sbf)], incs=[(mm2, 1)])
            mm2_c = mm2.n
            p.v(lambda e, sl=sl: e.tensor_copy(esb[:, sl, :], pB[:, 256:272]),
                waits=[(mm2, mm2_c)], incs=[(vv2, 1)])
            vv2_e = vv2.n
            p.t(lambda e, sl=sl: e.transpose(pE[:, 0, 0:16].rearrange("p a -> p a") if False else pE[0:16, 1, 0:128], esb[:, sl, :], identb[:]),
                waits=[(vv2, vv2_e)], incs=[(mm2, 1)])
            mm2_t = mm2.n
            p.v(lambda e, s_=s_: e.tensor_copy(t2b[:, s_, :], pE[0:16, 1, 0:128]),
                waits=[(mm2, mm2_t)], incs=[(vv2, 1)])
            vv2_t = vv2.n
            p.sp(lambda e, s_=s_: e.dma_start(
                    out=dram_ap(xm_d.ap(), s_ * T, [[128, 16], [1, 128]]),
                    in_=t2b[:, s_, :]),
                 waits=[(vv2, vv2_t)], incs=[(st, 16)])
        st_xm = st.n
        p.sp(lambda e: e.dma_start(out=x_meanT[:], in_=xm_d.ap()),
             waits=[(st, st_xm)], incs=[(ds, 16)])
        ds_xm = ds.n
        p.v(lambda e: e.tensor_copy(x_meanTb[:], x_meanT[:]), waits=[(ds, ds_xm)], incs=[(vv, 1)])
        if debug:
            p.sp(lambda e: e.dma_start(out=dbg["dbg_xm"], in_=x_meanT[:]),
                 waits=[(ds, ds_xm)], incs=[(ds, 16)])
            p.sp(lambda e: e.dma_start(out=dbg["dbg_sown"], in_=S_own[:]), incs=[(ds, 16)])
        p.sp(lambda e: e.nop(), waits=[(ds, ds.n)])
        p.run(nc)
        _ph.close()

    if max_phase < 3:
        return nc, din, outT_d, dbg

    xnT = sb("xnT", (128, 8, T), BF16)
    outT = sb("outT_sb", (128, 8, T), F32)
    yT_guard = nc.sbuf_tensor("yT", [128, 8, T], BF16)
    yT = yT_guard.__enter__()

    _ph = ExitStack()
    xf32 = _ph.enter_context(nc.sbuf_tensor("xf32", [128, 2, 8, 256], F32))
    rsq_sb = _ph.enter_context(nc.sbuf_tensor("rsq_sb", [1, 2, 3, 256], F32))
    rs2bc = _ph.enter_context(nc.sbuf_tensor("rs2bc", [128, T], BF16))
    ccol = _ph.enter_context(nc.sbuf_tensor("ccol", [128, 4], F32))
    zt1 = _ph.enter_context(nc.sbuf_tensor("zt1", [128, 2, 512], BF16))
    lgt = _ph.enter_context(nc.sbuf_tensor("lgt", [128, 8], F32))
    lgt2 = _ph.enter_context(nc.sbuf_tensor("lgt2", [128, 8], F32))
    m1c = _ph.enter_context(nc.sbuf_tensor("m1c", [128, 4], F32))
    combt = _ph.enter_context(nc.sbuf_tensor("combt", [128, 8], BF16))
    xnscr = _ph.enter_context(nc.sbuf_tensor("xnscr", [128, 2, 256], F32))
    wpb = _ph.enter_context(nc.sbuf_tensor("wpb", [128, 2, 8, 128], BF16))
    wzb = _ph.enter_context(nc.sbuf_tensor("wzb", [128, 2, 8, 128], BF16))
    zscr = _ph.enter_context(nc.sbuf_tensor("zscr", [128, 2, 512], BF16))
    yscr = _ph.enter_context(nc.sbuf_tensor("yscr", [128, 2, 512], F32))
    qA = _ph.enter_context(nc.psum_tensor("qA", [128, 512], F32))
    qB = _ph.enter_context(nc.psum_tensor("qB", [128, 512], F32))
    qC = _ph.enter_context(nc.psum_tensor("qC", [128, 512], F32))
    qD = _ph.enter_context(nc.psum_tensor("qD", [128, 512], F32))
    qE = _ph.enter_context(nc.psum_tensor("qE", [128, 512], F32))
    qF = _ph.enter_context(nc.psum_tensor("qF", [128, 512], F32))
    qG = _ph.enter_context(nc.psum_tensor("qG", [128, 256], BF16))
    if True:
        p = Prog()
        vv_tc = {}; mm_tc = {}; vv_rs2bc = {}
        for tc in range(8):
            sl = tc % 2
            wt = []
            if tc >= 2:
                wt = [(vv, vv_tc[tc - 2]), (mm, mm_tc[tc - 2])]
            src = din["xt"][:, L - T + tc * 256: L - T + (tc + 1) * 256].rearrange(
                "(dk p) t -> p dk t", p=128)
            p.sp(lambda e, d=xf32[:, sl], s=src: e.dma_start(out=d, in_=s),
                 waits=wt, incs=[(ds, 16)])
            for r_ in range(3):
                p.sp(lambda e, sl=sl, r_=r_, tc=tc: e.dma_start(
                        out=rsq_sb[:, sl, r_, :],
                        in_=rsq_d.ap()[r_:r_ + 1, tc * 256:(tc + 1) * 256]),
                     incs=[(ds, 16)])
            ds_x = ds.n
            p.t(lambda e, sl=sl: e.matmul(qA[:, 0:256], ones_r_f[:], rsq_sb[0:1, sl, 0, :],
                                          start=True, stop=True),
                waits=[(ds, ds_x), (vv, vv_tc[tc - 1] if tc >= 1 else 0)])
            p.t(lambda e, sl=sl: e.matmul(qA[:, 256:512], ones_r_f[:], rsq_sb[0:1, sl, 1, :],
                                          start=True, stop=True), incs=[(mm, 1)])
            mm_b = mm.n
            p.t(lambda e, sl=sl: e.matmul(qC[:, 0:256], ones_r_f[:], rsq_sb[0:1, sl, 2, :],
                                          start=True, stop=True),
                waits=[(vv, vv_rs2bc[tc - 1]) if tc >= 1 else (vv, 0)], incs=[(mm, 1)])
            mm_r2 = mm.n
            p.v(lambda e, tc=tc: e.tensor_copy(rs2bc[:, tc * 256:(tc + 1) * 256], qC[:, 0:256]),
                waits=[(mm, mm_r2)], incs=[(vv, 1)])
            vv_rs2bc[tc] = vv.n
            for dk in range(8):
                p.v(lambda e, sl=sl, dk=dk: e.tensor_mul(xnscr[:, sl, :], xf32[:, sl, dk, :],
                                                         qA[:, 0:256]),
                    waits=[(mm, mm_b)] if dk == 0 else [])
                p.v(lambda e, sl=sl, dk=dk, tc=tc: e.tensor_sub(
                        xnT[:, dk, tc * 256:(tc + 1) * 256], xnscr[:, sl, :], qA[:, 256:512]))
                p.v(lambda e, sl=sl, dk=dk, tc=tc: e.tensor_add(
                        outT[:, dk, tc * 256:(tc + 1) * 256], xf32[:, sl, dk, :],
                        xnT[:, dk, tc * 256:(tc + 1) * 256]),
                    incs=[(vv, 1)])
            for h in range(2):
                tt = tc * 2 + h
                jcol = JOWN0 + tt
                def mg(e, sl=sl, h=h):
                    for dk in range(8):
                        ins = e.matmul(qB[:, 0:8], xf32[:, sl, dk, h * 128:(h + 1) * 128],
                                       gate_f[:, dk, :], start=(dk == 0), stop=(dk == 7))
                    return ins
                p.t(mg, waits=[(vv, vv.n)], incs=[(mm, 1)])
                mm_gx = mm.n
                def vc(fn, extra=()):
                    p.v(fn, waits=[(vv, vv.n)] + list(extra), incs=[(vv, 1)])
                p.v(lambda e, jcol=jcol: e.tensor_scalar(lgt2[:], sg_bc[:],
                                                         xnq_cm[:, jcol:jcol + 1], None, OP.mult),
                    waits=[(mm, mm_gx)], incs=[(vv, 1)])
                vc(lambda e, jcol=jcol: e.scalar_tensor_tensor(
                        lgt[:], qB[:, 0:8], rs1_cm[:, jcol:jcol + 1], lgt2[:],
                        OP.mult, OP.subtract))
                vc(lambda e: e.reduce_max(m1c[:, 0:1], lgt[:], axis=AX.X))
                vc(lambda e: e.tensor_scalar(lgt2[:], lgt[:], m1c[:, 0:1], None, OP.is_ge))
                vc(lambda e: e.scalar_tensor_tensor(lgt2[:], lgt2[:], NEG_BIG, lgt[:],
                                                    OP.mult, OP.add))
                vc(lambda e: e.reduce_max(m1c[:, 1:2], lgt2[:], axis=AX.X))
                vc(lambda e: e.tensor_scalar(m1c[:, 2:3], m1c[:, 0:1], -1.0, None, OP.mult))
                vv_r = vv.n
                p.s(lambda e: e.activation(lgt2[:], lgt[:], AF.Exp, bias=m1c[:, 2:3]),
                    waits=[(vv, vv_r)], incs=[(ev, 1)])
                p.s(lambda e: e.activation(m1c[:, 3:4], m1c[:, 1:2], AF.Exp, bias=m1c[:, 2:3]),
                    waits=[(ev, ev.n)], incs=[(ev, 1)])
                ev_r = ev.n
                p.v(lambda e: e.tensor_scalar(m1c[:, 3:4], m1c[:, 3:4], 1.0, None, OP.add),
                    waits=[(ev, ev_r)], incs=[(vv, 1)])
                vc(lambda e: e.reciprocal(m1c[:, 3:4], m1c[:, 3:4]))
                vc(lambda e: e.tensor_scalar(lgt[:], lgt[:], m1c[:, 1:2], None, OP.is_ge))
                vc(lambda e: e.scalar_tensor_tensor(lgt2[:], lgt2[:], m1c[:, 3:4], lgt[:],
                                                    OP.mult, OP.mult))
                vc(lambda e: e.tensor_copy(combt[:], lgt2[:]))
                vv_cb = vv.n
                p.t(lambda e: e.transpose(qG[0:8, 0:128], combt[:], identb[:]),
                    waits=[(vv, vv_cb)], incs=[(mm, 1)])
                mm_tr = mm.n
                p.v(lambda e, tt=tt: e.tensor_copy(comb8[:, tt * 128:(tt + 1) * 128],
                                                   qG[0:8, 0:128]),
                    waits=[(mm, mm_tr)], incs=[(vv, 1)])
            vv_tc[tc] = vv.n
            mm_tc[tc] = mm.n
        vv_p3a = vv.n
        if debug:
            p.sp(lambda e: e.dma_start(out=dbg["dbg_xnt"], in_=xnT[:]),
                 waits=[(vv, vv_p3a)], incs=[(ds, 16)])
            p.sp(lambda e: e.dma_start(out=dbg["dbg_comb8"], in_=comb8[:]), incs=[(ds, 16)])
        pP3 = (qC, qD); pZ3 = (qE, qF); pY3 = (qA, qB)
        qH = qG
        vv_rs2 = vv_p3a
        mm_chi = {}; vv_y = {}
        for chi in range(8):
            wsl = chi % 2
            wt_w = []
            if chi >= 2:
                wt_w = [(mm, mm_chi[chi - 2])]
            srcp = din["w_in_t"][:, chi * 128:(chi + 1) * 128].rearrange(
                "(dk p) c -> p dk c", p=128)
            srcz = din["w_in_t"][:, D + chi * 128: D + (chi + 1) * 128].rearrange(
                "(dk p) c -> p dk c", p=128)
            p.g(lambda e, d=wpb[:, wsl], s=srcp: e.dma_start(out=d, in_=s),
                waits=wt_w, incs=[(dg, 16)])
            p.g(lambda e, d=wzb[:, wsl], s=srcz: e.dma_start(out=d, in_=s), incs=[(dg, 16)])
            dg_w = dg.n
            wt_cb2 = [(dg, dg_w)]
            if chi == 0:
                wt_cb2.append((vv, vv_rs2))
            if chi >= 2:
                wt_cb2.append((vv, vv_y[(chi - 2) * 4 + 3]))
            def mcb2(e, wsl=wsl):
                for dk in range(8):
                    e.matmul(qB[:, 0:1], wpb[:, wsl, dk, :], lnbcol[:, dk:dk + 1],
                             start=(dk == 0), stop=(dk == 7))
                for dk in range(8):
                    ins = e.matmul(qB[:, 1:2], wzb[:, wsl, dk, :], lnbcol[:, dk:dk + 1],
                                   start=(dk == 0), stop=(dk == 7))
                return ins
            p.t(mcb2, waits=wt_cb2, incs=[(mm, 1)])
            mm_cb2 = mm.n
            p.v(lambda e, chi=chi: e.tensor_scalar(ccol[:, 0:1], qB[:, 0:1],
                                                   binfat[:, chi:chi + 1], None, OP.add),
                waits=[(mm, mm_cb2)], incs=[(vv, 1)])
            p.v(lambda e, chi=chi: e.tensor_scalar(ccol[:, 1:2], qB[:, 1:2],
                                                   binfat[:, 8 + chi:9 + chi], None, OP.add),
                waits=[(vv, vv.n)], incs=[(vv, 1)])
            for dk in range(8):
                p.v(lambda e, wsl=wsl, dk=dk: e.tensor_scalar(
                        wpb[:, wsl, dk, :], wpb[:, wsl, dk, :], gfat[:, dk:dk + 1],
                        None, OP.mult))
                p.v(lambda e, wsl=wsl, dk=dk: e.tensor_scalar(
                        wzb[:, wsl, dk, :], wzb[:, wsl, dk, :], gfat[:, dk:dk + 1],
                        None, OP.mult))
            p.v(lambda e: e.nop(), incs=[(vv, 1)])
            vv_scaled = vv.n
            for tc in range(4):
                gi = chi * 4 + tc
                psl = gi % 2
                wt_t = [(vv, vv_scaled)]
                if gi >= 2:
                    wt_t.append((vv, vv_y[gi - 2]))
                def mxz(e, wsl=wsl, tc=tc, psl=psl, chi=chi):
                    for dk in range(8):
                        e.matmul(pP3[psl][:, 0:512], wpb[:, wsl, dk, :],
                                 xnT[:, dk, tc * 512:(tc + 1) * 512],
                                 start=(dk == 0), stop=(dk == 7))
                    for dk in range(8):
                        e.matmul(pZ3[psl][:, 0:512], wzb[:, wsl, dk, :],
                                 xnT[:, dk, tc * 512:(tc + 1) * 512],
                                 start=(dk == 0), stop=(dk == 7))
                    return e.matmul(pY3[psl][:, 0:512], wct_sb[:, chi * 128:(chi + 1) * 128],
                                    x_meanTb[:, tc * 512:(tc + 1) * 512],
                                    start=True, stop=True)
                p.t(mxz, waits=wt_t, incs=[(mm, 1)])
                mm_xz = mm.n
                p.v(lambda e, psl=psl, tc=tc: e.tensor_mul(
                        zt1[:, psl, :], pZ3[psl][:, 0:512], rs2bc[:, tc * 512:(tc + 1) * 512]),
                    waits=[(mm, mm_xz)], incs=[(vv, 1)])
                p.s(lambda e, psl=psl: e.activation(zscr[:, psl, :], zt1[:, psl, :], AF.Silu,
                                                    bias=ccol[:, 1:2]),
                    waits=[(vv, vv.n)], incs=[(ev, 1)])
                ev_z = ev.n
                p.v(lambda e, psl=psl, tc=tc: e.tensor_mul(
                        yscr[:, psl, :], pP3[psl][:, 0:512], rs2bc[:, tc * 512:(tc + 1) * 512]))
                p.v(lambda e, psl=psl: e.tensor_scalar(
                        yscr[:, psl, :], yscr[:, psl, :], ccol[:, 0:1], None, OP.add))
                p.v(lambda e, psl=psl, chi=chi: e.scalar_tensor_tensor(
                        yscr[:, psl, :], yscr[:, psl, :], dfat[:, chi:chi + 1],
                        pY3[psl][:, 0:512], OP.mult, OP.add))
                p.v(lambda e, psl=psl, chi=chi, tc=tc: e.tensor_mul(
                        yT[:, chi, tc * 512:(tc + 1) * 512], yscr[:, psl, :], zscr[:, psl, :]),
                    waits=[(ev, ev_z)], incs=[(vv, 1)])
                vv_y[gi] = vv.n
            mm_chi[chi] = mm.n
        if debug:
            p.sp(lambda e: e.dma_start(out=dbg["dbg_yt"], in_=yT[:]),
                 waits=[(vv, vv.n)], incs=[(ds, 16)])
        p.sp(lambda e: e.nop(), waits=[(ds, ds.n)])
        p.run(nc)
        _ph.close()

    if max_phase < 4:
        return nc, din, outT_d, dbg

    _ph = ExitStack()
    woutb = _ph.enter_context(nc.sbuf_tensor("woutb", [128, 8, D], BF16))
    rA = _ph.enter_context(nc.psum_tensor("rA", [128, 512], F32))
    rB = _ph.enter_context(nc.psum_tensor("rB", [128, 512], F32))
    if True:
        p = Prog()
        p.g(lambda e: e.dma_start(out=woutb[:],
                                  in_=din["w_out_t"].rearrange("(dik p) do -> p dik do", p=128)),
            incs=[(dg, 16)])
        dg_w = dg.n
        rP = (rA, rB)
        vv_o = {}
        for dot in range(8):
            for tc in range(4):
                gi = dot * 4 + tc
                psl = gi % 2
                wt_t = [(dg, dg_w)]
                if gi >= 2:
                    wt_t.append((vv, vv_o[gi - 2]))
                def mwo(e, dot=dot, tc=tc, psl=psl):
                    for dik in range(8):
                        ins = e.matmul(rP[psl][:, 0:512], woutb[:, dik, dot * 128:(dot + 1) * 128],
                                       yT[:, dik, tc * 512:(tc + 1) * 512],
                                       start=(dik == 0), stop=(dik == 7))
                    return ins
                p.t(mwo, waits=wt_t, incs=[(mm, 1)])
                mm_o = mm.n
                p.v(lambda e, dot=dot, tc=tc, psl=psl: e.scalar_tensor_tensor(
                        outT[:, dot, tc * 512:(tc + 1) * 512], rP[psl][:, 0:512],
                        boutfat[:, dot:dot + 1], outT[:, dot, tc * 512:(tc + 1) * 512],
                        OP.add, OP.add),
                    waits=[(mm, mm_o)], incs=[(vv, 1)])
                vv_o[gi] = vv.n
        p.run(nc)
        _ph.close()

    yT_guard.__exit__(None, None, None)

    if max_phase < 5:
        return nc, din, outT_d, dbg

    _ph = ExitStack()
    hT = _ph.enter_context(nc.sbuf_tensor("hT", [128, 8, T], BF16))
    wgb = _ph.enter_context(nc.sbuf_tensor("wgb", [128, 2, 8, 512], BF16))
    wub = _ph.enter_context(nc.sbuf_tensor("wub", [128, 2, 8, 512], BF16))
    wdb = _ph.enter_context(nc.sbuf_tensor("wdb", [128, 2, 8, 128], BF16))
    cbc = _ph.enter_context(nc.sbuf_tensor("cbc", [128, 4, 512], BF16))
    sgscr = _ph.enter_context(nc.sbuf_tensor("sgscr", [128, 2, 512], BF16))
    uscr = _ph.enter_context(nc.sbuf_tensor("uscr", [128, 2, 512], BF16))
    mg0 = _ph.enter_context(nc.psum_tensor("mg0", [128, 512], F32))
    mg1 = _ph.enter_context(nc.psum_tensor("mg1", [128, 512], F32))
    mu0 = _ph.enter_context(nc.psum_tensor("mu0", [128, 512], F32))
    mu1 = _ph.enter_context(nc.psum_tensor("mu1", [128, 512], F32))
    mo0 = _ph.enter_context(nc.psum_tensor("mo0", [128, 512], F32))
    mo1 = _ph.enter_context(nc.psum_tensor("mo1", [128, 512], F32))
    mcb = _ph.enter_context(nc.psum_tensor("mcb", [128, 512], F32))
    if True:
        p = Prog()
        pg = (mg0, mg1); pu = (mu0, mu1); po = (mo0, mo1)
        wgu, wgufr, wd, wdfr = sm["wgu"], sm["wgufr"], sm["wd"], sm["wdfr"]
        pa, pb, hb = sm["pa"], sm["pb"], sm["hb"]
        gi5 = 0
        gb5 = 0
        gh = 0
        gw = 0
        evA = {}; vvA = {}; vvB = {}
        for e_ in range(E):
            for tcb in range(4):
                wt_cb = [(pb, pb.n)] if (e_ > 0 or tcb > 0) else []
                p.t(lambda e, e_=e_, tcb=tcb: e.matmul(
                        mcb[:, 0:512], selb[:, e_ * 128:(e_ + 1) * 128],
                        comb8[:, tcb * 512:(tcb + 1) * 512], start=True, stop=True),
                    waits=wt_cb, incs=[(pa, 1)])
                p.v(lambda e, tcb=tcb: e.tensor_copy(cbc[:, tcb, :], mcb[:, 0:512]),
                    waits=[(pa, pa.n)], incs=[(pb, 1)])
            pb_e = pb.n
            for q in range(4):
                eq = e_ * 4 + q
                for h in range(2):
                    wt_dma = [(wgufr, gh - 1)] if gh >= 2 else []
                    srcg = din["wg_t"][e_][:, q * 1024 + h * 512: q * 1024 + (h + 1) * 512] \
                        .rearrange("(dk p) i -> p dk i", p=128)
                    srcu = din["wu_t"][e_][:, q * 1024 + h * 512: q * 1024 + (h + 1) * 512] \
                        .rearrange("(dk p) i -> p dk i", p=128)
                    p.g(lambda e, d=wgb[:, gh % 2], s=srcg: e.dma_start(out=d, in_=s),
                        waits=wt_dma, incs=[(wgu, 16)])
                    p.g(lambda e, d=wub[:, gh % 2], s=srcu: e.dma_start(out=d, in_=s),
                        incs=[(wgu, 16)])
                    wgu_h = wgu.n
                    for icw in range(4):
                        ic = h * 4 + icw
                        for tc in range(4):
                            psl = gi5 % 2
                            wt_t = [(wgu, wgu_h)]
                            if e_ == 0 and q == 0 and h == 0 and icw == 0 and tc == 0:
                                wt_t.append((pb, pb_e))
                            if gi5 >= 2:
                                wt_t.append((ev2, evA[gi5 - 2]))
                                wt_t.append((vv2, vvA[gi5 - 2]))
                            def mA(e, wsl=gh % 2, icw=icw, tc=tc, psl=psl):
                                for dk in range(8):
                                    e.matmul(pg[psl][:, 0:512],
                                             wgb[:, wsl, dk, icw * 128:(icw + 1) * 128],
                                             xnT[:, dk, tc * 512:(tc + 1) * 512],
                                             start=(dk == 0), stop=(dk == 7))
                                for dk in range(8):
                                    ins = e.matmul(pu[psl][:, 0:512],
                                                   wub[:, wsl, dk, icw * 128:(icw + 1) * 128],
                                                   xnT[:, dk, tc * 512:(tc + 1) * 512],
                                                   start=(dk == 0), stop=(dk == 7))
                                return ins
                            last_of_half = (icw == 3 and tc == 3)
                            incs_t = [(mm2, 1)] + ([(wgufr, 1)] if last_of_half else [])
                            p.t(mA, waits=wt_t, incs=incs_t)
                            mm2_a = mm2.n
                            wt_sg = [(mm2, mm2_a)]
                            if gi5 >= 2:
                                wt_sg.append((vv2, vvA[gi5 - 2]))
                            p.s(lambda e, psl=psl: e.activation(sgscr[:, psl, :],
                                                                pg[psl][:, 0:512], AF.Silu),
                                waits=wt_sg, incs=[(ev2, 1)])
                            evA[gi5] = ev2.n
                            wt_u = [(mm2, mm2_a)]
                            if eq >= 1:
                                wt_u.append((hb, eq))
                            p.v(lambda e, psl=psl, tc=tc: e.tensor_mul(
                                    uscr[:, psl, :], pu[psl][:, 0:512], cbc[:, tc, :]),
                                waits=wt_u)
                            p.v(lambda e, psl=psl, ic=ic, tc=tc: e.tensor_mul(
                                    hT[:, ic, tc * 512:(tc + 1) * 512],
                                    sgscr[:, psl, :], uscr[:, psl, :]),
                                waits=[(ev2, evA[gi5])], incs=[(vv2, 1)])
                            vvA[gi5] = vv2.n
                            gi5 += 1
                    gh += 1
                vv2_heq = vv2.n
                for dmt in range(8):
                    wt_dma = [(wdfr, gw - 1)] if gw >= 2 else []
                    srcd = din["wd_t"][e_][q * 1024:(q + 1) * 1024, dmt * 128:(dmt + 1) * 128] \
                        .rearrange("(ik p) d -> p ik d", p=128)
                    p.g(lambda e, d=wdb[:, gw % 2], s=srcd: e.dma_start(out=d, in_=s),
                        waits=wt_dma, incs=[(wd, 16)])
                    wd_c = wd.n
                    for tc in range(4):
                        psl = gb5 % 2
                        wt_t = [(wd, wd_c)]
                        if tc == 0 and dmt == 0:
                            wt_t.append((vv2, vv2_heq))
                        if gb5 >= 2:
                            wt_t.append((vv2, vvB[gb5 - 2]))
                        def mB(e, wsl=gw % 2, tc=tc, psl=psl):
                            for ik in range(8):
                                ins = e.matmul(po[psl][:, 0:512], wdb[:, wsl, ik, :],
                                               hT[:, ik, tc * 512:(tc + 1) * 512],
                                               start=(ik == 0), stop=(ik == 7))
                            return ins
                        last_of_chunk = (tc == 3)
                        last_of_eq = (dmt == 7 and tc == 3)
                        incs_t = [(mm2, 1)] + ([(wdfr, 1)] if last_of_chunk else []) \
                            + ([(hb, 1)] if last_of_eq else [])
                        p.t(mB, waits=wt_t, incs=incs_t)
                        mm2_b = mm2.n
                        p.v(lambda e, dmt=dmt, tc=tc, psl=psl: e.tensor_add(
                                outT[:, dmt, tc * 512:(tc + 1) * 512],
                                outT[:, dmt, tc * 512:(tc + 1) * 512], po[psl][:, 0:512]),
                            waits=[(mm2, mm2_b)], incs=[(vv2, 1)])
                        vvB[gb5] = vv2.n
                        gb5 += 1
                    gw += 1
        p.run(nc)
        _ph.close()

    p = Prog()
    p.sp(lambda e: e.dma_start(out=outT_d.rearrange("(dk p) t -> p dk t", p=128),
                               in_=outT[:]),
         incs=[(st, 16)])
    p.sp(lambda e: e.nop(), waits=[(st, st.n)])
    p.run(nc)
    return nc, din, outT_d, dbg


def _stage_inputs(inputs):
    f = np.float32
    x = np.asarray(inputs["L0_attn_x"], dtype=f)
    xT = np.ascontiguousarray(x[0].T)
    W_in = np.asarray(inputs["L0_attn_W_in"], f)
    W_out = np.asarray(inputs["L0_attn_W_out"], f)
    W_B = np.asarray(inputs["L0_attn_W_B"], f)
    W_C = np.asarray(inputs["L0_attn_W_C"], f)
    A_log = np.asarray(inputs["L0_attn_A_log"], f)
    D_skip = np.asarray(inputs["L0_attn_D_skip"], f)
    log_dt = np.asarray(inputs["L0_attn_log_dt"], f)
    b_in = np.asarray(inputs["L0_attn_b_in"], f)
    b_out = np.asarray(inputs["L0_attn_b_out"], f)
    ln_g = np.asarray(inputs["L0_attn_ln_g"], f)
    ln_b = np.asarray(inputs["L0_attn_ln_b"], f)
    gate_w = np.asarray(inputs["L0_ffn_gate_w"], f)
    wg = np.asarray(inputs["L0_ffn_wg"], f)
    wu = np.asarray(inputs["L0_ffn_wu"], f)
    wd = np.asarray(inputs["L0_ffn_wd"], f)

    shared = {
        "w_in_t": np.ascontiguousarray(W_in.T),
        "w_in_upn": np.ascontiguousarray(W_in[0:D, :]),
        "w_out_t": np.ascontiguousarray(W_out.T),
        "w_b_t": np.ascontiguousarray(W_B.T),
        "w_c_t": np.ascontiguousarray(W_C.T),
        "gate_t": np.ascontiguousarray(gate_w.T),
        "wg_t": np.ascontiguousarray(wg.transpose(0, 2, 1)),
        "wu_t": np.ascontiguousarray(wu.transpose(0, 2, 1)),
        "wd_t": np.ascontiguousarray(wd.transpose(0, 2, 1)),
        "g_fat": np.ascontiguousarray(ln_g.reshape(8, 128).T),
        "b_fat": np.ascontiguousarray(ln_b.reshape(8, 128).T),
        "d_fat": np.ascontiguousarray(D_skip.reshape(8, 128).T),
        "logdt_fat": np.ascontiguousarray(log_dt.reshape(8, 128).T),
        "bout_fat": np.ascontiguousarray(b_out.reshape(8, 128).T),
        "bin_fat": np.ascontiguousarray(b_in.reshape(16, 128).T),
        "binup_col": np.ascontiguousarray(b_in[0:D].reshape(8, 128).T),
        "lnb_col": np.ascontiguousarray(ln_b.reshape(8, 128).T),
        "alog_fat": np.ascontiguousarray(
            A_log.reshape(8, 128, S).transpose(1, 0, 2).reshape(128, 128)),
    }
    in_maps = []
    for c in range(NCORE):
        pad = (NCORE - 1 - c) * T
        xt_c = np.zeros((D, L), f)
        if pad:
            xt_c[:, pad:] = xT[:, :L - pad]
        else:
            xt_c[:] = xT
        mask = np.zeros((128, J), f)
        mask[:, JOWN0 - 16 * c:] = 1.0
        in_maps.append({**shared, "xt": xt_c, "mask_cm": mask})
    return in_maps


def kernel(**inputs):
    in_maps = _stage_inputs(inputs)
    nc, din, outT_d, dbg = build(debug=False)
    res = run_bass_kernel_spmd(nc, in_maps, core_ids=list(range(NCORE)))
    outs = [np.asarray(res.results[c]["outT"]) for c in range(NCORE)]
    full = np.concatenate(outs, axis=1)
    return np.ascontiguousarray(full.T).reshape(1, L, D).astype(np.float32)


# version 03
# speedup vs baseline: 1.2078x; 1.0015x over previous
import dataclasses
from contextlib import ExitStack
import numpy as np
import ml_dtypes
import concourse.bass as bass
import concourse.mybir as mybir
from concourse.bass_utils import run_bass_kernel_spmd

F32 = mybir.dt.float32
BF16 = mybir.dt.bfloat16
AF = mybir.ActivationFunctionType
AX = mybir.AxisListType
OP = mybir.AluOpType

D = 1024
S = 16
L = 16384
E = 8
II = 4096
NCORE = 8
T = 2048
TC = 128
J = L // TC
JOWN0 = J - T // TC
EPS = 1e-5
LOG_LO = float(np.log(np.float64(1e-8)))
LOG_HI = float(np.log1p(np.float64(-1e-8)))
DTL, DTH = 1e-4, 1.0
NEG_BIG = -1.0e30


class Sem:
    def __init__(self, nc, name):
        self.h = nc.alloc_semaphore(name)
        self.n = 0


class Prog:
    ENGINES = ("tensor", "vector", "scalar", "gpsimd", "sync")

    def __init__(self):
        self.items = {e: [] for e in self.ENGINES}

    def emit(self, eng, fn, waits=(), incs=()):
        waits = [(s.h, n) for (s, n) in waits if n > 0]
        snap = []
        for (s, k) in incs:
            s.n += k
            snap.append((s.h, k))

        def closure(engine, _w=waits, _f=fn, _i=snap):
            for (h, n) in _w:
                engine.wait_ge(h, n)
            ins = _f(engine)
            if _i:
                ins.then_inc(*_i[0])
                for (h, k) in _i[1:]:
                    engine.nop().then_inc(h, k)
        self.items[eng].append(closure)

    def t(self, fn, **kw): self.emit("tensor", fn, **kw)
    def v(self, fn, **kw): self.emit("vector", fn, **kw)
    def s(self, fn, **kw): self.emit("scalar", fn, **kw)
    def g(self, fn, **kw): self.emit("gpsimd", fn, **kw)
    def sp(self, fn, **kw): self.emit("sync", fn, **kw)

    def run(self, nc):
        with nc.Block() as block:
            for e in self.ENGINES:
                lst = self.items[e]
                if not lst:
                    continue
                getattr(block, e)(lambda eng, lst=lst: [f(eng) for f in lst])


def dram_ap(ap0, offset, pattern):
    return dataclasses.replace(ap0, offset=offset, ap=pattern)


def build(debug=False, max_phase=9):
    nc = bass.Bass("TRN2", target_bir_lowering=False, detect_race_conditions=False)

    din = {}
    def di(name, shape, dt=F32):
        din[name] = nc.dram_tensor(name, list(shape), dt, kind="ExternalInput").ap()
    di("xt", (D, L))
    di("mask_cm", (128, J))
    di("w_in_t", (D, 2 * D))
    di("w_in_upn", (D, D))
    di("w_out_t", (D, D))
    di("w_b_t", (D, S))
    di("w_c_t", (S, D))
    di("gate_t", (D, E))
    di("wg_t", (E, D, II))
    di("wu_t", (E, D, II))
    di("wd_t", (E, II, D))
    di("g_fat", (128, 8)); di("b_fat", (128, 8)); di("d_fat", (128, 8))
    di("logdt_fat", (128, 8)); di("bout_fat", (128, 8)); di("bin_fat", (128, 16))
    di("binup_col", (128, 8)); di("lnb_col", (128, 8))
    di("alog_fat", (128, 128))

    outT_d = nc.dram_tensor("outT", [D, T], F32, kind="ExternalOutput").ap()
    dbg = {}
    def do_(name, shape, dt=F32):
        if debug:
            dbg[name] = nc.dram_tensor(name, list(shape), dt, kind="ExternalOutput").ap()
    do_("dbg_bcm", (128, S, J), BF16)
    do_("dbg_mu", (128, J)); do_("dbg_rs1", (128, J)); do_("dbg_q2", (128, J))
    do_("dbg_xm", (S, T))
    do_("dbg_xnt", (128, 8, T), BF16)
    do_("dbg_comb8", (E, T), BF16)
    do_("dbg_yt", (128, 8, T), BF16)
    do_("dbg_logafat", (128, 128))
    do_("dbg_statw", (128, 8, 18), BF16)
    do_("dbg_rows", (1, 64))
    do_("dbg_cbm", (128, S)); do_("dbg_negsw", (128, S))
    do_("dbg_pcm", (128, S, 128))
    do_("dbg_sown", (128, 8, S, 16))

    ident_bf_d = nc.inline_tensor(np.eye(128, dtype=ml_dtypes.bfloat16), name="c_ident_bf").ap()
    ident_f32_d = nc.inline_tensor(np.eye(128, dtype=np.float32), name="c_ident_f32").ap()
    jrev_np = np.zeros((128, 128), dtype=ml_dtypes.bfloat16)
    for c in range(128):
        jrev_np[127 - c, c] = 1.0
    jrev_d = nc.inline_tensor(jrev_np, name="c_jrev").ap()
    ones_r_b_d = nc.inline_tensor(np.ones((1, 128), dtype=ml_dtypes.bfloat16), name="c_ones_r_b").ap()
    ones_r_f_d = nc.inline_tensor(np.ones((1, 128), dtype=np.float32), name="c_ones_r_f").ap()
    ones_c_b_d = nc.inline_tensor(np.ones((128, 1), dtype=ml_dtypes.bfloat16), name="c_ones_c_b").ap()
    ones_c_f_d = nc.inline_tensor(np.ones((128, 1), dtype=np.float32), name="c_ones_c_f").ap()
    iota_p1_d = nc.inline_tensor(np.broadcast_to(np.arange(1, 129, dtype=np.float32),
                                                 (128, 128)).copy(), name="c_iota_p1").ap()
    iota_r2_d = nc.inline_tensor(np.broadcast_to(127.0 - np.arange(128, dtype=np.float32),
                                                 (128, 128)).copy(), name="c_iota_r2").ap()
    sel_np = np.zeros((E, E * 128), dtype=ml_dtypes.bfloat16)
    for e in range(E):
        sel_np[e, e * 128:(e + 1) * 128] = 1.0
    sel_d = nc.inline_tensor(sel_np, name="c_sel").ap()

    kpad2_d = nc.dram_tensor("kpad2_d", [S, 128, 255], BF16, kind="Internal")
    xm_d = nc.dram_tensor("xm_d", [S, T], F32, kind="Internal")
    rsq_d = nc.dram_tensor("rsq_d", [3, T], F32, kind="Internal")

    names = ("dg", "ds", "mm", "ev", "vv", "mm2", "vv2", "ev2", "st",
             "wgu", "wgufr", "wd", "wdfr", "pa", "pb", "hb")
    sm = {k: Sem(nc, k) for k in names}
    dg, ds, mm, ev, vv = sm["dg"], sm["ds"], sm["mm"], sm["ev"], sm["vv"]
    mm2, vv2, ev2, st = sm["mm2"], sm["vv2"], sm["ev2"], sm["st"]

    ctx = nc.ctx
    def sb(name, shape, dt=F32):
        return ctx.enter_context(nc.sbuf_tensor(name, list(shape), dt))

    statW = sb("statW", (128, 8, 18), BF16)
    wct_sb = sb("wct_sb", (S, D), BF16)
    gate_f = sb("gate_f", (128, 8, E), F32)
    gfat = sb("gfat", (128, 8)); bfat_ = sb("bfat", (128, 8))
    dfat = sb("dfat", (128, 8))
    boutfat = sb("boutfat", (128, 8)); binfat = sb("binfat", (128, 16))
    logafat = sb("logafat", (128, 128))
    a128fat = sb("a128fat", (128, 128))
    dtm_bc = sb("dtm_bc", (128, 1))
    cbm_bc = sb("cbm_bc", (128, S))
    negsw_bc = sb("negsw_bc", (128, S))
    sg_bc = sb("sg_bc", (128, E))
    identb = sb("identb", (128, 128), BF16)
    identf = sb("identf", (128, 128), F32)
    jrevb = sb("jrevb", (128, 128), BF16)
    ones_r_b = sb("ones_r_b", (1, 128), BF16)
    ones_r_f = sb("ones_r_f", (1, 128), F32)
    ones_c_b = sb("ones_c_b", (128, 1), BF16)
    ones_c_f = sb("ones_c_f", (128, 1), F32)
    iota_p1 = sb("iota_p1", (128, 128), F32)
    iota_r2 = sb("iota_r2", (128, 128), F32)
    selb = sb("selb", (E, E * 128), BF16)
    mask_sb = sb("mask_sb", (128, J), F32)
    B_cm = sb("B_cm", (128, S, J), BF16)
    x_meanTb = sb("x_meanTb", (S, T), BF16)
    comb8 = sb("comb8", (E, T), BF16)
    rs1_cm = sb("rs1_cm", (128, J))
    xnq_cm = sb("xnq_cm", (128, J))
    lnbcol = sb("lnbcol", (128, 8), BF16)

    _ph = ExitStack()
    winupn = _ph.enter_context(nc.sbuf_tensor("winupn", [128, 8, D], BF16))
    wbt = _ph.enter_context(nc.sbuf_tensor("wbt", [128, 8, S], BF16))
    binup = _ph.enter_context(nc.sbuf_tensor("binup", [128, 8], BF16))
    wbbraw = _ph.enter_context(nc.sbuf_tensor("wbbraw", [128, 8, S], BF16))
    p0rows = _ph.enter_context(nc.sbuf_tensor("p0rows", [1, 64], F32))
    alogfat = _ph.enter_context(nc.sbuf_tensor("alogfat", [128, 128], F32))
    ldtfat = _ph.enter_context(nc.sbuf_tensor("ldtfat", [128, 8], F32))
    dt_fat = _ph.enter_context(nc.sbuf_tensor("dt_fat", [128, 8], F32))
    dt8 = _ph.enter_context(nc.sbuf_tensor("dt8", [128, 1], F32))
    ps0 = _ph.enter_context(nc.psum_tensor("ps0", [128, 512], F32))
    ps1 = _ph.enter_context(nc.psum_tensor("ps1", [128, 512], F32))
    ps2 = _ph.enter_context(nc.psum_tensor("ps2", [128, 512], F32))
    if True:
        p = Prog()
        def ld_s(dst, src):
            p.sp(lambda e, d=dst, s=src: e.dma_start(out=d, in_=s), incs=[(ds, 16)])
        def ld_g(dst, src):
            p.g(lambda e, d=dst, s=src: e.dma_start(out=d, in_=s), incs=[(dg, 16)])
        ld_s(gfat[:], din["g_fat"]); ld_s(bfat_[:], din["b_fat"])
        ld_s(dfat[:], din["d_fat"]); ld_s(ldtfat[:], din["logdt_fat"])
        ld_s(boutfat[:], din["bout_fat"]); ld_s(binfat[:], din["bin_fat"])
        ld_s(alogfat[:], din["alog_fat"]); ld_s(mask_sb[:], din["mask_cm"])
        ld_s(identf[:], ident_f32_d); ld_s(ones_r_f[:], ones_r_f_d)
        ld_s(ones_c_f[:], ones_c_f_d)
        ld_s(iota_p1[:], iota_p1_d); ld_s(iota_r2[:], iota_r2_d)
        ld_s(gate_f[:], din["gate_t"].rearrange("(dk p) e -> p dk e", p=128))
        ld_s(identb[:], ident_bf_d); ld_s(jrevb[:], jrev_d)
        ld_s(ones_r_b[:], ones_r_b_d); ld_s(ones_c_b[:], ones_c_b_d)
        ld_s(selb[:], sel_d)
        n_lds = ds.n
        ld_g(wct_sb[:], din["w_c_t"])
        ld_g(winupn[:], din["w_in_upn"].rearrange("(dok p) di -> p dok di", p=128))
        ld_g(wbt[:], din["w_b_t"].rearrange("(dok p) s -> p dok s", p=128))
        ld_g(binup[:], din["binup_col"])
        ld_g(lnbcol[:], din["lnb_col"])
        n_ldg = dg.n

        p.v(lambda e: e.memset(p0rows[:], 0.0))
        p.v(lambda e: e.memset(statW[:, :, 0:1], 1.0), waits=[(ds, n_lds)])
        p.v(lambda e: e.memset(statW[:, :, 17:18], 0.0))
        p.s(lambda e: e.activation(dt_fat[:], ldtfat[:], AF.Exp),
            waits=[(ds, n_lds)], incs=[(ev, 1)])
        ev_dt = ev.n
        p.v(lambda e: e.tensor_scalar(dt_fat[:], dt_fat[:], DTL, DTH, OP.max, OP.min),
            waits=[(ev, ev_dt)], incs=[(vv, 1)])
        vv_dt = vv.n
        p.s(lambda e: e.activation(logafat[:], alogfat[:], AF.Exp), incs=[(ev, 1)])
        ev_la = ev.n
        for dhi in range(8):
            p.v(lambda e, k=dhi: e.tensor_scalar(logafat[:, k * 16:(k + 1) * 16],
                                                 logafat[:, k * 16:(k + 1) * 16],
                                                 dt_fat[:, k:k + 1], -1.0, OP.mult, OP.mult),
                waits=[(ev, ev_la), (vv, vv_dt)] if dhi == 0 else [])
        p.v(lambda e: e.nop(), incs=[(vv, 1)])
        p.v(lambda e: e.tensor_scalar(logafat[:], logafat[:], LOG_LO, LOG_HI, OP.max, OP.min),
            waits=[(vv, vv.n)], incs=[(vv, 1)])
        vv_lf = vv.n
        p.s(lambda e: e.activation(a128fat[:], logafat[:], AF.Exp, scale=128.0),
            waits=[(vv, vv_lf)], incs=[(ev, 1)])
        ev_a128 = ev.n
        p.v(lambda e: e.reduce_sum(dt8[:], dt_fat[:], axis=AX.X), incs=[(vv, 1)])
        vv_dt8 = vv.n
        p.t(lambda e: e.matmul(ps0[0:1, 0:1], ones_c_f[:], dt8[:], start=True, stop=True),
            waits=[(vv, vv_dt8)], incs=[(mm, 1)])
        mm_dtm = mm.n
        p.v(lambda e: e.tensor_scalar(p0rows[0:1, 0:1], ps0[0:1, 0:1], 1.0 / D, None, OP.mult),
            waits=[(mm, mm_dtm)], incs=[(vv, 1)])
        vv_dtm = vv.n
        p.t(lambda e: e.matmul(ps1[:, 0:1], ones_r_f[:], p0rows[0:1, 0:1], start=True, stop=True),
            waits=[(vv, vv_dtm)], incs=[(mm, 1)])
        mm_dbc = mm.n
        p.v(lambda e: e.tensor_copy(dtm_bc[:], ps1[:, 0:1]), waits=[(mm, mm_dbc)], incs=[(vv, 1)])

        vv_wbb = {}
        for dik in range(8):
            def mmf(e, dik=dik):
                for dok in range(8):
                    ins = e.matmul(ps2[:, 0:S], winupn[:, dok, dik * 128:(dik + 1) * 128],
                                   wbt[:, dok, :], start=(dok == 0), stop=(dok == 7))
                return ins
            wt = [(dg, n_ldg)] if dik == 0 else [(vv, vv_wbb[dik - 1])]
            p.t(mmf, waits=wt, incs=[(mm, 1)])
            mm_w = mm.n
            p.v(lambda e, dik=dik: e.tensor_copy(wbbraw[:, dik, :], ps2[:, 0:S]),
                waits=[(mm, mm_w)])
            p.v(lambda e, dik=dik: e.tensor_scalar(statW[:, dik, 1:17], ps2[:, 0:S],
                                                   gfat[:, dik:dik + 1], None, OP.mult),
                incs=[(vv, 1)])
            vv_wbb[dik] = vv.n
        def wbf(e):
            for dik in range(8):
                ins = e.matmul(ps0[0:1, 4:20], lnbcol[:, dik:dik + 1], wbbraw[:, dik, :],
                               start=(dik == 0), stop=(dik == 7))
            return ins
        p.t(wbf, waits=[(vv, vv_wbb[7])], incs=[(mm, 1)])
        def cbf(e):
            for dok in range(8):
                ins = e.matmul(ps1[0:1, 4:20], binup[:, dok:dok + 1], wbt[:, dok, :],
                               start=(dok == 0), stop=(dok == 7))
            return ins
        p.t(cbf, incs=[(mm, 1)])
        def swf(e):
            for dik in range(8):
                ins = e.matmul(ps2[0:1, 0:S], ones_c_b[:], statW[:, dik, 1:17],
                               start=(dik == 0), stop=(dik == 7))
            return ins
        p.t(swf, incs=[(mm, 1)])
        def sgf(e):
            for dk in range(8):
                ins = e.matmul(ps0[0:1, 24:32], ones_c_f[:], gate_f[:, dk, :],
                               start=(dk == 0), stop=(dk == 7))
            return ins
        p.t(sgf, incs=[(mm, 1)])
        mm_p0 = mm.n
        p.v(lambda e: e.tensor_copy(p0rows[0:1, 4:20], ps0[0:1, 4:20]), waits=[(mm, mm_p0)],
            incs=[(vv, 1)])
        p.v(lambda e: e.tensor_add(p0rows[0:1, 4:20], p0rows[0:1, 4:20], ps1[0:1, 4:20]),
            waits=[(vv, vv.n)], incs=[(vv, 1)])
        p.v(lambda e: e.tensor_scalar(p0rows[0:1, 20:36], ps2[0:1, 0:S], -1.0, None, OP.mult))
        p.v(lambda e: e.tensor_copy(p0rows[0:1, 40:48], ps0[0:1, 24:32]), waits=[(vv, vv.n)],
            incs=[(vv, 1)])
        vv_rows = vv.n
        p.t(lambda e: e.matmul(ps0[:, 32:48], ones_r_f[:], p0rows[0:1, 4:20], start=True, stop=True),
            waits=[(vv, vv_rows)])
        p.t(lambda e: e.matmul(ps1[:, 32:48], ones_r_f[:], p0rows[0:1, 20:36], start=True, stop=True))
        p.t(lambda e: e.matmul(ps2[:, 32:40], ones_r_f[:], p0rows[0:1, 40:48], start=True, stop=True),
            incs=[(mm, 1)])
        mm_bc = mm.n
        p.v(lambda e: e.tensor_scalar(cbm_bc[:], ps0[:, 32:48], dtm_bc[:, 0:1], None, OP.mult),
            waits=[(mm, mm_bc)])
        p.v(lambda e: e.tensor_copy(negsw_bc[:], ps1[:, 32:48]))
        p.v(lambda e: e.tensor_copy(sg_bc[:], ps2[:, 32:40]), incs=[(vv, 1)])
        if debug:
            vfin = vv.n
            p.sp(lambda e: e.dma_start(out=dbg["dbg_logafat"], in_=logafat[:]),
                 waits=[(vfin_s, vfin) for vfin_s in (vv,)], incs=[(ds, 16)])
            p.sp(lambda e: e.dma_start(out=dbg["dbg_statw"], in_=statW[:]), incs=[(ds, 16)])
            p.sp(lambda e: e.dma_start(out=dbg["dbg_cbm"], in_=cbm_bc[:]), incs=[(ds, 16)])
            p.sp(lambda e: e.dma_start(out=dbg["dbg_negsw"], in_=negsw_bc[:]), incs=[(ds, 16)])
            p.sp(lambda e: e.dma_start(out=dbg["dbg_rows"], in_=p0rows[:]), incs=[(ds, 16)])
            p.sp(lambda e: e.nop(), waits=[(ds, ds.n)])
        vv_p0 = vv.n
        _ph.close()

    if max_phase < 1:
        p.run(nc)
        return nc, din, outT_d, dbg

    _ph = ExitStack()
    xbf = _ph.enter_context(nc.sbuf_tensor("xbf", [128, 2, 8, 512], BF16))
    xsq = _ph.enter_context(nc.sbuf_tensor("xsq", [128, 2, 8, 512], BF16))
    P_cm = _ph.enter_context(nc.sbuf_tensor("P_cm", [128, S, 128], F32))
    sumx_cm = _ph.enter_context(nc.sbuf_tensor("sumx_cm", [128, 128], F32))
    sumsq_cm = _ph.enter_context(nc.sbuf_tensor("sumsq_cm", [128, 128], F32))
    scrA = _ph.enter_context(nc.sbuf_tensor("scrA", [128, 128], F32))
    scrB = _ph.enter_context(nc.sbuf_tensor("scrB", [128, 128], F32))
    mu_cm = _ph.enter_context(nc.sbuf_tensor("mu_cm", [128, J], F32))
    u_cm = _ph.enter_context(nc.sbuf_tensor("u_cm", [128, J], F32))
    q2_cm = _ph.enter_context(nc.sbuf_tensor("q2_cm", [128, J], F32))
    rs2_cm = _ph.enter_context(nc.sbuf_tensor("rs2_cm", [128, J], F32))
    t2f = _ph.enter_context(nc.sbuf_tensor("t2f", [16, 3, 128], F32))
    pE = _ph.enter_context(nc.psum_tensor("pE", [128, 2, 1024], BF16))
    _php1 = ExitStack()
    pP0 = _php1.enter_context(nc.psum_tensor("pP0", [128, 512], F32))
    pP1 = _php1.enter_context(nc.psum_tensor("pP1", [128, 512], F32))
    pQ0 = _php1.enter_context(nc.psum_tensor("pQ0", [128, 512], F32))
    pQ1 = _php1.enter_context(nc.psum_tensor("pQ1", [128, 512], F32))
    pT = _php1.enter_context(nc.psum_tensor("pT", [128, 512], F32))
    if True:
        pP = (pP0, pP1); pQ = (pQ0, pQ1)
        vv_grp = {}; ev_chunk = {}; mm_chunk = {}
        for tc in range(32):
            slot = tc % 2
            wt = [(vv, vv_p0)] if tc < 2 else []
            if tc >= 2:
                wt = [(mm, mm_chunk[tc - 2]), (ev, ev_chunk[tc - 2])]
            src = din["xt"][:, tc * 512:(tc + 1) * 512].rearrange("(dk p) t -> p dk t", p=128)
            p.g(lambda e, d=xbf[:, slot], s=src: e.dma_start(out=d, in_=s),
                waits=wt, incs=[(dg, 16)])
            dg_c = dg.n
            wt_s = [(dg, dg_c)]
            if tc >= 2:
                wt_s.append((mm, mm_chunk[tc - 2]))
            p.s(lambda e, sl=slot: e.activation(
                    xsq[:, sl].rearrange("p a b -> p (a b)"),
                    xbf[:, sl].rearrange("p a b -> p (a b)"), AF.Square),
                waits=wt_s, incs=[(ev, 1)])
            ev_chunk[tc] = ev.n
            for jj in range(4):
                gidx = tc * 4 + jj
                psl = gidx % 2
                wt_t = [(dg, dg_c), (ev, ev_chunk[tc])]
                if gidx < 2:
                    wt_t.append((vv, vv_p0))
                if gidx >= 2:
                    wt_t.append((vv, vv_grp[gidx - 2]))
                def mP(e, sl=slot, jj=jj, psl=psl):
                    for dk in range(8):
                        ins = e.matmul(pP[psl][:, 0:18], xbf[:, sl, dk, jj * 128:(jj + 1) * 128],
                                       statW[:, dk, :], start=(dk == 0), stop=(dk == 7))
                    return ins
                p.t(mP, waits=wt_t)
                def mQ(e, sl=slot, jj=jj, psl=psl):
                    for dk in range(8):
                        ins = e.matmul(pQ[psl][:, 0:1], xsq[:, sl, dk, jj * 128:(jj + 1) * 128],
                                       ones_c_b[:], start=(dk == 0), stop=(dk == 7))
                    return ins
                p.t(mQ, incs=[(mm, 1)])
                mm_g = mm.n
                jg = gidx
                p.v(lambda e, psl=psl, jg=jg: e.tensor_copy(P_cm[:, :, jg], pP[psl][:, 1:17]),
                    waits=[(mm, mm_g)])
                p.v(lambda e, psl=psl, jg=jg: e.tensor_copy(sumx_cm[:, jg:jg + 1], pP[psl][:, 0:1]))
                p.v(lambda e, psl=psl, jg=jg: e.tensor_copy(sumsq_cm[:, jg:jg + 1], pQ[psl][:, 0:1]),
                    incs=[(vv, 1)])
                vv_grp[gidx] = vv.n
            mm_chunk[tc] = mm.n
        p.v(lambda e: e.tensor_scalar(mu_cm[:], sumx_cm[:], 1.0 / D, None, OP.mult),
            waits=[(vv, vv.n)])
        p.v(lambda e: e.tensor_scalar(scrA[:], sumsq_cm[:], 1.0 / D, None, OP.mult),
            incs=[(vv, 1)])
        vv_a = vv.n
        p.s(lambda e: e.activation(scrB[:], mu_cm[:], AF.Square), waits=[(vv, vv_a)], incs=[(ev, 1)])
        ev_a = ev.n
        p.v(lambda e: e.tensor_sub(scrA[:], scrA[:], scrB[:]), waits=[(ev, ev_a)])
        p.v(lambda e: e.tensor_scalar(scrB[:], scrA[:], EPS, None, OP.add), incs=[(vv, 1)])
        vv_b = vv.n
        p.s(lambda e: e.activation(scrB[:], scrB[:], AF.Ln), waits=[(vv, vv_b)])
        p.s(lambda e: e.activation(rs1_cm[:], scrB[:], AF.Exp, scale=-0.5), incs=[(ev, 1)])
        ev_b = ev.n
        p.v(lambda e: e.tensor_scalar(scrB[:], scrA[:], EPS, None, OP.add), waits=[(ev, ev_b)])
        p.v(lambda e: e.reciprocal(scrB[:], scrB[:]))
        p.v(lambda e: e.tensor_mul(scrB[:], scrA[:], scrB[:]))
        p.v(lambda e: e.tensor_scalar(scrB[:], scrB[:], EPS, None, OP.add), incs=[(vv, 1)])
        vv_c2 = vv.n
        p.s(lambda e: e.activation(scrB[:], scrB[:], AF.Ln), waits=[(vv, vv_c2)])
        p.s(lambda e: e.activation(rs2_cm[:], scrB[:], AF.Exp, scale=-0.5), incs=[(ev, 1)])
        ev_c = ev.n
        p.v(lambda e: e.scalar_tensor_tensor(q2_cm[:], rs1_cm[:], dtm_bc[:, 0:1], rs2_cm[:],
                                             OP.mult, OP.mult), waits=[(ev, ev_c)])
        p.v(lambda e: e.tensor_mul(xnq_cm[:], mu_cm[:], rs1_cm[:]), incs=[(vv, 1)])
        vv_fin = vv.n
        for s_ in range(S):
            p.v(lambda e, s_=s_: e.scalar_tensor_tensor(scrA[:], mu_cm[:], negsw_bc[:, s_:s_ + 1],
                                                        P_cm[:, s_, :], OP.mult, OP.add),
                waits=[(vv, vv_fin)] if s_ == 0 else [])
            p.v(lambda e: e.tensor_mul(scrA[:], scrA[:], q2_cm[:]))
            p.v(lambda e, s_=s_: e.tensor_scalar(scrA[:], scrA[:], cbm_bc[:, s_:s_ + 1], None, OP.add))
            p.v(lambda e, s_=s_: e.tensor_mul(B_cm[:, s_, :], scrA[:], mask_sb[:]),
                incs=[(vv, 1)])
        vv_bcm = vv.n
        p.t(lambda e: e.transpose(pT[0:16, 0:128], rs1_cm[:, JOWN0:J], identf[:]),
            waits=[(vv, vv_fin)], incs=[(mm, 1)])
        mm_t1 = mm.n
        p.v(lambda e: e.tensor_copy(t2f[:, 0, :], pT[0:16, 0:128]), waits=[(mm, mm_t1)],
            incs=[(vv, 1)])
        vv_t1 = vv.n
        p.t(lambda e: e.transpose(pT[0:16, 128:256], xnq_cm[:, JOWN0:J], identf[:]),
            waits=[(vv, vv_t1)], incs=[(mm, 1)])
        mm_t2 = mm.n
        p.v(lambda e: e.tensor_copy(t2f[:, 1, :], pT[0:16, 128:256]), waits=[(mm, mm_t2)],
            incs=[(vv, 1)])
        vv_t2 = vv.n
        p.sp(lambda e: e.dma_start(out=dram_ap(rsq_d.ap(), 0, [[128, 16], [1, 128]]),
                                   in_=t2f[:, 0, :]),
             waits=[(vv, vv_t2)], incs=[(ds, 16)])
        p.sp(lambda e: e.dma_start(out=dram_ap(rsq_d.ap(), T, [[128, 16], [1, 128]]),
                                   in_=t2f[:, 1, :]), incs=[(ds, 16)])
        p.t(lambda e: e.transpose(pT[0:16, 256:384], rs2_cm[:, JOWN0:J], identf[:]),
            waits=[(vv, vv_t2)], incs=[(mm, 1)])
        mm_t3 = mm.n
        p.v(lambda e: e.tensor_copy(t2f[:, 2, :], pT[0:16, 256:384]), waits=[(mm, mm_t3)],
            incs=[(vv, 1)])
        vv_t3 = vv.n
        p.sp(lambda e: e.dma_start(out=dram_ap(rsq_d.ap(), 2 * T, [[128, 16], [1, 128]]),
                                   in_=t2f[:, 2, :]),
             waits=[(vv, vv_t3)], incs=[(ds, 16)])
        if debug:
            p.sp(lambda e: e.dma_start(out=dbg["dbg_bcm"], in_=B_cm[:]),
                 waits=[(vv, vv_bcm)], incs=[(ds, 16)])
            p.sp(lambda e: e.dma_start(out=dbg["dbg_mu"], in_=mu_cm[:]), incs=[(ds, 16)])
            p.sp(lambda e: e.dma_start(out=dbg["dbg_rs1"], in_=rs1_cm[:]), incs=[(ds, 16)])
            p.sp(lambda e: e.dma_start(out=dbg["dbg_q2"], in_=q2_cm[:]), incs=[(ds, 16)])
            p.sp(lambda e: e.dma_start(out=dbg["dbg_pcm"], in_=P_cm[:]), incs=[(ds, 16)])
        p.sp(lambda e: e.nop(), waits=[(ds, ds.n)])

    if max_phase < 2:
        p.run(nc)
        return nc, din, outT_d, dbg

    apowc = _ph.enter_context(nc.sbuf_tensor("apowc", [128, 128, 128], BF16))
    gsc_c = _ph.enter_context(nc.sbuf_tensor("gsc_c", [128, 2, 16, 128], BF16))
    arev_c = _ph.enter_context(nc.sbuf_tensor("arev_c", [128, 4, 128], BF16))
    loc_all = _ph.enter_context(nc.sbuf_tensor("loc_all", [128, 8, S, 128], BF16))
    scrbig = _ph.enter_context(nc.sbuf_tensor("scrbig", [128, 2, 2048], F32))
    S_own = _ph.enter_context(nc.sbuf_tensor("S_own", [128, 8, S, 16], F32))
    S_bf = _ph.enter_context(nc.sbuf_tensor("S_bf", [128, 8, S, 16], BF16))
    x_meanT = _ph.enter_context(nc.sbuf_tensor("x_meanT", [S, T], F32))
    onetile = _ph.enter_context(nc.sbuf_tensor("onetile", [128, 128], F32))
    rep_v = _ph.enter_context(nc.sbuf_tensor("rep_v", [128, 128], F32))
    rep_g = _ph.enter_context(nc.sbuf_tensor("rep_g", [128, 128], F32))
    sall_v = _ph.enter_context(nc.sbuf_tensor("sall_v", [128, 128], F32))
    sall_g = _ph.enter_context(nc.sbuf_tensor("sall_g", [128, 128], F32))
    kpadf = _ph.enter_context(nc.sbuf_tensor("kpadf", [1, 2, 255], F32))
    kpadb = _ph.enter_context(nc.sbuf_tensor("kpadb", [1, 2, 255], BF16))
    k2sb = _ph.enter_context(nc.sbuf_tensor("k2sb", [128, 2, 255], BF16))
    hsb = _ph.enter_context(nc.sbuf_tensor("hsb", [128, 2, 128], BF16))
    msb = _ph.enter_context(nc.sbuf_tensor("msb", [128, 2, 128], BF16))
    esb = _ph.enter_context(nc.sbuf_tensor("esb", [128, 2, 16], BF16))
    t2b = _ph.enter_context(nc.sbuf_tensor("t2b", [16, 16, 128], F32))
    _php1.close()
    pA = _ph.enter_context(nc.psum_tensor("pA", [128, 512], F32))
    pB = _ph.enter_context(nc.psum_tensor("pB", [128, 512], F32))
    pF = _ph.enter_context(nc.psum_tensor("pF", [128, 512], F32))
    pC = _ph.enter_context(nc.psum_tensor("pC", [128, 512], F32))
    pD = _ph.enter_context(nc.psum_tensor("pD", [128, 512], F32))
    if True:
        gp = sm["pa"]
        p.v(lambda e: e.memset(onetile[:], 1.0))
        import dataclasses as _dc
        for q in range(8):
            ssl = q % 2
            la = logafat[:, q * 16:(q + 1) * 16]
            in0 = _dc.replace(la, ap=[la.ap[0], [1, 16], [0, 128]])
            i1 = iota_p1[:, :]
            in1p = _dc.replace(i1, ap=[i1.ap[0], [0, 16], [1, 128]])
            outv = scrbig[:, ssl, :].rearrange("p (a c) -> p a c", a=16)
            wt_q = [(ev2, ev2.n - 1)] if q >= 2 else [(vv, vv_lf)]
            p.v(lambda e, in0=in0, in1p=in1p, outv=outv: e.tensor_tensor(outv, in0, in1p, OP.mult),
                waits=wt_q, incs=[(vv2, 1)])
            p.s(lambda e, q=q, ssl=ssl: e.activation(
                    apowc[:, q * 16:(q + 1) * 16, :].rearrange("p a c -> p (a c)"),
                    scrbig[:, ssl, :], AF.Exp),
                waits=[(vv2, vv2.n)], incs=[(ev2, 1)])
        ev2_apow = ev2.n
        pLOC = (pB, pF)
        vv2_tr = {}
        vv2_loc = {}
        mm2_tr = {}
        ev2_gsc = {}
        for q in range(8):
            ssl = q % 2
            la = logafat[:, q * 16:(q + 1) * 16]
            in0 = _dc.replace(la, ap=[la.ap[0], [1, 16], [0, 128]])
            i2 = iota_r2[:, :]
            in1r = _dc.replace(i2, ap=[i2.ap[0], [0, 16], [1, 128]])
            outv = scrbig[:, ssl, :].rearrange("p (a c) -> p a c", a=16)
            wt_q = [(ev2, ev2_gsc[q - 2])] if q >= 2 else []
            p.v(lambda e, in0=in0, in1r=in1r, outv=outv: e.tensor_tensor(outv, in0, in1r, OP.mult),
                waits=wt_q, incs=[(vv2, 1)])
            wt_e = [(vv2, vv2.n)]
            if q >= 2:
                wt_e.append((mm2, mm2_tr[q - 2]))
            p.s(lambda e, ssl=ssl: e.activation(
                    gsc_c[:, ssl].rearrange("p a c -> p (a c)"),
                    scrbig[:, ssl, :], AF.Exp),
                waits=wt_e, incs=[(ev2, 1)])
            ev2_gsc[q] = ev2.n
            for cc in range(16):
                col = q * 16 + cc
                tsl = col % 2
                asl = col % 4
                wt_t = [(ev2, ev2_gsc[q])]
                if col >= 2:
                    wt_t.append((vv2, vv2_tr[col - 2]))
                p.t(lambda e, ssl=ssl, cc=cc, tsl=tsl: e.transpose(
                        pE[:, tsl, 0:128], gsc_c[:, ssl, cc, :], identb[:]),
                    waits=wt_t, incs=[(mm2, 1)])
                mm2_t = mm2.n
                wt_a = [(mm2, mm2_t)]
                if col >= 4:
                    wt_a.append((mm2, mm2_locmm[col - 4]))
                p.v(lambda e, asl=asl, tsl=tsl: e.tensor_copy(arev_c[:, asl, :],
                                                              pE[:, tsl, 0:128]),
                    waits=wt_a, incs=[(vv2, 1)])
                vv2_tr[col] = vv2.n
                dk, s_ = col // 16, col % 16
                lsl = col % 2
                wt_l = [(vv2, vv2_tr[col])]
                if col == 0:
                    wt_l.append((vv, vv_bcm))
                if col >= 2:
                    wt_l.append((vv2, vv2_loc[col - 2]))
                p.t(lambda e, asl=asl, lsl=lsl, s_=s_: e.matmul(
                        pLOC[lsl][:, 0:128], arev_c[:, asl, :], B_cm[:, s_, :],
                        start=True, stop=True),
                    waits=wt_l, incs=[(mm2, 1)])
                if col == 0:
                    mm2_locmm = {}
                mm2_locmm[col] = mm2.n
                mm2_l = mm2.n
                p.v(lambda e, dk=dk, s_=s_, lsl=lsl: e.tensor_copy(loc_all[:, dk, s_, :],
                                                                   pLOC[lsl][:, 0:128]),
                    waits=[(mm2, mm2_l)], incs=[(vv2, 1)])
                vv2_loc[col] = vv2.n
            mm2_tr[q] = mm2.n
        vv2_locend = vv2.n
        for col in range(128):
            dk, s_ = col // 16, col % 16
            acol = dk * 16 + s_
            if True:
                em, sem_e = p.v, vv
                repb, sallb = (rep_v, sall_v) if col % 2 == 0 else (rep_g, sall_g)
            wt0 = [(vv2, vv2_loc[col])]
            if col < 2:
                wt0.append((ev, ev_a128))
            aap = a128fat[:, acol:acol + 1]
            abc = dataclasses.replace(aap, ap=[aap.ap[0], [0, 128]])
            em(lambda e, repb=repb, abc=abc: e.tensor_tensor(
                    repb[:], onetile[:], abc, OP.mult),
               waits=wt0 + [(sem_e, sem_e.n)], incs=[(sem_e, 1)])
            em(lambda e, repb=repb, sallb=sallb, dk=dk, s_=s_: e.tensor_tensor_scan(
                    sallb[:], repb[:], loc_all[:, dk, s_, :], 0.0, OP.mult, OP.add),
               waits=[(sem_e, sem_e.n)], incs=[(sem_e, 1)])
            em(lambda e, sallb=sallb, dk=dk, s_=s_: e.tensor_copy(
                    S_own[:, dk, s_, :], sallb[:, 111:127]),
               waits=[(sem_e, sem_e.n)], incs=[(sem_e, 1)])
        vv_scans = vv.n; gp_scans = gp.n
        p.v(lambda e: e.tensor_scalar(S_bf[:], S_own[:], 1.0 / D, None, OP.mult),
            waits=[(vv, vv_scans), (gp, gp_scans), (vv, vv.n)], incs=[(vv, 1)])
        vv_sbf = vv.n
        st_kp = {}
        for s_ in range(S):
            def mK(e, s_=s_):
                for dk in range(8):
                    ins = e.matmul(pC[0:1, 0:128], ones_c_b[:], apowc[:, dk * 16 + s_, :],
                                   start=(dk == 0), stop=(dk == 7))
                return ins
            wt_mk = [(ev2, ev2_apow)]
            if s_ == 0:
                wt_mk.append((vv, vv_bcm))
            if s_ >= 1:
                wt_mk.append((vv2, vv2_kprev))
            p.t(mK, waits=wt_mk, incs=[(mm2, 1)])
            mm2_k = mm2.n
            ksl = s_ % 2
            wt_kp = [(mm2, mm2_k)]
            if s_ >= 2:
                wt_kp.append((st, st_kp[s_ - 2]))
            p.v(lambda e, ksl=ksl: e.memset(kpadf[:, ksl, :], 0.0), waits=wt_kp,
                incs=[(vv2, 1)])
            p.v(lambda e, ksl=ksl: e.memset(kpadf[0:1, ksl, 127:128], 1.0),
                waits=[(vv2, vv2.n)], incs=[(vv2, 1)])
            p.v(lambda e, ksl=ksl: e.tensor_scalar(kpadf[0:1, ksl, 128:255], pC[0:1, 0:127],
                                                   1.0 / D, None, OP.mult),
                waits=[(vv2, vv2.n)], incs=[(vv2, 1)])
            p.v(lambda e, ksl=ksl: e.tensor_copy(kpadb[:, ksl, :], kpadf[:, ksl, :]),
                waits=[(vv2, vv2.n)], incs=[(vv2, 1)])
            vv2_k = vv2.n
            p.t(lambda e, ksl=ksl: e.matmul(pD[:, 0:255], ones_r_b[:], kpadb[0:1, ksl, :],
                                            start=True, stop=True),
                waits=[(vv2, vv2_k)], incs=[(mm2, 1)])
            mm2_k2 = mm2.n
            p.v(lambda e, ksl=ksl: e.tensor_copy(k2sb[:, ksl, :], pD[:, 0:255]),
                waits=[(mm2, mm2_k2)], incs=[(vv2, 1)])
            vv2_k2 = vv2.n
            p.sp(lambda e, s_=s_, ksl=ksl: e.dma_start(
                    out=kpad2_d.ap()[s_], in_=k2sb[:, ksl, :]),
                 waits=[(vv2, vv2_k2)], incs=[(st, 16)])
            st_kp[s_] = st.n
            vv2_kprev = vv2.n
        st_kpad = st.n
        for s_ in range(S):
            sl = s_ % 2
            wt_h = [(st, st_kpad)]
            if s_ >= 2:
                wt_h.append((mm2, mm2.n))
            p.sp(lambda e, s_=s_, sl=sl: e.dma_start(
                    out=hsb[:, sl, :],
                    in_=dram_ap(kpad2_d.ap(), s_ * 128 * 255, [[256, 128], [1, 128]])),
                 waits=wt_h, incs=[(ds, 16)])
            ds_h = ds.n
            wt_j = [(ds, ds_h)]
            if s_ >= 1:
                wt_j.append((vv2, vv2.n))
            p.t(lambda e, sl=sl: e.matmul(pA[:, 0:128], jrevb[:], hsb[:, sl, :],
                                          start=True, stop=True),
                waits=wt_j, incs=[(mm2, 1)])
            mm2_j = mm2.n
            p.v(lambda e, sl=sl: e.tensor_copy(msb[:, sl, :], pA[:, 0:128]),
                waits=[(mm2, mm2_j)], incs=[(vv2, 1)])
            vv2_m = vv2.n
            def mconv(e, s_=s_, sl=sl):
                e.matmul(pB[:, 256:272], msb[:, sl, :], B_cm[:, s_, 112:128],
                         start=True, stop=False)
                for dk in range(8):
                    ins = e.matmul(pB[:, 256:272], apowc[:, dk * 16 + s_, :], S_bf[:, dk, s_, :],
                                   start=False, stop=(dk == 7))
                return ins
            p.t(mconv, waits=[(vv2, vv2_m), (vv, vv_sbf)], incs=[(mm2, 1)])
            mm2_c = mm2.n
            p.v(lambda e, sl=sl: e.tensor_copy(esb[:, sl, :], pB[:, 256:272]),
                waits=[(mm2, mm2_c)], incs=[(vv2, 1)])
            vv2_e = vv2.n
            p.t(lambda e, sl=sl: e.transpose(pE[:, 0, 0:16].rearrange("p a -> p a") if False else pE[0:16, 1, 0:128], esb[:, sl, :], identb[:]),
                waits=[(vv2, vv2_e)], incs=[(mm2, 1)])
            mm2_t = mm2.n
            p.v(lambda e, s_=s_: e.tensor_copy(t2b[:, s_, :], pE[0:16, 1, 0:128]),
                waits=[(mm2, mm2_t)], incs=[(vv2, 1)])
            vv2_t = vv2.n
            p.sp(lambda e, s_=s_: e.dma_start(
                    out=dram_ap(xm_d.ap(), s_ * T, [[128, 16], [1, 128]]),
                    in_=t2b[:, s_, :]),
                 waits=[(vv2, vv2_t)], incs=[(st, 16)])
        st_xm = st.n
        p.sp(lambda e: e.dma_start(out=x_meanT[:], in_=xm_d.ap()),
             waits=[(st, st_xm)], incs=[(ds, 16)])
        ds_xm = ds.n
        p.v(lambda e: e.tensor_copy(x_meanTb[:], x_meanT[:]), waits=[(ds, ds_xm)], incs=[(vv, 1)])
        if debug:
            p.sp(lambda e: e.dma_start(out=dbg["dbg_xm"], in_=x_meanT[:]),
                 waits=[(ds, ds_xm)], incs=[(ds, 16)])
            p.sp(lambda e: e.dma_start(out=dbg["dbg_sown"], in_=S_own[:]), incs=[(ds, 16)])
        p.sp(lambda e: e.nop(), waits=[(ds, ds.n)])
        p.run(nc)
        _ph.close()

    if max_phase < 3:
        return nc, din, outT_d, dbg

    xnT = sb("xnT", (128, 8, T), BF16)
    outT = sb("outT_sb", (128, 8, T), F32)
    yT_guard = nc.sbuf_tensor("yT", [128, 8, T], BF16)
    yT = yT_guard.__enter__()

    _ph = ExitStack()
    xf32 = _ph.enter_context(nc.sbuf_tensor("xf32", [128, 2, 8, 256], F32))
    rsq_sb = _ph.enter_context(nc.sbuf_tensor("rsq_sb", [1, 2, 3, 256], F32))
    rs2bc = _ph.enter_context(nc.sbuf_tensor("rs2bc", [128, T], BF16))
    ccol = _ph.enter_context(nc.sbuf_tensor("ccol", [128, 4], F32))
    zt1 = _ph.enter_context(nc.sbuf_tensor("zt1", [128, 2, 512], BF16))
    lgt = _ph.enter_context(nc.sbuf_tensor("lgt", [128, 8], F32))
    lgt2 = _ph.enter_context(nc.sbuf_tensor("lgt2", [128, 8], F32))
    m1c = _ph.enter_context(nc.sbuf_tensor("m1c", [128, 4], F32))
    combt = _ph.enter_context(nc.sbuf_tensor("combt", [128, 8], BF16))
    xnscr = _ph.enter_context(nc.sbuf_tensor("xnscr", [128, 2, 256], F32))
    wpb = _ph.enter_context(nc.sbuf_tensor("wpb", [128, 2, 8, 128], BF16))
    wzb = _ph.enter_context(nc.sbuf_tensor("wzb", [128, 2, 8, 128], BF16))
    zscr = _ph.enter_context(nc.sbuf_tensor("zscr", [128, 2, 512], BF16))
    yscr = _ph.enter_context(nc.sbuf_tensor("yscr", [128, 2, 512], F32))
    qA = _ph.enter_context(nc.psum_tensor("qA", [128, 512], F32))
    qB = _ph.enter_context(nc.psum_tensor("qB", [128, 512], F32))
    qC = _ph.enter_context(nc.psum_tensor("qC", [128, 512], F32))
    qD = _ph.enter_context(nc.psum_tensor("qD", [128, 512], F32))
    qE = _ph.enter_context(nc.psum_tensor("qE", [128, 512], F32))
    qF = _ph.enter_context(nc.psum_tensor("qF", [128, 512], F32))
    qG = _ph.enter_context(nc.psum_tensor("qG", [128, 256], BF16))
    if True:
        p = Prog()
        vv_tc = {}; mm_tc = {}; vv_rs2bc = {}
        for tc in range(8):
            sl = tc % 2
            wt = []
            if tc >= 2:
                wt = [(vv, vv_tc[tc - 2]), (mm, mm_tc[tc - 2])]
            src = din["xt"][:, L - T + tc * 256: L - T + (tc + 1) * 256].rearrange(
                "(dk p) t -> p dk t", p=128)
            p.sp(lambda e, d=xf32[:, sl], s=src: e.dma_start(out=d, in_=s),
                 waits=wt, incs=[(ds, 16)])
            for r_ in range(3):
                p.sp(lambda e, sl=sl, r_=r_, tc=tc: e.dma_start(
                        out=rsq_sb[:, sl, r_, :],
                        in_=rsq_d.ap()[r_:r_ + 1, tc * 256:(tc + 1) * 256]),
                     incs=[(ds, 16)])
            ds_x = ds.n
            p.t(lambda e, sl=sl: e.matmul(qA[:, 0:256], ones_r_f[:], rsq_sb[0:1, sl, 0, :],
                                          start=True, stop=True),
                waits=[(ds, ds_x), (vv, vv_tc[tc - 1] if tc >= 1 else 0)])
            p.t(lambda e, sl=sl: e.matmul(qA[:, 256:512], ones_r_f[:], rsq_sb[0:1, sl, 1, :],
                                          start=True, stop=True), incs=[(mm, 1)])
            mm_b = mm.n
            p.t(lambda e, sl=sl: e.matmul(qC[:, 0:256], ones_r_f[:], rsq_sb[0:1, sl, 2, :],
                                          start=True, stop=True),
                waits=[(vv, vv_rs2bc[tc - 1]) if tc >= 1 else (vv, 0)], incs=[(mm, 1)])
            mm_r2 = mm.n
            p.v(lambda e, tc=tc: e.tensor_copy(rs2bc[:, tc * 256:(tc + 1) * 256], qC[:, 0:256]),
                waits=[(mm, mm_r2)], incs=[(vv, 1)])
            vv_rs2bc[tc] = vv.n
            for dk in range(8):
                p.v(lambda e, sl=sl, dk=dk: e.tensor_mul(xnscr[:, sl, :], xf32[:, sl, dk, :],
                                                         qA[:, 0:256]),
                    waits=[(mm, mm_b)] if dk == 0 else [])
                p.v(lambda e, sl=sl, dk=dk, tc=tc: e.tensor_sub(
                        xnT[:, dk, tc * 256:(tc + 1) * 256], xnscr[:, sl, :], qA[:, 256:512]))
                p.v(lambda e, sl=sl, dk=dk, tc=tc: e.tensor_add(
                        outT[:, dk, tc * 256:(tc + 1) * 256], xf32[:, sl, dk, :],
                        xnT[:, dk, tc * 256:(tc + 1) * 256]),
                    incs=[(vv, 1)])
            for h in range(2):
                tt = tc * 2 + h
                jcol = JOWN0 + tt
                def mg(e, sl=sl, h=h):
                    for dk in range(8):
                        ins = e.matmul(qB[:, 0:8], xf32[:, sl, dk, h * 128:(h + 1) * 128],
                                       gate_f[:, dk, :], start=(dk == 0), stop=(dk == 7))
                    return ins
                p.t(mg, waits=[(vv, vv.n)], incs=[(mm, 1)])
                mm_gx = mm.n
                def vc(fn, extra=()):
                    p.v(fn, waits=[(vv, vv.n)] + list(extra), incs=[(vv, 1)])
                p.v(lambda e, jcol=jcol: e.tensor_scalar(lgt2[:], sg_bc[:],
                                                         xnq_cm[:, jcol:jcol + 1], None, OP.mult),
                    waits=[(mm, mm_gx)], incs=[(vv, 1)])
                vc(lambda e, jcol=jcol: e.scalar_tensor_tensor(
                        lgt[:], qB[:, 0:8], rs1_cm[:, jcol:jcol + 1], lgt2[:],
                        OP.mult, OP.subtract))
                vc(lambda e: e.reduce_max(m1c[:, 0:1], lgt[:], axis=AX.X))
                vc(lambda e: e.tensor_scalar(lgt2[:], lgt[:], m1c[:, 0:1], None, OP.is_ge))
                vc(lambda e: e.scalar_tensor_tensor(lgt2[:], lgt2[:], NEG_BIG, lgt[:],
                                                    OP.mult, OP.add))
                vc(lambda e: e.reduce_max(m1c[:, 1:2], lgt2[:], axis=AX.X))
                vc(lambda e: e.tensor_scalar(m1c[:, 2:3], m1c[:, 0:1], -1.0, None, OP.mult))
                vv_r = vv.n
                p.s(lambda e: e.activation(lgt2[:], lgt[:], AF.Exp, bias=m1c[:, 2:3]),
                    waits=[(vv, vv_r)], incs=[(ev, 1)])
                p.s(lambda e: e.activation(m1c[:, 3:4], m1c[:, 1:2], AF.Exp, bias=m1c[:, 2:3]),
                    waits=[(ev, ev.n)], incs=[(ev, 1)])
                ev_r = ev.n
                p.v(lambda e: e.tensor_scalar(m1c[:, 3:4], m1c[:, 3:4], 1.0, None, OP.add),
                    waits=[(ev, ev_r)], incs=[(vv, 1)])
                vc(lambda e: e.reciprocal(m1c[:, 3:4], m1c[:, 3:4]))
                vc(lambda e: e.tensor_scalar(lgt[:], lgt[:], m1c[:, 1:2], None, OP.is_ge))
                vc(lambda e: e.scalar_tensor_tensor(lgt2[:], lgt2[:], m1c[:, 3:4], lgt[:],
                                                    OP.mult, OP.mult))
                vc(lambda e: e.tensor_copy(combt[:], lgt2[:]))
                vv_cb = vv.n
                p.t(lambda e: e.transpose(qG[0:8, 0:128], combt[:], identb[:]),
                    waits=[(vv, vv_cb)], incs=[(mm, 1)])
                mm_tr = mm.n
                p.v(lambda e, tt=tt: e.tensor_copy(comb8[:, tt * 128:(tt + 1) * 128],
                                                   qG[0:8, 0:128]),
                    waits=[(mm, mm_tr)], incs=[(vv, 1)])
            vv_tc[tc] = vv.n
            mm_tc[tc] = mm.n
        vv_p3a = vv.n
        if debug:
            p.sp(lambda e: e.dma_start(out=dbg["dbg_xnt"], in_=xnT[:]),
                 waits=[(vv, vv_p3a)], incs=[(ds, 16)])
            p.sp(lambda e: e.dma_start(out=dbg["dbg_comb8"], in_=comb8[:]), incs=[(ds, 16)])
        pP3 = (qC, qD); pZ3 = (qE, qF); pY3 = (qA, qB)
        qH = qG
        vv_rs2 = vv_p3a
        mm_chi = {}; vv_y = {}
        for chi in range(8):
            wsl = chi % 2
            wt_w = []
            if chi >= 2:
                wt_w = [(mm, mm_chi[chi - 2])]
            srcp = din["w_in_t"][:, chi * 128:(chi + 1) * 128].rearrange(
                "(dk p) c -> p dk c", p=128)
            srcz = din["w_in_t"][:, D + chi * 128: D + (chi + 1) * 128].rearrange(
                "(dk p) c -> p dk c", p=128)
            p.g(lambda e, d=wpb[:, wsl], s=srcp: e.dma_start(out=d, in_=s),
                waits=wt_w, incs=[(dg, 16)])
            p.g(lambda e, d=wzb[:, wsl], s=srcz: e.dma_start(out=d, in_=s), incs=[(dg, 16)])
            dg_w = dg.n
            wt_cb2 = [(dg, dg_w)]
            if chi == 0:
                wt_cb2.append((vv, vv_rs2))
            if chi >= 2:
                wt_cb2.append((vv, vv_y[(chi - 2) * 4 + 3]))
            def mcb2(e, wsl=wsl):
                for dk in range(8):
                    e.matmul(qB[:, 0:1], wpb[:, wsl, dk, :], lnbcol[:, dk:dk + 1],
                             start=(dk == 0), stop=(dk == 7))
                for dk in range(8):
                    ins = e.matmul(qB[:, 1:2], wzb[:, wsl, dk, :], lnbcol[:, dk:dk + 1],
                                   start=(dk == 0), stop=(dk == 7))
                return ins
            p.t(mcb2, waits=wt_cb2, incs=[(mm, 1)])
            mm_cb2 = mm.n
            p.v(lambda e, chi=chi: e.tensor_scalar(ccol[:, 0:1], qB[:, 0:1],
                                                   binfat[:, chi:chi + 1], None, OP.add),
                waits=[(mm, mm_cb2)], incs=[(vv, 1)])
            p.v(lambda e, chi=chi: e.tensor_scalar(ccol[:, 1:2], qB[:, 1:2],
                                                   binfat[:, 8 + chi:9 + chi], None, OP.add),
                waits=[(vv, vv.n)], incs=[(vv, 1)])
            for dk in range(8):
                p.v(lambda e, wsl=wsl, dk=dk: e.tensor_scalar(
                        wpb[:, wsl, dk, :], wpb[:, wsl, dk, :], gfat[:, dk:dk + 1],
                        None, OP.mult))
                p.v(lambda e, wsl=wsl, dk=dk: e.tensor_scalar(
                        wzb[:, wsl, dk, :], wzb[:, wsl, dk, :], gfat[:, dk:dk + 1],
                        None, OP.mult))
            p.v(lambda e: e.nop(), incs=[(vv, 1)])
            vv_scaled = vv.n
            for tc in range(4):
                gi = chi * 4 + tc
                psl = gi % 2
                wt_t = [(vv, vv_scaled)]
                if gi >= 2:
                    wt_t.append((vv, vv_y[gi - 2]))
                def mxz(e, wsl=wsl, tc=tc, psl=psl, chi=chi):
                    for dk in range(8):
                        e.matmul(pP3[psl][:, 0:512], wpb[:, wsl, dk, :],
                                 xnT[:, dk, tc * 512:(tc + 1) * 512],
                                 start=(dk == 0), stop=(dk == 7))
                    for dk in range(8):
                        e.matmul(pZ3[psl][:, 0:512], wzb[:, wsl, dk, :],
                                 xnT[:, dk, tc * 512:(tc + 1) * 512],
                                 start=(dk == 0), stop=(dk == 7))
                    return e.matmul(pY3[psl][:, 0:512], wct_sb[:, chi * 128:(chi + 1) * 128],
                                    x_meanTb[:, tc * 512:(tc + 1) * 512],
                                    start=True, stop=True)
                p.t(mxz, waits=wt_t, incs=[(mm, 1)])
                mm_xz = mm.n
                p.v(lambda e, psl=psl, tc=tc: e.tensor_mul(
                        zt1[:, psl, :], pZ3[psl][:, 0:512], rs2bc[:, tc * 512:(tc + 1) * 512]),
                    waits=[(mm, mm_xz)], incs=[(vv, 1)])
                p.s(lambda e, psl=psl: e.activation(zscr[:, psl, :], zt1[:, psl, :], AF.Silu,
                                                    bias=ccol[:, 1:2]),
                    waits=[(vv, vv.n)], incs=[(ev, 1)])
                ev_z = ev.n
                p.v(lambda e, psl=psl, tc=tc: e.tensor_mul(
                        yscr[:, psl, :], pP3[psl][:, 0:512], rs2bc[:, tc * 512:(tc + 1) * 512]))
                p.v(lambda e, psl=psl: e.tensor_scalar(
                        yscr[:, psl, :], yscr[:, psl, :], ccol[:, 0:1], None, OP.add))
                p.v(lambda e, psl=psl, chi=chi: e.scalar_tensor_tensor(
                        yscr[:, psl, :], yscr[:, psl, :], dfat[:, chi:chi + 1],
                        pY3[psl][:, 0:512], OP.mult, OP.add))
                p.v(lambda e, psl=psl, chi=chi, tc=tc: e.tensor_mul(
                        yT[:, chi, tc * 512:(tc + 1) * 512], yscr[:, psl, :], zscr[:, psl, :]),
                    waits=[(ev, ev_z)], incs=[(vv, 1)])
                vv_y[gi] = vv.n
            mm_chi[chi] = mm.n
        if debug:
            p.sp(lambda e: e.dma_start(out=dbg["dbg_yt"], in_=yT[:]),
                 waits=[(vv, vv.n)], incs=[(ds, 16)])
        p.sp(lambda e: e.nop(), waits=[(ds, ds.n)])
        p.run(nc)
        _ph.close()

    if max_phase < 4:
        return nc, din, outT_d, dbg

    _ph = ExitStack()
    woutb = _ph.enter_context(nc.sbuf_tensor("woutb", [128, 8, D], BF16))
    rA = _ph.enter_context(nc.psum_tensor("rA", [128, 512], F32))
    rB = _ph.enter_context(nc.psum_tensor("rB", [128, 512], F32))
    if True:
        p = Prog()
        p.g(lambda e: e.dma_start(out=woutb[:],
                                  in_=din["w_out_t"].rearrange("(dik p) do -> p dik do", p=128)),
            incs=[(dg, 16)])
        dg_w = dg.n
        rP = (rA, rB)
        vv_o = {}
        for dot in range(8):
            for tc in range(4):
                gi = dot * 4 + tc
                psl = gi % 2
                wt_t = [(dg, dg_w)]
                if gi >= 2:
                    wt_t.append((vv, vv_o[gi - 2]))
                def mwo(e, dot=dot, tc=tc, psl=psl):
                    for dik in range(8):
                        ins = e.matmul(rP[psl][:, 0:512], woutb[:, dik, dot * 128:(dot + 1) * 128],
                                       yT[:, dik, tc * 512:(tc + 1) * 512],
                                       start=(dik == 0), stop=(dik == 7))
                    return ins
                p.t(mwo, waits=wt_t, incs=[(mm, 1)])
                mm_o = mm.n
                p.v(lambda e, dot=dot, tc=tc, psl=psl: e.scalar_tensor_tensor(
                        outT[:, dot, tc * 512:(tc + 1) * 512], rP[psl][:, 0:512],
                        boutfat[:, dot:dot + 1], outT[:, dot, tc * 512:(tc + 1) * 512],
                        OP.add, OP.add),
                    waits=[(mm, mm_o)], incs=[(vv, 1)])
                vv_o[gi] = vv.n
        p.run(nc)
        _ph.close()

    yT_guard.__exit__(None, None, None)

    if max_phase < 5:
        return nc, din, outT_d, dbg

    _ph = ExitStack()
    hT = _ph.enter_context(nc.sbuf_tensor("hT", [128, 8, T], BF16))
    wgb = _ph.enter_context(nc.sbuf_tensor("wgb", [128, 2, 8, 512], BF16))
    wub = _ph.enter_context(nc.sbuf_tensor("wub", [128, 2, 8, 512], BF16))
    wdb = _ph.enter_context(nc.sbuf_tensor("wdb", [128, 2, 8, 128], BF16))
    cbc = _ph.enter_context(nc.sbuf_tensor("cbc", [128, 4, 512], BF16))
    sgscr = _ph.enter_context(nc.sbuf_tensor("sgscr", [128, 2, 512], BF16))
    uscr = _ph.enter_context(nc.sbuf_tensor("uscr", [128, 2, 512], BF16))
    mg0 = _ph.enter_context(nc.psum_tensor("mg0", [128, 512], F32))
    mg1 = _ph.enter_context(nc.psum_tensor("mg1", [128, 512], F32))
    mu0 = _ph.enter_context(nc.psum_tensor("mu0", [128, 512], F32))
    mu1 = _ph.enter_context(nc.psum_tensor("mu1", [128, 512], F32))
    mo0 = _ph.enter_context(nc.psum_tensor("mo0", [128, 512], F32))
    mo1 = _ph.enter_context(nc.psum_tensor("mo1", [128, 512], F32))
    mcb = _ph.enter_context(nc.psum_tensor("mcb", [128, 512], F32))
    if True:
        p = Prog()
        pg = (mg0, mg1); pu = (mu0, mu1); po = (mo0, mo1)
        wgu, wgufr, wd, wdfr = sm["wgu"], sm["wgufr"], sm["wd"], sm["wdfr"]
        pa, pb, hb = sm["pa"], sm["pb"], sm["hb"]
        gi5 = 0
        gb5 = 0
        gh = 0
        gw = 0
        evA = {}; vvA = {}; vvB = {}
        for e_ in range(E):
            for tcb in range(4):
                wt_cb = [(pb, pb.n)] if (e_ > 0 or tcb > 0) else []
                p.t(lambda e, e_=e_, tcb=tcb: e.matmul(
                        mcb[:, 0:512], selb[:, e_ * 128:(e_ + 1) * 128],
                        comb8[:, tcb * 512:(tcb + 1) * 512], start=True, stop=True),
                    waits=wt_cb, incs=[(pa, 1)])
                p.v(lambda e, tcb=tcb: e.tensor_copy(cbc[:, tcb, :], mcb[:, 0:512]),
                    waits=[(pa, pa.n)], incs=[(pb, 1)])
            pb_e = pb.n
            for q in range(4):
                eq = e_ * 4 + q
                for h in range(2):
                    wt_dma = [(wgufr, gh - 1)] if gh >= 2 else []
                    srcg = din["wg_t"][e_][:, q * 1024 + h * 512: q * 1024 + (h + 1) * 512] \
                        .rearrange("(dk p) i -> p dk i", p=128)
                    srcu = din["wu_t"][e_][:, q * 1024 + h * 512: q * 1024 + (h + 1) * 512] \
                        .rearrange("(dk p) i -> p dk i", p=128)
                    p.g(lambda e, d=wgb[:, gh % 2], s=srcg: e.dma_start(out=d, in_=s),
                        waits=wt_dma, incs=[(wgu, 16)])
                    p.g(lambda e, d=wub[:, gh % 2], s=srcu: e.dma_start(out=d, in_=s),
                        incs=[(wgu, 16)])
                    wgu_h = wgu.n
                    for icw in range(4):
                        ic = h * 4 + icw
                        for tc in range(4):
                            psl = gi5 % 2
                            wt_t = [(wgu, wgu_h)]
                            if e_ == 0 and q == 0 and h == 0 and icw == 0 and tc == 0:
                                wt_t.append((pb, pb_e))
                            if gi5 >= 2:
                                wt_t.append((ev2, evA[gi5 - 2]))
                                wt_t.append((vv2, vvA[gi5 - 2]))
                            def mA(e, wsl=gh % 2, icw=icw, tc=tc, psl=psl):
                                for dk in range(8):
                                    e.matmul(pg[psl][:, 0:512],
                                             wgb[:, wsl, dk, icw * 128:(icw + 1) * 128],
                                             xnT[:, dk, tc * 512:(tc + 1) * 512],
                                             start=(dk == 0), stop=(dk == 7))
                                for dk in range(8):
                                    ins = e.matmul(pu[psl][:, 0:512],
                                                   wub[:, wsl, dk, icw * 128:(icw + 1) * 128],
                                                   xnT[:, dk, tc * 512:(tc + 1) * 512],
                                                   start=(dk == 0), stop=(dk == 7))
                                return ins
                            last_of_half = (icw == 3 and tc == 3)
                            incs_t = [(mm2, 1)] + ([(wgufr, 1)] if last_of_half else [])
                            p.t(mA, waits=wt_t, incs=incs_t)
                            mm2_a = mm2.n
                            wt_sg = [(mm2, mm2_a)]
                            if gi5 >= 2:
                                wt_sg.append((vv2, vvA[gi5 - 2]))
                            p.s(lambda e, psl=psl: e.activation(sgscr[:, psl, :],
                                                                pg[psl][:, 0:512], AF.Silu),
                                waits=wt_sg, incs=[(ev2, 1)])
                            evA[gi5] = ev2.n
                            wt_u = [(mm2, mm2_a)]
                            if eq >= 1:
                                wt_u.append((hb, eq))
                            p.v(lambda e, psl=psl, tc=tc: e.tensor_mul(
                                    uscr[:, psl, :], pu[psl][:, 0:512], cbc[:, tc, :]),
                                waits=wt_u)
                            p.v(lambda e, psl=psl, ic=ic, tc=tc: e.tensor_mul(
                                    hT[:, ic, tc * 512:(tc + 1) * 512],
                                    sgscr[:, psl, :], uscr[:, psl, :]),
                                waits=[(ev2, evA[gi5])], incs=[(vv2, 1)])
                            vvA[gi5] = vv2.n
                            gi5 += 1
                    gh += 1
                vv2_heq = vv2.n
                for dmt in range(8):
                    wt_dma = [(wdfr, gw - 1)] if gw >= 2 else []
                    srcd = din["wd_t"][e_][q * 1024:(q + 1) * 1024, dmt * 128:(dmt + 1) * 128] \
                        .rearrange("(ik p) d -> p ik d", p=128)
                    p.g(lambda e, d=wdb[:, gw % 2], s=srcd: e.dma_start(out=d, in_=s),
                        waits=wt_dma, incs=[(wd, 16)])
                    wd_c = wd.n
                    for tc in range(4):
                        psl = gb5 % 2
                        wt_t = [(wd, wd_c)]
                        if tc == 0 and dmt == 0:
                            wt_t.append((vv2, vv2_heq))
                        if gb5 >= 2:
                            wt_t.append((vv2, vvB[gb5 - 2]))
                        def mB(e, wsl=gw % 2, tc=tc, psl=psl):
                            for ik in range(8):
                                ins = e.matmul(po[psl][:, 0:512], wdb[:, wsl, ik, :],
                                               hT[:, ik, tc * 512:(tc + 1) * 512],
                                               start=(ik == 0), stop=(ik == 7))
                            return ins
                        last_of_chunk = (tc == 3)
                        last_of_eq = (dmt == 7 and tc == 3)
                        incs_t = [(mm2, 1)] + ([(wdfr, 1)] if last_of_chunk else []) \
                            + ([(hb, 1)] if last_of_eq else [])
                        p.t(mB, waits=wt_t, incs=incs_t)
                        mm2_b = mm2.n
                        p.v(lambda e, dmt=dmt, tc=tc, psl=psl: e.tensor_add(
                                outT[:, dmt, tc * 512:(tc + 1) * 512],
                                outT[:, dmt, tc * 512:(tc + 1) * 512], po[psl][:, 0:512]),
                            waits=[(mm2, mm2_b)], incs=[(vv2, 1)])
                        vvB[gb5] = vv2.n
                        gb5 += 1
                    gw += 1
        p.run(nc)
        _ph.close()

    p = Prog()
    p.sp(lambda e: e.dma_start(out=outT_d.rearrange("(dk p) t -> p dk t", p=128),
                               in_=outT[:]),
         incs=[(st, 16)])
    p.sp(lambda e: e.nop(), waits=[(st, st.n)])
    p.run(nc)
    return nc, din, outT_d, dbg


def _stage_inputs(inputs):
    f = np.float32
    x = np.asarray(inputs["L0_attn_x"], dtype=f)
    xT = np.ascontiguousarray(x[0].T)
    W_in = np.asarray(inputs["L0_attn_W_in"], f)
    W_out = np.asarray(inputs["L0_attn_W_out"], f)
    W_B = np.asarray(inputs["L0_attn_W_B"], f)
    W_C = np.asarray(inputs["L0_attn_W_C"], f)
    A_log = np.asarray(inputs["L0_attn_A_log"], f)
    D_skip = np.asarray(inputs["L0_attn_D_skip"], f)
    log_dt = np.asarray(inputs["L0_attn_log_dt"], f)
    b_in = np.asarray(inputs["L0_attn_b_in"], f)
    b_out = np.asarray(inputs["L0_attn_b_out"], f)
    ln_g = np.asarray(inputs["L0_attn_ln_g"], f)
    ln_b = np.asarray(inputs["L0_attn_ln_b"], f)
    gate_w = np.asarray(inputs["L0_ffn_gate_w"], f)
    wg = np.asarray(inputs["L0_ffn_wg"], f)
    wu = np.asarray(inputs["L0_ffn_wu"], f)
    wd = np.asarray(inputs["L0_ffn_wd"], f)

    shared = {
        "w_in_t": np.ascontiguousarray(W_in.T),
        "w_in_upn": np.ascontiguousarray(W_in[0:D, :]),
        "w_out_t": np.ascontiguousarray(W_out.T),
        "w_b_t": np.ascontiguousarray(W_B.T),
        "w_c_t": np.ascontiguousarray(W_C.T),
        "gate_t": np.ascontiguousarray(gate_w.T),
        "wg_t": np.ascontiguousarray(wg.transpose(0, 2, 1)),
        "wu_t": np.ascontiguousarray(wu.transpose(0, 2, 1)),
        "wd_t": np.ascontiguousarray(wd.transpose(0, 2, 1)),
        "g_fat": np.ascontiguousarray(ln_g.reshape(8, 128).T),
        "b_fat": np.ascontiguousarray(ln_b.reshape(8, 128).T),
        "d_fat": np.ascontiguousarray(D_skip.reshape(8, 128).T),
        "logdt_fat": np.ascontiguousarray(log_dt.reshape(8, 128).T),
        "bout_fat": np.ascontiguousarray(b_out.reshape(8, 128).T),
        "bin_fat": np.ascontiguousarray(b_in.reshape(16, 128).T),
        "binup_col": np.ascontiguousarray(b_in[0:D].reshape(8, 128).T),
        "lnb_col": np.ascontiguousarray(ln_b.reshape(8, 128).T),
        "alog_fat": np.ascontiguousarray(
            A_log.reshape(8, 128, S).transpose(1, 0, 2).reshape(128, 128)),
    }
    in_maps = []
    for c in range(NCORE):
        pad = (NCORE - 1 - c) * T
        xt_c = np.zeros((D, L), f)
        if pad:
            xt_c[:, pad:] = xT[:, :L - pad]
        else:
            xt_c[:] = xT
        mask = np.zeros((128, J), f)
        mask[:, JOWN0 - 16 * c:] = 1.0
        in_maps.append({**shared, "xt": xt_c, "mask_cm": mask})
    return in_maps


def kernel(**inputs):
    in_maps = _stage_inputs(inputs)
    nc, din, outT_d, dbg = build(debug=False)
    res = run_bass_kernel_spmd(nc, in_maps, core_ids=list(range(NCORE)))
    outs = [np.asarray(res.results[c]["outT"]) for c in range(NCORE)]
    full = np.concatenate(outs, axis=1)
    return np.ascontiguousarray(full.T).reshape(1, L, D).astype(np.float32)


# version 04 (final)
# speedup vs baseline: 1.2366x; 1.0239x over previous
import dataclasses
from contextlib import ExitStack
import numpy as np
import ml_dtypes
import concourse.bass as bass
import concourse.mybir as mybir
from concourse.bass_utils import run_bass_kernel_spmd

F32 = mybir.dt.float32
BF16 = mybir.dt.bfloat16
AF = mybir.ActivationFunctionType
AX = mybir.AxisListType
OP = mybir.AluOpType

D = 1024
S = 16
L = 16384
E = 8
II = 4096
NCORE = 8
T = 2048
TC = 128
J = L // TC
JOWN0 = J - T // TC
EPS = 1e-5
LOG_LO = float(np.log(np.float64(1e-8)))
LOG_HI = float(np.log1p(np.float64(-1e-8)))
DTL, DTH = 1e-4, 1.0
NEG_BIG = -1.0e30


class Sem:
    def __init__(self, nc, name):
        self.h = nc.alloc_semaphore(name)
        self.n = 0


class Prog:
    ENGINES = ("tensor", "vector", "scalar", "gpsimd", "sync")

    def __init__(self):
        self.items = {e: [] for e in self.ENGINES}

    def emit(self, eng, fn, waits=(), incs=()):
        waits = [(s.h, n) for (s, n) in waits if n > 0]
        snap = []
        for (s, k) in incs:
            s.n += k
            snap.append((s.h, k))

        def closure(engine, _w=waits, _f=fn, _i=snap):
            for (h, n) in _w:
                engine.wait_ge(h, n)
            ins = _f(engine)
            if _i:
                ins.then_inc(*_i[0])
                for (h, k) in _i[1:]:
                    engine.nop().then_inc(h, k)
        self.items[eng].append(closure)

    def t(self, fn, **kw): self.emit("tensor", fn, **kw)
    def v(self, fn, **kw): self.emit("vector", fn, **kw)
    def s(self, fn, **kw): self.emit("scalar", fn, **kw)
    def g(self, fn, **kw): self.emit("gpsimd", fn, **kw)
    def sp(self, fn, **kw): self.emit("sync", fn, **kw)

    def run(self, nc):
        with nc.Block() as block:
            for e in self.ENGINES:
                lst = self.items[e]
                if not lst:
                    continue
                getattr(block, e)(lambda eng, lst=lst: [f(eng) for f in lst])


def dram_ap(ap0, offset, pattern):
    return dataclasses.replace(ap0, offset=offset, ap=pattern)


def build(debug=False, max_phase=9):
    nc = bass.Bass("TRN2", target_bir_lowering=False, detect_race_conditions=False)

    din = {}
    def di(name, shape, dt=F32):
        din[name] = nc.dram_tensor(name, list(shape), dt, kind="ExternalInput").ap()
    di("xt", (D, L))
    di("mask_cm", (128, J))
    di("w_in_t", (D, 2 * D))
    di("w_in_upn", (D, D))
    di("w_out_t", (D, D))
    di("w_b_t", (D, S))
    di("w_c_t", (S, D))
    di("gate_t", (D, E))
    di("wg_t", (E, D, II))
    di("wu_t", (E, D, II))
    di("wd_t", (E, II, D))
    di("g_fat", (128, 8)); di("b_fat", (128, 8)); di("d_fat", (128, 8))
    di("logdt_fat", (128, 8)); di("bout_fat", (128, 8)); di("bin_fat", (128, 16))
    di("binup_col", (128, 8)); di("lnb_col", (128, 8))
    di("alog_fat", (128, 128))

    outT_d = nc.dram_tensor("outT", [D, T], F32, kind="ExternalOutput").ap()
    dbg = {}
    def do_(name, shape, dt=F32):
        if debug:
            dbg[name] = nc.dram_tensor(name, list(shape), dt, kind="ExternalOutput").ap()
    do_("dbg_bcm", (128, S, J), BF16)
    do_("dbg_mu", (128, J)); do_("dbg_rs1", (128, J)); do_("dbg_q2", (128, J))
    do_("dbg_xm", (S, T))
    do_("dbg_xnt", (128, 8, T), BF16)
    do_("dbg_comb8", (E, T), BF16)
    do_("dbg_yt", (128, 8, T), BF16)
    do_("dbg_logafat", (128, 128))
    do_("dbg_statw", (128, 8, 18), BF16)
    do_("dbg_rows", (1, 64))
    do_("dbg_cbm", (128, S)); do_("dbg_negsw", (128, S))
    do_("dbg_pcm", (128, S, 128))
    do_("dbg_sown", (128, 8, S, 16))

    ident_bf_d = nc.inline_tensor(np.eye(128, dtype=ml_dtypes.bfloat16), name="c_ident_bf").ap()
    ident_f32_d = nc.inline_tensor(np.eye(128, dtype=np.float32), name="c_ident_f32").ap()
    jrev_np = np.zeros((128, 128), dtype=ml_dtypes.bfloat16)
    for c in range(128):
        jrev_np[127 - c, c] = 1.0
    jrev_d = nc.inline_tensor(jrev_np, name="c_jrev").ap()
    ones_r_b_d = nc.inline_tensor(np.ones((1, 128), dtype=ml_dtypes.bfloat16), name="c_ones_r_b").ap()
    ones_r_f_d = nc.inline_tensor(np.ones((1, 128), dtype=np.float32), name="c_ones_r_f").ap()
    ones_c_b_d = nc.inline_tensor(np.ones((128, 1), dtype=ml_dtypes.bfloat16), name="c_ones_c_b").ap()
    ones_c_f_d = nc.inline_tensor(np.ones((128, 1), dtype=np.float32), name="c_ones_c_f").ap()
    iota_p1_d = nc.inline_tensor(np.broadcast_to(np.arange(1, 129, dtype=np.float32),
                                                 (128, 128)).copy(), name="c_iota_p1").ap()
    iota_r2_d = nc.inline_tensor(np.broadcast_to(127.0 - np.arange(128, dtype=np.float32),
                                                 (128, 128)).copy(), name="c_iota_r2").ap()
    sel_np = np.zeros((E, E * 128), dtype=ml_dtypes.bfloat16)
    for e in range(E):
        sel_np[e, e * 128:(e + 1) * 128] = 1.0
    sel_d = nc.inline_tensor(sel_np, name="c_sel").ap()

    kpad2_d = nc.dram_tensor("kpad2_d", [S, 128, 255], BF16, kind="Internal")
    xm_d = nc.dram_tensor("xm_d", [S, T], F32, kind="Internal")
    rsq_d = nc.dram_tensor("rsq_d", [3, T], F32, kind="Internal")

    names = ("dg", "ds", "mm", "ev", "vv", "mm2", "vv2", "ev2", "st",
             "wgu", "wgufr", "wd", "wdfr", "pa", "pb", "hb")
    sm = {k: Sem(nc, k) for k in names}
    dg, ds, mm, ev, vv = sm["dg"], sm["ds"], sm["mm"], sm["ev"], sm["vv"]
    mm2, vv2, ev2, st = sm["mm2"], sm["vv2"], sm["ev2"], sm["st"]

    ctx = nc.ctx
    def sb(name, shape, dt=F32):
        return ctx.enter_context(nc.sbuf_tensor(name, list(shape), dt))

    statW = sb("statW", (128, 8, 18), BF16)
    wct_sb = sb("wct_sb", (S, D), BF16)
    gate_f = sb("gate_f", (128, 8, E), F32)
    gfat = sb("gfat", (128, 8)); bfat_ = sb("bfat", (128, 8))
    dfat = sb("dfat", (128, 8))
    boutfat = sb("boutfat", (128, 8)); binfat = sb("binfat", (128, 16))
    logafat = sb("logafat", (128, 128))
    a128fat = sb("a128fat", (128, 128))
    dtm_bc = sb("dtm_bc", (128, 1))
    cbm_bc = sb("cbm_bc", (128, S))
    negsw_bc = sb("negsw_bc", (128, S))
    sg_bc = sb("sg_bc", (128, E))
    identb = sb("identb", (128, 128), BF16)
    identf = sb("identf", (128, 128), F32)
    jrevb = sb("jrevb", (128, 128), BF16)
    ones_r_b = sb("ones_r_b", (1, 128), BF16)
    ones_r_f = sb("ones_r_f", (1, 128), F32)
    ones_c_b = sb("ones_c_b", (128, 1), BF16)
    ones_c_f = sb("ones_c_f", (128, 1), F32)
    iota_p1 = sb("iota_p1", (128, 128), F32)
    iota_r2 = sb("iota_r2", (128, 128), F32)
    selb = sb("selb", (E, E * 128), BF16)
    mask_sb = sb("mask_sb", (128, J), F32)
    B_cm = sb("B_cm", (128, S, J), BF16)
    x_meanTb = sb("x_meanTb", (S, T), BF16)
    comb8 = sb("comb8", (E, T), BF16)
    rs1_cm = sb("rs1_cm", (128, J))
    xnq_cm = sb("xnq_cm", (128, J))
    lnbcol = sb("lnbcol", (128, 8), BF16)

    _ph = ExitStack()
    winupn = _ph.enter_context(nc.sbuf_tensor("winupn", [128, 8, D], BF16))
    wbt = _ph.enter_context(nc.sbuf_tensor("wbt", [128, 8, S], BF16))
    binup = _ph.enter_context(nc.sbuf_tensor("binup", [128, 8], BF16))
    wbbraw = _ph.enter_context(nc.sbuf_tensor("wbbraw", [128, 8, S], BF16))
    p0rows = _ph.enter_context(nc.sbuf_tensor("p0rows", [1, 64], F32))
    alogfat = _ph.enter_context(nc.sbuf_tensor("alogfat", [128, 128], F32))
    ldtfat = _ph.enter_context(nc.sbuf_tensor("ldtfat", [128, 8], F32))
    dt_fat = _ph.enter_context(nc.sbuf_tensor("dt_fat", [128, 8], F32))
    dt8 = _ph.enter_context(nc.sbuf_tensor("dt8", [128, 1], F32))
    ps0 = _ph.enter_context(nc.psum_tensor("ps0", [128, 512], F32))
    ps1 = _ph.enter_context(nc.psum_tensor("ps1", [128, 512], F32))
    ps2 = _ph.enter_context(nc.psum_tensor("ps2", [128, 512], F32))
    if True:
        p = Prog()
        def ld_s(dst, src):
            p.sp(lambda e, d=dst, s=src: e.dma_start(out=d, in_=s), incs=[(ds, 16)])
        def ld_g(dst, src):
            p.g(lambda e, d=dst, s=src: e.dma_start(out=d, in_=s), incs=[(dg, 16)])
        ld_s(gfat[:], din["g_fat"]); ld_s(bfat_[:], din["b_fat"])
        ld_s(dfat[:], din["d_fat"]); ld_s(ldtfat[:], din["logdt_fat"])
        ld_s(boutfat[:], din["bout_fat"]); ld_s(binfat[:], din["bin_fat"])
        ld_s(alogfat[:], din["alog_fat"]); ld_s(mask_sb[:], din["mask_cm"])
        ld_s(identf[:], ident_f32_d); ld_s(ones_r_f[:], ones_r_f_d)
        ld_s(ones_c_f[:], ones_c_f_d)
        ld_s(iota_p1[:], iota_p1_d); ld_s(iota_r2[:], iota_r2_d)
        ld_s(gate_f[:], din["gate_t"].rearrange("(dk p) e -> p dk e", p=128))
        ld_s(identb[:], ident_bf_d); ld_s(jrevb[:], jrev_d)
        ld_s(ones_r_b[:], ones_r_b_d); ld_s(ones_c_b[:], ones_c_b_d)
        ld_s(selb[:], sel_d)
        n_lds = ds.n
        ld_g(wct_sb[:], din["w_c_t"])
        ld_g(winupn[:], din["w_in_upn"].rearrange("(dok p) di -> p dok di", p=128))
        ld_g(wbt[:], din["w_b_t"].rearrange("(dok p) s -> p dok s", p=128))
        ld_g(binup[:], din["binup_col"])
        ld_g(lnbcol[:], din["lnb_col"])
        n_ldg = dg.n

        p.v(lambda e: e.memset(p0rows[:], 0.0))
        p.v(lambda e: e.memset(statW[:, :, 0:1], 1.0), waits=[(ds, n_lds)])
        p.v(lambda e: e.memset(statW[:, :, 17:18], 0.0))
        p.s(lambda e: e.activation(dt_fat[:], ldtfat[:], AF.Exp),
            waits=[(ds, n_lds)], incs=[(ev, 1)])
        ev_dt = ev.n
        p.v(lambda e: e.tensor_scalar(dt_fat[:], dt_fat[:], DTL, DTH, OP.max, OP.min),
            waits=[(ev, ev_dt)], incs=[(vv, 1)])
        vv_dt = vv.n
        p.s(lambda e: e.activation(logafat[:], alogfat[:], AF.Exp), incs=[(ev, 1)])
        ev_la = ev.n
        for dhi in range(8):
            p.v(lambda e, k=dhi: e.tensor_scalar(logafat[:, k * 16:(k + 1) * 16],
                                                 logafat[:, k * 16:(k + 1) * 16],
                                                 dt_fat[:, k:k + 1], -1.0, OP.mult, OP.mult),
                waits=[(ev, ev_la), (vv, vv_dt)] if dhi == 0 else [])
        p.v(lambda e: e.nop(), incs=[(vv, 1)])
        p.v(lambda e: e.tensor_scalar(logafat[:], logafat[:], LOG_LO, LOG_HI, OP.max, OP.min),
            waits=[(vv, vv.n)], incs=[(vv, 1)])
        vv_lf = vv.n
        p.s(lambda e: e.activation(a128fat[:], logafat[:], AF.Exp, scale=128.0),
            waits=[(vv, vv_lf)], incs=[(ev, 1)])
        ev_a128 = ev.n
        p.v(lambda e: e.reduce_sum(dt8[:], dt_fat[:], axis=AX.X), incs=[(vv, 1)])
        vv_dt8 = vv.n
        p.t(lambda e: e.matmul(ps0[0:1, 0:1], ones_c_f[:], dt8[:], start=True, stop=True),
            waits=[(vv, vv_dt8)], incs=[(mm, 1)])
        mm_dtm = mm.n
        p.v(lambda e: e.tensor_scalar(p0rows[0:1, 0:1], ps0[0:1, 0:1], 1.0 / D, None, OP.mult),
            waits=[(mm, mm_dtm)], incs=[(vv, 1)])
        vv_dtm = vv.n
        p.t(lambda e: e.matmul(ps1[:, 0:1], ones_r_f[:], p0rows[0:1, 0:1], start=True, stop=True),
            waits=[(vv, vv_dtm)], incs=[(mm, 1)])
        mm_dbc = mm.n
        p.v(lambda e: e.tensor_copy(dtm_bc[:], ps1[:, 0:1]), waits=[(mm, mm_dbc)], incs=[(vv, 1)])

        vv_wbb = {}
        for dik in range(8):
            def mmf(e, dik=dik):
                for dok in range(8):
                    ins = e.matmul(ps2[:, 0:S], winupn[:, dok, dik * 128:(dik + 1) * 128],
                                   wbt[:, dok, :], start=(dok == 0), stop=(dok == 7))
                return ins
            wt = [(dg, n_ldg)] if dik == 0 else [(vv, vv_wbb[dik - 1])]
            p.t(mmf, waits=wt, incs=[(mm, 1)])
            mm_w = mm.n
            p.v(lambda e, dik=dik: e.tensor_copy(wbbraw[:, dik, :], ps2[:, 0:S]),
                waits=[(mm, mm_w)])
            p.v(lambda e, dik=dik: e.tensor_scalar(statW[:, dik, 1:17], ps2[:, 0:S],
                                                   gfat[:, dik:dik + 1], None, OP.mult),
                incs=[(vv, 1)])
            vv_wbb[dik] = vv.n
        def wbf(e):
            for dik in range(8):
                ins = e.matmul(ps0[0:1, 4:20], lnbcol[:, dik:dik + 1], wbbraw[:, dik, :],
                               start=(dik == 0), stop=(dik == 7))
            return ins
        p.t(wbf, waits=[(vv, vv_wbb[7])], incs=[(mm, 1)])
        def cbf(e):
            for dok in range(8):
                ins = e.matmul(ps1[0:1, 4:20], binup[:, dok:dok + 1], wbt[:, dok, :],
                               start=(dok == 0), stop=(dok == 7))
            return ins
        p.t(cbf, incs=[(mm, 1)])
        def swf(e):
            for dik in range(8):
                ins = e.matmul(ps2[0:1, 0:S], ones_c_b[:], statW[:, dik, 1:17],
                               start=(dik == 0), stop=(dik == 7))
            return ins
        p.t(swf, incs=[(mm, 1)])
        def sgf(e):
            for dk in range(8):
                ins = e.matmul(ps0[0:1, 24:32], ones_c_f[:], gate_f[:, dk, :],
                               start=(dk == 0), stop=(dk == 7))
            return ins
        p.t(sgf, incs=[(mm, 1)])
        mm_p0 = mm.n
        p.v(lambda e: e.tensor_copy(p0rows[0:1, 4:20], ps0[0:1, 4:20]), waits=[(mm, mm_p0)],
            incs=[(vv, 1)])
        p.v(lambda e: e.tensor_add(p0rows[0:1, 4:20], p0rows[0:1, 4:20], ps1[0:1, 4:20]),
            waits=[(vv, vv.n)], incs=[(vv, 1)])
        p.v(lambda e: e.tensor_scalar(p0rows[0:1, 20:36], ps2[0:1, 0:S], -1.0, None, OP.mult))
        p.v(lambda e: e.tensor_copy(p0rows[0:1, 40:48], ps0[0:1, 24:32]), waits=[(vv, vv.n)],
            incs=[(vv, 1)])
        vv_rows = vv.n
        p.t(lambda e: e.matmul(ps0[:, 32:48], ones_r_f[:], p0rows[0:1, 4:20], start=True, stop=True),
            waits=[(vv, vv_rows)])
        p.t(lambda e: e.matmul(ps1[:, 32:48], ones_r_f[:], p0rows[0:1, 20:36], start=True, stop=True))
        p.t(lambda e: e.matmul(ps2[:, 32:40], ones_r_f[:], p0rows[0:1, 40:48], start=True, stop=True),
            incs=[(mm, 1)])
        mm_bc = mm.n
        p.v(lambda e: e.tensor_scalar(cbm_bc[:], ps0[:, 32:48], dtm_bc[:, 0:1], None, OP.mult),
            waits=[(mm, mm_bc)])
        p.v(lambda e: e.tensor_copy(negsw_bc[:], ps1[:, 32:48]))
        p.v(lambda e: e.tensor_copy(sg_bc[:], ps2[:, 32:40]), incs=[(vv, 1)])
        if debug:
            vfin = vv.n
            p.sp(lambda e: e.dma_start(out=dbg["dbg_logafat"], in_=logafat[:]),
                 waits=[(vfin_s, vfin) for vfin_s in (vv,)], incs=[(ds, 16)])
            p.sp(lambda e: e.dma_start(out=dbg["dbg_statw"], in_=statW[:]), incs=[(ds, 16)])
            p.sp(lambda e: e.dma_start(out=dbg["dbg_cbm"], in_=cbm_bc[:]), incs=[(ds, 16)])
            p.sp(lambda e: e.dma_start(out=dbg["dbg_negsw"], in_=negsw_bc[:]), incs=[(ds, 16)])
            p.sp(lambda e: e.dma_start(out=dbg["dbg_rows"], in_=p0rows[:]), incs=[(ds, 16)])
            p.sp(lambda e: e.nop(), waits=[(ds, ds.n)])
        vv_p0 = vv.n
        _ph.close()

    if max_phase < 1:
        p.run(nc)
        return nc, din, outT_d, dbg

    _ph = ExitStack()
    P_cm = _ph.enter_context(nc.sbuf_tensor("P_cm", [128, S, 128], F32))
    sumx_cm = _ph.enter_context(nc.sbuf_tensor("sumx_cm", [128, 128], F32))
    sumsq_cm = _ph.enter_context(nc.sbuf_tensor("sumsq_cm", [128, 128], F32))
    scrA = _ph.enter_context(nc.sbuf_tensor("scrA", [128, 128], F32))
    scrB = _ph.enter_context(nc.sbuf_tensor("scrB", [128, 128], F32))
    mu_cm = _ph.enter_context(nc.sbuf_tensor("mu_cm", [128, J], F32))
    u_cm = _ph.enter_context(nc.sbuf_tensor("u_cm", [128, J], F32))
    q2_cm = _ph.enter_context(nc.sbuf_tensor("q2_cm", [128, J], F32))
    rs2_cm = _ph.enter_context(nc.sbuf_tensor("rs2_cm", [128, J], F32))
    t2f = _ph.enter_context(nc.sbuf_tensor("t2f", [16, 3, 128], F32))
    pE = _ph.enter_context(nc.psum_tensor("pE", [128, 2, 1024], BF16))
    apowc = _ph.enter_context(nc.sbuf_tensor("apowc", [128, 128, 128], BF16))
    gsc_c = _ph.enter_context(nc.sbuf_tensor("gsc_c", [128, 2, 16, 128], BF16))
    arev_all = _ph.enter_context(nc.sbuf_tensor("arev_all", [128, 128, 128], BF16))
    scrbig = _ph.enter_context(nc.sbuf_tensor("scrbig", [128, 2, 2048], F32))
    _phx = ExitStack()
    xbf = _phx.enter_context(nc.sbuf_tensor("xbf", [128, 2, 8, 512], BF16))
    xsq = _phx.enter_context(nc.sbuf_tensor("xsq", [128, 2, 8, 512], BF16))
    _php1 = ExitStack()
    pP0 = _php1.enter_context(nc.psum_tensor("pP0", [128, 512], F32))
    pP1 = _php1.enter_context(nc.psum_tensor("pP1", [128, 512], F32))
    pQ0 = _php1.enter_context(nc.psum_tensor("pQ0", [128, 512], F32))
    pQ1 = _php1.enter_context(nc.psum_tensor("pQ1", [128, 512], F32))
    pT = _php1.enter_context(nc.psum_tensor("pT", [128, 512], F32))
    if True:
        pP = (pP0, pP1); pQ = (pQ0, pQ1)
        import dataclasses as _dc
        vv2_tr = {}
        ev2_gsc = {}
        mm2_tr = {}

        def p2prep_units():
            for q in range(8):
                ssl = q % 2
                la = logafat[:, q * 16:(q + 1) * 16]
                in0 = _dc.replace(la, ap=[la.ap[0], [1, 16], [0, 128]])
                i1 = iota_p1[:, :]
                in1p = _dc.replace(i1, ap=[i1.ap[0], [0, 16], [1, 128]])
                outv = scrbig[:, ssl, :].rearrange("p (a c) -> p a c", a=16)
                wt_q = [(ev2, ev2.n - 1)] if q >= 2 else [(vv, vv_lf)]
                p.v(lambda e, in0=in0, in1p=in1p, outv=outv: e.tensor_tensor(
                        outv, in0, in1p, OP.mult),
                    waits=wt_q, incs=[(vv2, 1)])
                p.s(lambda e, q=q, ssl=ssl: e.activation(
                        apowc[:, q * 16:(q + 1) * 16, :].rearrange("p a c -> p (a c)"),
                        scrbig[:, ssl, :], AF.Exp),
                    waits=[(vv2, vv2.n)], incs=[(ev2, 1)])
                yield
            p2_state["ev2_apow"] = ev2.n
            for q in range(8):
                ssl = q % 2
                la = logafat[:, q * 16:(q + 1) * 16]
                in0 = _dc.replace(la, ap=[la.ap[0], [1, 16], [0, 128]])
                i2 = iota_r2[:, :]
                in1r = _dc.replace(i2, ap=[i2.ap[0], [0, 16], [1, 128]])
                outv = scrbig[:, ssl, :].rearrange("p (a c) -> p a c", a=16)
                wt_q = [(ev2, ev2_gsc[q - 2])] if q >= 2 else [(ev2, p2_state["ev2_apow"])]
                p.v(lambda e, in0=in0, in1r=in1r, outv=outv: e.tensor_tensor(
                        outv, in0, in1r, OP.mult),
                    waits=wt_q, incs=[(vv2, 1)])
                wt_e = [(vv2, vv2.n)]
                if q >= 2:
                    wt_e.append((mm2, mm2_tr[q - 2]))
                p.s(lambda e, ssl=ssl: e.activation(
                        gsc_c[:, ssl].rearrange("p a c -> p (a c)"),
                        scrbig[:, ssl, :], AF.Exp),
                    waits=wt_e, incs=[(ev2, 1)])
                ev2_gsc[q] = ev2.n
                for cc in range(16):
                    col = q * 16 + cc
                    tsl = col % 2
                    wt_t = [(ev2, ev2_gsc[q])]
                    if col >= 2:
                        wt_t.append((vv2, vv2_tr[col - 2]))
                    p.t(lambda e, ssl=ssl, cc=cc, tsl=tsl: e.transpose(
                            pE[:, tsl, 0:128], gsc_c[:, ssl, cc, :], identb[:]),
                        waits=wt_t, incs=[(mm2, 1)])
                    mm2_t = mm2.n
                    p.v(lambda e, col=col, tsl=tsl: e.tensor_copy(arev_all[:, col, :],
                                                                  pE[:, tsl, 0:128]),
                        waits=[(mm2, mm2_t)], incs=[(vv2, 1)])
                    vv2_tr[col] = vv2.n
                mm2_tr[q] = mm2.n
                yield
        p2_state = {}
        p2prep_gen = p2prep_units()
        vv_grp = {}; ev_chunk = {}; mm_chunk = {}
        for tc in range(32):
            slot = tc % 2
            wt = [(vv, vv_p0)] if tc < 2 else []
            if tc >= 2:
                wt = [(mm, mm_chunk[tc - 2]), (ev, ev_chunk[tc - 2])]
            src = din["xt"][:, tc * 512:(tc + 1) * 512].rearrange("(dk p) t -> p dk t", p=128)
            p.g(lambda e, d=xbf[:, slot], s=src: e.dma_start(out=d, in_=s),
                waits=wt, incs=[(dg, 16)])
            dg_c = dg.n
            wt_s = [(dg, dg_c)]
            if tc >= 2:
                wt_s.append((mm, mm_chunk[tc - 2]))
            p.s(lambda e, sl=slot: e.activation(
                    xsq[:, sl].rearrange("p a b -> p (a b)"),
                    xbf[:, sl].rearrange("p a b -> p (a b)"), AF.Square),
                waits=wt_s, incs=[(ev, 1)])
            ev_chunk[tc] = ev.n
            for jj in range(4):
                gidx = tc * 4 + jj
                psl = gidx % 2
                wt_t = [(dg, dg_c), (ev, ev_chunk[tc])]
                if gidx < 2:
                    wt_t.append((vv, vv_p0))
                if gidx >= 2:
                    wt_t.append((vv, vv_grp[gidx - 2]))
                def mP(e, sl=slot, jj=jj, psl=psl):
                    for dk in range(8):
                        ins = e.matmul(pP[psl][:, 0:18], xbf[:, sl, dk, jj * 128:(jj + 1) * 128],
                                       statW[:, dk, :], start=(dk == 0), stop=(dk == 7))
                    return ins
                p.t(mP, waits=wt_t)
                def mQ(e, sl=slot, jj=jj, psl=psl):
                    for dk in range(8):
                        ins = e.matmul(pQ[psl][:, 0:1], xsq[:, sl, dk, jj * 128:(jj + 1) * 128],
                                       ones_c_b[:], start=(dk == 0), stop=(dk == 7))
                    return ins
                p.t(mQ, incs=[(mm, 1)])
                mm_g = mm.n
                jg = gidx
                p.v(lambda e, psl=psl, jg=jg: e.tensor_copy(P_cm[:, :, jg], pP[psl][:, 1:17]),
                    waits=[(mm, mm_g)])
                p.v(lambda e, psl=psl, jg=jg: e.tensor_copy(sumx_cm[:, jg:jg + 1], pP[psl][:, 0:1]))
                p.v(lambda e, psl=psl, jg=jg: e.tensor_copy(sumsq_cm[:, jg:jg + 1], pQ[psl][:, 0:1]),
                    incs=[(vv, 1)])
                vv_grp[gidx] = vv.n
            mm_chunk[tc] = mm.n
            if tc % 2 == 1:
                next(p2prep_gen, None)
        p.v(lambda e: e.tensor_scalar(mu_cm[:], sumx_cm[:], 1.0 / D, None, OP.mult),
            waits=[(vv, vv.n)])
        p.v(lambda e: e.tensor_scalar(scrA[:], sumsq_cm[:], 1.0 / D, None, OP.mult),
            incs=[(vv, 1)])
        vv_a = vv.n
        p.s(lambda e: e.activation(scrB[:], mu_cm[:], AF.Square), waits=[(vv, vv_a)], incs=[(ev, 1)])
        ev_a = ev.n
        p.v(lambda e: e.tensor_sub(scrA[:], scrA[:], scrB[:]), waits=[(ev, ev_a)])
        p.v(lambda e: e.tensor_scalar(scrB[:], scrA[:], EPS, None, OP.add), incs=[(vv, 1)])
        vv_b = vv.n
        p.s(lambda e: e.activation(scrB[:], scrB[:], AF.Ln), waits=[(vv, vv_b)])
        p.s(lambda e: e.activation(rs1_cm[:], scrB[:], AF.Exp, scale=-0.5), incs=[(ev, 1)])
        ev_b = ev.n
        p.v(lambda e: e.tensor_scalar(scrB[:], scrA[:], EPS, None, OP.add), waits=[(ev, ev_b)])
        p.v(lambda e: e.reciprocal(scrB[:], scrB[:]))
        p.v(lambda e: e.tensor_mul(scrB[:], scrA[:], scrB[:]))
        p.v(lambda e: e.tensor_scalar(scrB[:], scrB[:], EPS, None, OP.add), incs=[(vv, 1)])
        vv_c2 = vv.n
        p.s(lambda e: e.activation(scrB[:], scrB[:], AF.Ln), waits=[(vv, vv_c2)])
        p.s(lambda e: e.activation(rs2_cm[:], scrB[:], AF.Exp, scale=-0.5), incs=[(ev, 1)])
        ev_c = ev.n
        p.v(lambda e: e.scalar_tensor_tensor(q2_cm[:], rs1_cm[:], dtm_bc[:, 0:1], rs2_cm[:],
                                             OP.mult, OP.mult), waits=[(ev, ev_c)])
        p.v(lambda e: e.tensor_mul(xnq_cm[:], mu_cm[:], rs1_cm[:]), incs=[(vv, 1)])
        vv_fin = vv.n
        for s_ in range(S):
            p.v(lambda e, s_=s_: e.scalar_tensor_tensor(scrA[:], mu_cm[:], negsw_bc[:, s_:s_ + 1],
                                                        P_cm[:, s_, :], OP.mult, OP.add),
                waits=[(vv, vv_fin)] if s_ == 0 else [])
            p.v(lambda e: e.tensor_mul(scrA[:], scrA[:], q2_cm[:]))
            p.v(lambda e, s_=s_: e.tensor_scalar(scrA[:], scrA[:], cbm_bc[:, s_:s_ + 1], None, OP.add))
            p.v(lambda e, s_=s_: e.tensor_mul(B_cm[:, s_, :], scrA[:], mask_sb[:]),
                incs=[(vv, 1)])
        vv_bcm = vv.n
        p.t(lambda e: e.transpose(pT[0:16, 0:128], rs1_cm[:, JOWN0:J], identf[:]),
            waits=[(vv, vv_fin)], incs=[(mm, 1)])
        mm_t1 = mm.n
        p.v(lambda e: e.tensor_copy(t2f[:, 0, :], pT[0:16, 0:128]), waits=[(mm, mm_t1)],
            incs=[(vv, 1)])
        vv_t1 = vv.n
        p.t(lambda e: e.transpose(pT[0:16, 128:256], xnq_cm[:, JOWN0:J], identf[:]),
            waits=[(vv, vv_t1)], incs=[(mm, 1)])
        mm_t2 = mm.n
        p.v(lambda e: e.tensor_copy(t2f[:, 1, :], pT[0:16, 128:256]), waits=[(mm, mm_t2)],
            incs=[(vv, 1)])
        vv_t2 = vv.n
        p.sp(lambda e: e.dma_start(out=dram_ap(rsq_d.ap(), 0, [[128, 16], [1, 128]]),
                                   in_=t2f[:, 0, :]),
             waits=[(vv, vv_t2)], incs=[(ds, 16)])
        p.sp(lambda e: e.dma_start(out=dram_ap(rsq_d.ap(), T, [[128, 16], [1, 128]]),
                                   in_=t2f[:, 1, :]), incs=[(ds, 16)])
        p.t(lambda e: e.transpose(pT[0:16, 256:384], rs2_cm[:, JOWN0:J], identf[:]),
            waits=[(vv, vv_t2)], incs=[(mm, 1)])
        mm_t3 = mm.n
        p.v(lambda e: e.tensor_copy(t2f[:, 2, :], pT[0:16, 256:384]), waits=[(mm, mm_t3)],
            incs=[(vv, 1)])
        vv_t3 = vv.n
        p.sp(lambda e: e.dma_start(out=dram_ap(rsq_d.ap(), 2 * T, [[128, 16], [1, 128]]),
                                   in_=t2f[:, 2, :]),
             waits=[(vv, vv_t3)], incs=[(ds, 16)])
        if debug:
            p.sp(lambda e: e.dma_start(out=dbg["dbg_bcm"], in_=B_cm[:]),
                 waits=[(vv, vv_bcm)], incs=[(ds, 16)])
            p.sp(lambda e: e.dma_start(out=dbg["dbg_mu"], in_=mu_cm[:]), incs=[(ds, 16)])
            p.sp(lambda e: e.dma_start(out=dbg["dbg_rs1"], in_=rs1_cm[:]), incs=[(ds, 16)])
            p.sp(lambda e: e.dma_start(out=dbg["dbg_q2"], in_=q2_cm[:]), incs=[(ds, 16)])
            p.sp(lambda e: e.dma_start(out=dbg["dbg_pcm"], in_=P_cm[:]), incs=[(ds, 16)])
        p.sp(lambda e: e.nop(), waits=[(ds, ds.n)])

    if max_phase < 2:
        p.run(nc)
        return nc, din, outT_d, dbg

    _phx.close()
    loc_all = _ph.enter_context(nc.sbuf_tensor("loc_all", [128, 8, S, 128], BF16))
    S_own = _ph.enter_context(nc.sbuf_tensor("S_own", [128, 8, S, 16], F32))
    S_bf = _ph.enter_context(nc.sbuf_tensor("S_bf", [128, 8, S, 16], BF16))
    x_meanT = _ph.enter_context(nc.sbuf_tensor("x_meanT", [S, T], F32))
    onetile = _ph.enter_context(nc.sbuf_tensor("onetile", [128, 128], F32))
    rep_v = _ph.enter_context(nc.sbuf_tensor("rep_v", [128, 128], F32))
    rep_g = _ph.enter_context(nc.sbuf_tensor("rep_g", [128, 128], F32))
    sall_v = _ph.enter_context(nc.sbuf_tensor("sall_v", [128, 128], F32))
    sall_g = _ph.enter_context(nc.sbuf_tensor("sall_g", [128, 128], F32))
    kpadf = _ph.enter_context(nc.sbuf_tensor("kpadf", [1, 2, 255], F32))
    kpadb = _ph.enter_context(nc.sbuf_tensor("kpadb", [1, 2, 255], BF16))
    k2sb = _ph.enter_context(nc.sbuf_tensor("k2sb", [128, 2, 255], BF16))
    hsb = _ph.enter_context(nc.sbuf_tensor("hsb", [128, 2, 128], BF16))
    msb = _ph.enter_context(nc.sbuf_tensor("msb", [128, 2, 128], BF16))
    esb = _ph.enter_context(nc.sbuf_tensor("esb", [128, 2, 16], BF16))
    t2b = _ph.enter_context(nc.sbuf_tensor("t2b", [16, 16, 128], F32))
    _php1.close()
    pA = _ph.enter_context(nc.psum_tensor("pA", [128, 512], F32))
    pB = _ph.enter_context(nc.psum_tensor("pB", [128, 512], F32))
    pF = _ph.enter_context(nc.psum_tensor("pF", [128, 512], F32))
    pC = _ph.enter_context(nc.psum_tensor("pC", [128, 512], F32))
    pD = _ph.enter_context(nc.psum_tensor("pD", [128, 512], F32))
    if True:
        gp = sm["pa"]
        p.v(lambda e: e.memset(onetile[:], 1.0))
        for _ in p2prep_gen:
            pass
        ev2_apow = p2_state["ev2_apow"]
        pLOC = (pB, pF)
        vv2_loc = {}
        mm2_locmm = {}
        for col in range(128):
            dk, s_ = col // 16, col % 16
            lsl = col % 2
            wt_l = [(vv2, vv2_tr[col])]
            if col == 0:
                wt_l.append((vv, vv_bcm))
            if col >= 2:
                wt_l.append((vv2, vv2_loc[col - 2]))
            p.t(lambda e, col=col, lsl=lsl, s_=s_: e.matmul(
                    pLOC[lsl][:, 0:128], arev_all[:, col, :], B_cm[:, s_, :],
                    start=True, stop=True),
                waits=wt_l, incs=[(mm2, 1)])
            mm2_locmm[col] = mm2.n
            mm2_l = mm2.n
            p.v(lambda e, dk=dk, s_=s_, lsl=lsl: e.tensor_copy(loc_all[:, dk, s_, :],
                                                               pLOC[lsl][:, 0:128]),
                waits=[(mm2, mm2_l)], incs=[(vv2, 1)])
            vv2_loc[col] = vv2.n
        vv2_locend = vv2.n
        for col in range(128):
            dk, s_ = col // 16, col % 16
            acol = dk * 16 + s_
            if True:
                em, sem_e = p.v, vv
                repb, sallb = (rep_v, sall_v) if col % 2 == 0 else (rep_g, sall_g)
            wt0 = [(vv2, vv2_loc[col])]
            if col < 2:
                wt0.append((ev, ev_a128))
            aap = a128fat[:, acol:acol + 1]
            abc = dataclasses.replace(aap, ap=[aap.ap[0], [0, 128]])
            em(lambda e, repb=repb, abc=abc: e.tensor_tensor(
                    repb[:], onetile[:], abc, OP.mult),
               waits=wt0 + [(sem_e, sem_e.n)], incs=[(sem_e, 1)])
            em(lambda e, repb=repb, sallb=sallb, dk=dk, s_=s_: e.tensor_tensor_scan(
                    sallb[:], repb[:], loc_all[:, dk, s_, :], 0.0, OP.mult, OP.add),
               waits=[(sem_e, sem_e.n)], incs=[(sem_e, 1)])
            em(lambda e, sallb=sallb, dk=dk, s_=s_: e.tensor_copy(
                    S_own[:, dk, s_, :], sallb[:, 111:127]),
               waits=[(sem_e, sem_e.n)], incs=[(sem_e, 1)])
        vv_scans = vv.n; gp_scans = gp.n
        p.v(lambda e: e.tensor_scalar(S_bf[:], S_own[:], 1.0 / D, None, OP.mult),
            waits=[(vv, vv_scans), (gp, gp_scans), (vv, vv.n)], incs=[(vv, 1)])
        vv_sbf = vv.n
        st_kp = {}
        for s_ in range(S):
            def mK(e, s_=s_):
                for dk in range(8):
                    ins = e.matmul(pC[0:1, 0:128], ones_c_b[:], apowc[:, dk * 16 + s_, :],
                                   start=(dk == 0), stop=(dk == 7))
                return ins
            wt_mk = [(ev2, ev2_apow)]
            if s_ == 0:
                wt_mk.append((vv, vv_bcm))
            if s_ >= 1:
                wt_mk.append((vv2, vv2_kprev))
            p.t(mK, waits=wt_mk, incs=[(mm2, 1)])
            mm2_k = mm2.n
            ksl = s_ % 2
            wt_kp = [(mm2, mm2_k)]
            if s_ >= 2:
                wt_kp.append((st, st_kp[s_ - 2]))
            p.v(lambda e, ksl=ksl: e.memset(kpadf[:, ksl, :], 0.0), waits=wt_kp,
                incs=[(vv2, 1)])
            p.v(lambda e, ksl=ksl: e.memset(kpadf[0:1, ksl, 127:128], 1.0),
                waits=[(vv2, vv2.n)], incs=[(vv2, 1)])
            p.v(lambda e, ksl=ksl: e.tensor_scalar(kpadf[0:1, ksl, 128:255], pC[0:1, 0:127],
                                                   1.0 / D, None, OP.mult),
                waits=[(vv2, vv2.n)], incs=[(vv2, 1)])
            p.v(lambda e, ksl=ksl: e.tensor_copy(kpadb[:, ksl, :], kpadf[:, ksl, :]),
                waits=[(vv2, vv2.n)], incs=[(vv2, 1)])
            vv2_k = vv2.n
            p.t(lambda e, ksl=ksl: e.matmul(pD[:, 0:255], ones_r_b[:], kpadb[0:1, ksl, :],
                                            start=True, stop=True),
                waits=[(vv2, vv2_k)], incs=[(mm2, 1)])
            mm2_k2 = mm2.n
            p.v(lambda e, ksl=ksl: e.tensor_copy(k2sb[:, ksl, :], pD[:, 0:255]),
                waits=[(mm2, mm2_k2)], incs=[(vv2, 1)])
            vv2_k2 = vv2.n
            p.sp(lambda e, s_=s_, ksl=ksl: e.dma_start(
                    out=kpad2_d.ap()[s_], in_=k2sb[:, ksl, :]),
                 waits=[(vv2, vv2_k2)], incs=[(st, 16)])
            st_kp[s_] = st.n
            vv2_kprev = vv2.n
        st_kpad = st.n
        for s_ in range(S):
            sl = s_ % 2
            wt_h = [(st, st_kpad)]
            if s_ >= 2:
                wt_h.append((mm2, mm2.n))
            p.sp(lambda e, s_=s_, sl=sl: e.dma_start(
                    out=hsb[:, sl, :],
                    in_=dram_ap(kpad2_d.ap(), s_ * 128 * 255, [[256, 128], [1, 128]])),
                 waits=wt_h, incs=[(ds, 16)])
            ds_h = ds.n
            wt_j = [(ds, ds_h)]
            if s_ >= 1:
                wt_j.append((vv2, vv2.n))
            p.t(lambda e, sl=sl: e.matmul(pA[:, 0:128], jrevb[:], hsb[:, sl, :],
                                          start=True, stop=True),
                waits=wt_j, incs=[(mm2, 1)])
            mm2_j = mm2.n
            p.v(lambda e, sl=sl: e.tensor_copy(msb[:, sl, :], pA[:, 0:128]),
                waits=[(mm2, mm2_j)], incs=[(vv2, 1)])
            vv2_m = vv2.n
            def mconv(e, s_=s_, sl=sl):
                e.matmul(pB[:, 256:272], msb[:, sl, :], B_cm[:, s_, 112:128],
                         start=True, stop=False)
                for dk in range(8):
                    ins = e.matmul(pB[:, 256:272], apowc[:, dk * 16 + s_, :], S_bf[:, dk, s_, :],
                                   start=False, stop=(dk == 7))
                return ins
            p.t(mconv, waits=[(vv2, vv2_m), (vv, vv_sbf)], incs=[(mm2, 1)])
            mm2_c = mm2.n
            p.v(lambda e, sl=sl: e.tensor_copy(esb[:, sl, :], pB[:, 256:272]),
                waits=[(mm2, mm2_c)], incs=[(vv2, 1)])
            vv2_e = vv2.n
            p.t(lambda e, sl=sl: e.transpose(pE[:, 0, 0:16].rearrange("p a -> p a") if False else pE[0:16, 1, 0:128], esb[:, sl, :], identb[:]),
                waits=[(vv2, vv2_e)], incs=[(mm2, 1)])
            mm2_t = mm2.n
            p.v(lambda e, s_=s_: e.tensor_copy(t2b[:, s_, :], pE[0:16, 1, 0:128]),
                waits=[(mm2, mm2_t)], incs=[(vv2, 1)])
            vv2_t = vv2.n
            p.sp(lambda e, s_=s_: e.dma_start(
                    out=dram_ap(xm_d.ap(), s_ * T, [[128, 16], [1, 128]]),
                    in_=t2b[:, s_, :]),
                 waits=[(vv2, vv2_t)], incs=[(st, 16)])
        st_xm = st.n
        p.sp(lambda e: e.dma_start(out=x_meanT[:], in_=xm_d.ap()),
             waits=[(st, st_xm)], incs=[(ds, 16)])
        ds_xm = ds.n
        p.v(lambda e: e.tensor_copy(x_meanTb[:], x_meanT[:]), waits=[(ds, ds_xm)], incs=[(vv, 1)])
        if debug:
            p.sp(lambda e: e.dma_start(out=dbg["dbg_xm"], in_=x_meanT[:]),
                 waits=[(ds, ds_xm)], incs=[(ds, 16)])
            p.sp(lambda e: e.dma_start(out=dbg["dbg_sown"], in_=S_own[:]), incs=[(ds, 16)])
        p.sp(lambda e: e.nop(), waits=[(ds, ds.n)])
        p.run(nc)
        _ph.close()

    if max_phase < 3:
        return nc, din, outT_d, dbg

    xnT = sb("xnT", (128, 8, T), BF16)
    outT = sb("outT_sb", (128, 8, T), F32)
    yT_guard = nc.sbuf_tensor("yT", [128, 8, T], BF16)
    yT = yT_guard.__enter__()

    _ph = ExitStack()
    xf32 = _ph.enter_context(nc.sbuf_tensor("xf32", [128, 2, 8, 256], F32))
    rsq_sb = _ph.enter_context(nc.sbuf_tensor("rsq_sb", [1, 2, 3, 256], F32))
    rs2bc = _ph.enter_context(nc.sbuf_tensor("rs2bc", [128, T], BF16))
    ccol = _ph.enter_context(nc.sbuf_tensor("ccol", [128, 4], F32))
    zt1 = _ph.enter_context(nc.sbuf_tensor("zt1", [128, 2, 512], BF16))
    lgt = _ph.enter_context(nc.sbuf_tensor("lgt", [128, 8], F32))
    lgt2 = _ph.enter_context(nc.sbuf_tensor("lgt2", [128, 8], F32))
    m1c = _ph.enter_context(nc.sbuf_tensor("m1c", [128, 4], F32))
    combt = _ph.enter_context(nc.sbuf_tensor("combt", [128, 8], BF16))
    xnscr = _ph.enter_context(nc.sbuf_tensor("xnscr", [128, 2, 256], F32))
    wpb = _ph.enter_context(nc.sbuf_tensor("wpb", [128, 2, 8, 128], BF16))
    wzb = _ph.enter_context(nc.sbuf_tensor("wzb", [128, 2, 8, 128], BF16))
    zscr = _ph.enter_context(nc.sbuf_tensor("zscr", [128, 2, 512], BF16))
    yscr = _ph.enter_context(nc.sbuf_tensor("yscr", [128, 2, 512], F32))
    qA = _ph.enter_context(nc.psum_tensor("qA", [128, 512], F32))
    qB = _ph.enter_context(nc.psum_tensor("qB", [128, 512], F32))
    qC = _ph.enter_context(nc.psum_tensor("qC", [128, 512], F32))
    qD = _ph.enter_context(nc.psum_tensor("qD", [128, 512], F32))
    qE = _ph.enter_context(nc.psum_tensor("qE", [128, 512], F32))
    qF = _ph.enter_context(nc.psum_tensor("qF", [128, 512], F32))
    qG = _ph.enter_context(nc.psum_tensor("qG", [128, 256], BF16))
    if True:
        p = Prog()
        vv_tc = {}; mm_tc = {}; vv_rs2bc = {}
        for tc in range(8):
            sl = tc % 2
            wt = []
            if tc >= 2:
                wt = [(vv, vv_tc[tc - 2]), (mm, mm_tc[tc - 2])]
            src = din["xt"][:, L - T + tc * 256: L - T + (tc + 1) * 256].rearrange(
                "(dk p) t -> p dk t", p=128)
            p.sp(lambda e, d=xf32[:, sl], s=src: e.dma_start(out=d, in_=s),
                 waits=wt, incs=[(ds, 16)])
            for r_ in range(3):
                p.sp(lambda e, sl=sl, r_=r_, tc=tc: e.dma_start(
                        out=rsq_sb[:, sl, r_, :],
                        in_=rsq_d.ap()[r_:r_ + 1, tc * 256:(tc + 1) * 256]),
                     incs=[(ds, 16)])
            ds_x = ds.n
            p.t(lambda e, sl=sl: e.matmul(qA[:, 0:256], ones_r_f[:], rsq_sb[0:1, sl, 0, :],
                                          start=True, stop=True),
                waits=[(ds, ds_x), (vv, vv_tc[tc - 1] if tc >= 1 else 0)])
            p.t(lambda e, sl=sl: e.matmul(qA[:, 256:512], ones_r_f[:], rsq_sb[0:1, sl, 1, :],
                                          start=True, stop=True), incs=[(mm, 1)])
            mm_b = mm.n
            p.t(lambda e, sl=sl: e.matmul(qC[:, 0:256], ones_r_f[:], rsq_sb[0:1, sl, 2, :],
                                          start=True, stop=True),
                waits=[(vv, vv_rs2bc[tc - 1]) if tc >= 1 else (vv, 0)], incs=[(mm, 1)])
            mm_r2 = mm.n
            p.v(lambda e, tc=tc: e.tensor_copy(rs2bc[:, tc * 256:(tc + 1) * 256], qC[:, 0:256]),
                waits=[(mm, mm_r2)], incs=[(vv, 1)])
            vv_rs2bc[tc] = vv.n
            for dk in range(8):
                p.v(lambda e, sl=sl, dk=dk: e.tensor_mul(xnscr[:, sl, :], xf32[:, sl, dk, :],
                                                         qA[:, 0:256]),
                    waits=[(mm, mm_b)] if dk == 0 else [])
                p.v(lambda e, sl=sl, dk=dk, tc=tc: e.tensor_sub(
                        xnT[:, dk, tc * 256:(tc + 1) * 256], xnscr[:, sl, :], qA[:, 256:512]))
                p.v(lambda e, sl=sl, dk=dk, tc=tc: e.tensor_add(
                        outT[:, dk, tc * 256:(tc + 1) * 256], xf32[:, sl, dk, :],
                        xnT[:, dk, tc * 256:(tc + 1) * 256]),
                    incs=[(vv, 1)])
            for h in range(2):
                tt = tc * 2 + h
                jcol = JOWN0 + tt
                def mg(e, sl=sl, h=h):
                    for dk in range(8):
                        ins = e.matmul(qB[:, 0:8], xf32[:, sl, dk, h * 128:(h + 1) * 128],
                                       gate_f[:, dk, :], start=(dk == 0), stop=(dk == 7))
                    return ins
                p.t(mg, waits=[(vv, vv.n)], incs=[(mm, 1)])
                mm_gx = mm.n
                def vc(fn, extra=()):
                    p.v(fn, waits=[(vv, vv.n)] + list(extra), incs=[(vv, 1)])
                p.v(lambda e, jcol=jcol: e.tensor_scalar(lgt2[:], sg_bc[:],
                                                         xnq_cm[:, jcol:jcol + 1], None, OP.mult),
                    waits=[(mm, mm_gx)], incs=[(vv, 1)])
                vc(lambda e, jcol=jcol: e.scalar_tensor_tensor(
                        lgt[:], qB[:, 0:8], rs1_cm[:, jcol:jcol + 1], lgt2[:],
                        OP.mult, OP.subtract))
                vc(lambda e: e.reduce_max(m1c[:, 0:1], lgt[:], axis=AX.X))
                vc(lambda e: e.tensor_scalar(lgt2[:], lgt[:], m1c[:, 0:1], None, OP.is_ge))
                vc(lambda e: e.scalar_tensor_tensor(lgt2[:], lgt2[:], NEG_BIG, lgt[:],
                                                    OP.mult, OP.add))
                vc(lambda e: e.reduce_max(m1c[:, 1:2], lgt2[:], axis=AX.X))
                vc(lambda e: e.tensor_scalar(m1c[:, 2:3], m1c[:, 0:1], -1.0, None, OP.mult))
                vv_r = vv.n
                p.s(lambda e: e.activation(lgt2[:], lgt[:], AF.Exp, bias=m1c[:, 2:3]),
                    waits=[(vv, vv_r)], incs=[(ev, 1)])
                p.s(lambda e: e.activation(m1c[:, 3:4], m1c[:, 1:2], AF.Exp, bias=m1c[:, 2:3]),
                    waits=[(ev, ev.n)], incs=[(ev, 1)])
                ev_r = ev.n
                p.v(lambda e: e.tensor_scalar(m1c[:, 3:4], m1c[:, 3:4], 1.0, None, OP.add),
                    waits=[(ev, ev_r)], incs=[(vv, 1)])
                vc(lambda e: e.reciprocal(m1c[:, 3:4], m1c[:, 3:4]))
                vc(lambda e: e.tensor_scalar(lgt[:], lgt[:], m1c[:, 1:2], None, OP.is_ge))
                vc(lambda e: e.scalar_tensor_tensor(lgt2[:], lgt2[:], m1c[:, 3:4], lgt[:],
                                                    OP.mult, OP.mult))
                vc(lambda e: e.tensor_copy(combt[:], lgt2[:]))
                vv_cb = vv.n
                p.t(lambda e: e.transpose(qG[0:8, 0:128], combt[:], identb[:]),
                    waits=[(vv, vv_cb)], incs=[(mm, 1)])
                mm_tr = mm.n
                p.v(lambda e, tt=tt: e.tensor_copy(comb8[:, tt * 128:(tt + 1) * 128],
                                                   qG[0:8, 0:128]),
                    waits=[(mm, mm_tr)], incs=[(vv, 1)])
            vv_tc[tc] = vv.n
            mm_tc[tc] = mm.n
        vv_p3a = vv.n
        if debug:
            p.sp(lambda e: e.dma_start(out=dbg["dbg_xnt"], in_=xnT[:]),
                 waits=[(vv, vv_p3a)], incs=[(ds, 16)])
            p.sp(lambda e: e.dma_start(out=dbg["dbg_comb8"], in_=comb8[:]), incs=[(ds, 16)])
        pP3 = (qC, qD); pZ3 = (qE, qF); pY3 = (qA, qB)
        qH = qG
        vv_rs2 = vv_p3a
        mm_chi = {}; vv_y = {}
        for chi in range(8):
            wsl = chi % 2
            wt_w = []
            if chi >= 2:
                wt_w = [(mm, mm_chi[chi - 2])]
            srcp = din["w_in_t"][:, chi * 128:(chi + 1) * 128].rearrange(
                "(dk p) c -> p dk c", p=128)
            srcz = din["w_in_t"][:, D + chi * 128: D + (chi + 1) * 128].rearrange(
                "(dk p) c -> p dk c", p=128)
            p.g(lambda e, d=wpb[:, wsl], s=srcp: e.dma_start(out=d, in_=s),
                waits=wt_w, incs=[(dg, 16)])
            p.g(lambda e, d=wzb[:, wsl], s=srcz: e.dma_start(out=d, in_=s), incs=[(dg, 16)])
            dg_w = dg.n
            wt_cb2 = [(dg, dg_w)]
            if chi == 0:
                wt_cb2.append((vv, vv_rs2))
            if chi >= 2:
                wt_cb2.append((vv, vv_y[(chi - 2) * 4 + 3]))
            def mcb2(e, wsl=wsl):
                for dk in range(8):
                    e.matmul(qB[:, 0:1], wpb[:, wsl, dk, :], lnbcol[:, dk:dk + 1],
                             start=(dk == 0), stop=(dk == 7))
                for dk in range(8):
                    ins = e.matmul(qB[:, 1:2], wzb[:, wsl, dk, :], lnbcol[:, dk:dk + 1],
                                   start=(dk == 0), stop=(dk == 7))
                return ins
            p.t(mcb2, waits=wt_cb2, incs=[(mm, 1)])
            mm_cb2 = mm.n
            p.v(lambda e, chi=chi: e.tensor_scalar(ccol[:, 0:1], qB[:, 0:1],
                                                   binfat[:, chi:chi + 1], None, OP.add),
                waits=[(mm, mm_cb2)], incs=[(vv, 1)])
            p.v(lambda e, chi=chi: e.tensor_scalar(ccol[:, 1:2], qB[:, 1:2],
                                                   binfat[:, 8 + chi:9 + chi], None, OP.add),
                waits=[(vv, vv.n)], incs=[(vv, 1)])
            for dk in range(8):
                p.v(lambda e, wsl=wsl, dk=dk: e.tensor_scalar(
                        wpb[:, wsl, dk, :], wpb[:, wsl, dk, :], gfat[:, dk:dk + 1],
                        None, OP.mult))
                p.v(lambda e, wsl=wsl, dk=dk: e.tensor_scalar(
                        wzb[:, wsl, dk, :], wzb[:, wsl, dk, :], gfat[:, dk:dk + 1],
                        None, OP.mult))
            p.v(lambda e: e.nop(), incs=[(vv, 1)])
            vv_scaled = vv.n
            for tc in range(4):
                gi = chi * 4 + tc
                psl = gi % 2
                wt_t = [(vv, vv_scaled)]
                if gi >= 2:
                    wt_t.append((vv, vv_y[gi - 2]))
                def mxz(e, wsl=wsl, tc=tc, psl=psl, chi=chi):
                    for dk in range(8):
                        e.matmul(pP3[psl][:, 0:512], wpb[:, wsl, dk, :],
                                 xnT[:, dk, tc * 512:(tc + 1) * 512],
                                 start=(dk == 0), stop=(dk == 7))
                    for dk in range(8):
                        e.matmul(pZ3[psl][:, 0:512], wzb[:, wsl, dk, :],
                                 xnT[:, dk, tc * 512:(tc + 1) * 512],
                                 start=(dk == 0), stop=(dk == 7))
                    return e.matmul(pY3[psl][:, 0:512], wct_sb[:, chi * 128:(chi + 1) * 128],
                                    x_meanTb[:, tc * 512:(tc + 1) * 512],
                                    start=True, stop=True)
                p.t(mxz, waits=wt_t, incs=[(mm, 1)])
                mm_xz = mm.n
                p.v(lambda e, psl=psl, tc=tc: e.tensor_mul(
                        zt1[:, psl, :], pZ3[psl][:, 0:512], rs2bc[:, tc * 512:(tc + 1) * 512]),
                    waits=[(mm, mm_xz)], incs=[(vv, 1)])
                p.s(lambda e, psl=psl: e.activation(zscr[:, psl, :], zt1[:, psl, :], AF.Silu,
                                                    bias=ccol[:, 1:2]),
                    waits=[(vv, vv.n)], incs=[(ev, 1)])
                ev_z = ev.n
                p.v(lambda e, psl=psl, tc=tc: e.tensor_mul(
                        yscr[:, psl, :], pP3[psl][:, 0:512], rs2bc[:, tc * 512:(tc + 1) * 512]))
                p.v(lambda e, psl=psl: e.tensor_scalar(
                        yscr[:, psl, :], yscr[:, psl, :], ccol[:, 0:1], None, OP.add))
                p.v(lambda e, psl=psl, chi=chi: e.scalar_tensor_tensor(
                        yscr[:, psl, :], yscr[:, psl, :], dfat[:, chi:chi + 1],
                        pY3[psl][:, 0:512], OP.mult, OP.add))
                p.v(lambda e, psl=psl, chi=chi, tc=tc: e.tensor_mul(
                        yT[:, chi, tc * 512:(tc + 1) * 512], yscr[:, psl, :], zscr[:, psl, :]),
                    waits=[(ev, ev_z)], incs=[(vv, 1)])
                vv_y[gi] = vv.n
            mm_chi[chi] = mm.n
        if debug:
            p.sp(lambda e: e.dma_start(out=dbg["dbg_yt"], in_=yT[:]),
                 waits=[(vv, vv.n)], incs=[(ds, 16)])
        p.sp(lambda e: e.nop(), waits=[(ds, ds.n)])
        p.run(nc)
        _ph.close()

    if max_phase < 4:
        return nc, din, outT_d, dbg

    _ph = ExitStack()
    woutb = _ph.enter_context(nc.sbuf_tensor("woutb", [128, 8, D], BF16))
    rA = _ph.enter_context(nc.psum_tensor("rA", [128, 512], F32))
    rB = _ph.enter_context(nc.psum_tensor("rB", [128, 512], F32))
    if True:
        p = Prog()
        p.g(lambda e: e.dma_start(out=woutb[:],
                                  in_=din["w_out_t"].rearrange("(dik p) do -> p dik do", p=128)),
            incs=[(dg, 16)])
        dg_w = dg.n
        rP = (rA, rB)
        vv_o = {}
        for dot in range(8):
            for tc in range(4):
                gi = dot * 4 + tc
                psl = gi % 2
                wt_t = [(dg, dg_w)]
                if gi >= 2:
                    wt_t.append((vv, vv_o[gi - 2]))
                def mwo(e, dot=dot, tc=tc, psl=psl):
                    for dik in range(8):
                        ins = e.matmul(rP[psl][:, 0:512], woutb[:, dik, dot * 128:(dot + 1) * 128],
                                       yT[:, dik, tc * 512:(tc + 1) * 512],
                                       start=(dik == 0), stop=(dik == 7))
                    return ins
                p.t(mwo, waits=wt_t, incs=[(mm, 1)])
                mm_o = mm.n
                p.v(lambda e, dot=dot, tc=tc, psl=psl: e.scalar_tensor_tensor(
                        outT[:, dot, tc * 512:(tc + 1) * 512], rP[psl][:, 0:512],
                        boutfat[:, dot:dot + 1], outT[:, dot, tc * 512:(tc + 1) * 512],
                        OP.add, OP.add),
                    waits=[(mm, mm_o)], incs=[(vv, 1)])
                vv_o[gi] = vv.n
        p.run(nc)
        _ph.close()

    yT_guard.__exit__(None, None, None)

    if max_phase < 5:
        return nc, din, outT_d, dbg

    _ph = ExitStack()
    hT = _ph.enter_context(nc.sbuf_tensor("hT", [128, 8, T], BF16))
    wgb = _ph.enter_context(nc.sbuf_tensor("wgb", [128, 2, 8, 512], BF16))
    wub = _ph.enter_context(nc.sbuf_tensor("wub", [128, 2, 8, 512], BF16))
    wdb = _ph.enter_context(nc.sbuf_tensor("wdb", [128, 2, 8, 128], BF16))
    cbc = _ph.enter_context(nc.sbuf_tensor("cbc", [128, 4, 512], BF16))
    sgscr = _ph.enter_context(nc.sbuf_tensor("sgscr", [128, 2, 512], BF16))
    uscr = _ph.enter_context(nc.sbuf_tensor("uscr", [128, 2, 512], BF16))
    mg0 = _ph.enter_context(nc.psum_tensor("mg0", [128, 512], F32))
    mg1 = _ph.enter_context(nc.psum_tensor("mg1", [128, 512], F32))
    mu0 = _ph.enter_context(nc.psum_tensor("mu0", [128, 512], F32))
    mu1 = _ph.enter_context(nc.psum_tensor("mu1", [128, 512], F32))
    mo0 = _ph.enter_context(nc.psum_tensor("mo0", [128, 512], F32))
    mo1 = _ph.enter_context(nc.psum_tensor("mo1", [128, 512], F32))
    mcb = _ph.enter_context(nc.psum_tensor("mcb", [128, 512], F32))
    if True:
        p = Prog()
        pg = (mg0, mg1); pu = (mu0, mu1); po = (mo0, mo1)
        wgu, wgufr, wd, wdfr = sm["wgu"], sm["wgufr"], sm["wd"], sm["wdfr"]
        pa, pb, hb = sm["pa"], sm["pb"], sm["hb"]
        gi5 = 0
        gb5 = 0
        gh = 0
        gw = 0
        evA = {}; vvA = {}; vvB = {}
        for e_ in range(E):
            for tcb in range(4):
                wt_cb = [(pb, pb.n)] if (e_ > 0 or tcb > 0) else []
                p.t(lambda e, e_=e_, tcb=tcb: e.matmul(
                        mcb[:, 0:512], selb[:, e_ * 128:(e_ + 1) * 128],
                        comb8[:, tcb * 512:(tcb + 1) * 512], start=True, stop=True),
                    waits=wt_cb, incs=[(pa, 1)])
                p.v(lambda e, tcb=tcb: e.tensor_copy(cbc[:, tcb, :], mcb[:, 0:512]),
                    waits=[(pa, pa.n)], incs=[(pb, 1)])
            pb_e = pb.n
            for q in range(4):
                eq = e_ * 4 + q
                for h in range(2):
                    wt_dma = [(wgufr, gh - 1)] if gh >= 2 else []
                    srcg = din["wg_t"][e_][:, q * 1024 + h * 512: q * 1024 + (h + 1) * 512] \
                        .rearrange("(dk p) i -> p dk i", p=128)
                    srcu = din["wu_t"][e_][:, q * 1024 + h * 512: q * 1024 + (h + 1) * 512] \
                        .rearrange("(dk p) i -> p dk i", p=128)
                    p.g(lambda e, d=wgb[:, gh % 2], s=srcg: e.dma_start(out=d, in_=s),
                        waits=wt_dma, incs=[(wgu, 16)])
                    p.g(lambda e, d=wub[:, gh % 2], s=srcu: e.dma_start(out=d, in_=s),
                        incs=[(wgu, 16)])
                    wgu_h = wgu.n
                    for icw in range(4):
                        ic = h * 4 + icw
                        for tc in range(4):
                            psl = gi5 % 2
                            wt_t = [(wgu, wgu_h)]
                            if e_ == 0 and q == 0 and h == 0 and icw == 0 and tc == 0:
                                wt_t.append((pb, pb_e))
                            if gi5 >= 2:
                                wt_t.append((ev2, evA[gi5 - 2]))
                                wt_t.append((vv2, vvA[gi5 - 2]))
                            def mA(e, wsl=gh % 2, icw=icw, tc=tc, psl=psl):
                                for dk in range(8):
                                    e.matmul(pg[psl][:, 0:512],
                                             wgb[:, wsl, dk, icw * 128:(icw + 1) * 128],
                                             xnT[:, dk, tc * 512:(tc + 1) * 512],
                                             start=(dk == 0), stop=(dk == 7))
                                for dk in range(8):
                                    ins = e.matmul(pu[psl][:, 0:512],
                                                   wub[:, wsl, dk, icw * 128:(icw + 1) * 128],
                                                   xnT[:, dk, tc * 512:(tc + 1) * 512],
                                                   start=(dk == 0), stop=(dk == 7))
                                return ins
                            last_of_half = (icw == 3 and tc == 3)
                            incs_t = [(mm2, 1)] + ([(wgufr, 1)] if last_of_half else [])
                            p.t(mA, waits=wt_t, incs=incs_t)
                            mm2_a = mm2.n
                            wt_sg = [(mm2, mm2_a)]
                            if gi5 >= 2:
                                wt_sg.append((vv2, vvA[gi5 - 2]))
                            p.s(lambda e, psl=psl: e.activation(sgscr[:, psl, :],
                                                                pg[psl][:, 0:512], AF.Silu),
                                waits=wt_sg, incs=[(ev2, 1)])
                            evA[gi5] = ev2.n
                            wt_u = [(mm2, mm2_a)]
                            if eq >= 1:
                                wt_u.append((hb, eq))
                            p.v(lambda e, psl=psl, tc=tc: e.tensor_mul(
                                    uscr[:, psl, :], pu[psl][:, 0:512], cbc[:, tc, :]),
                                waits=wt_u)
                            p.v(lambda e, psl=psl, ic=ic, tc=tc: e.tensor_mul(
                                    hT[:, ic, tc * 512:(tc + 1) * 512],
                                    sgscr[:, psl, :], uscr[:, psl, :]),
                                waits=[(ev2, evA[gi5])], incs=[(vv2, 1)])
                            vvA[gi5] = vv2.n
                            gi5 += 1
                    gh += 1
                vv2_heq = vv2.n
                for dmt in range(8):
                    wt_dma = [(wdfr, gw - 1)] if gw >= 2 else []
                    srcd = din["wd_t"][e_][q * 1024:(q + 1) * 1024, dmt * 128:(dmt + 1) * 128] \
                        .rearrange("(ik p) d -> p ik d", p=128)
                    p.g(lambda e, d=wdb[:, gw % 2], s=srcd: e.dma_start(out=d, in_=s),
                        waits=wt_dma, incs=[(wd, 16)])
                    wd_c = wd.n
                    for tc in range(4):
                        psl = gb5 % 2
                        wt_t = [(wd, wd_c)]
                        if tc == 0 and dmt == 0:
                            wt_t.append((vv2, vv2_heq))
                        if gb5 >= 2:
                            wt_t.append((vv2, vvB[gb5 - 2]))
                        def mB(e, wsl=gw % 2, tc=tc, psl=psl):
                            for ik in range(8):
                                ins = e.matmul(po[psl][:, 0:512], wdb[:, wsl, ik, :],
                                               hT[:, ik, tc * 512:(tc + 1) * 512],
                                               start=(ik == 0), stop=(ik == 7))
                            return ins
                        last_of_chunk = (tc == 3)
                        last_of_eq = (dmt == 7 and tc == 3)
                        incs_t = [(mm2, 1)] + ([(wdfr, 1)] if last_of_chunk else []) \
                            + ([(hb, 1)] if last_of_eq else [])
                        p.t(mB, waits=wt_t, incs=incs_t)
                        mm2_b = mm2.n
                        p.v(lambda e, dmt=dmt, tc=tc, psl=psl: e.tensor_add(
                                outT[:, dmt, tc * 512:(tc + 1) * 512],
                                outT[:, dmt, tc * 512:(tc + 1) * 512], po[psl][:, 0:512]),
                            waits=[(mm2, mm2_b)], incs=[(vv2, 1)])
                        vvB[gb5] = vv2.n
                        gb5 += 1
                    gw += 1
        p.run(nc)
        _ph.close()

    p = Prog()
    p.sp(lambda e: e.dma_start(out=outT_d.rearrange("(dk p) t -> p dk t", p=128),
                               in_=outT[:]),
         incs=[(st, 16)])
    p.sp(lambda e: e.nop(), waits=[(st, st.n)])
    p.run(nc)
    return nc, din, outT_d, dbg


def _stage_inputs(inputs):
    f = np.float32
    x = np.asarray(inputs["L0_attn_x"], dtype=f)
    xT = np.ascontiguousarray(x[0].T)
    W_in = np.asarray(inputs["L0_attn_W_in"], f)
    W_out = np.asarray(inputs["L0_attn_W_out"], f)
    W_B = np.asarray(inputs["L0_attn_W_B"], f)
    W_C = np.asarray(inputs["L0_attn_W_C"], f)
    A_log = np.asarray(inputs["L0_attn_A_log"], f)
    D_skip = np.asarray(inputs["L0_attn_D_skip"], f)
    log_dt = np.asarray(inputs["L0_attn_log_dt"], f)
    b_in = np.asarray(inputs["L0_attn_b_in"], f)
    b_out = np.asarray(inputs["L0_attn_b_out"], f)
    ln_g = np.asarray(inputs["L0_attn_ln_g"], f)
    ln_b = np.asarray(inputs["L0_attn_ln_b"], f)
    gate_w = np.asarray(inputs["L0_ffn_gate_w"], f)
    wg = np.asarray(inputs["L0_ffn_wg"], f)
    wu = np.asarray(inputs["L0_ffn_wu"], f)
    wd = np.asarray(inputs["L0_ffn_wd"], f)

    shared = {
        "w_in_t": np.ascontiguousarray(W_in.T),
        "w_in_upn": np.ascontiguousarray(W_in[0:D, :]),
        "w_out_t": np.ascontiguousarray(W_out.T),
        "w_b_t": np.ascontiguousarray(W_B.T),
        "w_c_t": np.ascontiguousarray(W_C.T),
        "gate_t": np.ascontiguousarray(gate_w.T),
        "wg_t": np.ascontiguousarray(wg.transpose(0, 2, 1)),
        "wu_t": np.ascontiguousarray(wu.transpose(0, 2, 1)),
        "wd_t": np.ascontiguousarray(wd.transpose(0, 2, 1)),
        "g_fat": np.ascontiguousarray(ln_g.reshape(8, 128).T),
        "b_fat": np.ascontiguousarray(ln_b.reshape(8, 128).T),
        "d_fat": np.ascontiguousarray(D_skip.reshape(8, 128).T),
        "logdt_fat": np.ascontiguousarray(log_dt.reshape(8, 128).T),
        "bout_fat": np.ascontiguousarray(b_out.reshape(8, 128).T),
        "bin_fat": np.ascontiguousarray(b_in.reshape(16, 128).T),
        "binup_col": np.ascontiguousarray(b_in[0:D].reshape(8, 128).T),
        "lnb_col": np.ascontiguousarray(ln_b.reshape(8, 128).T),
        "alog_fat": np.ascontiguousarray(
            A_log.reshape(8, 128, S).transpose(1, 0, 2).reshape(128, 128)),
    }
    in_maps = []
    for c in range(NCORE):
        pad = (NCORE - 1 - c) * T
        xt_c = np.zeros((D, L), f)
        if pad:
            xt_c[:, pad:] = xT[:, :L - pad]
        else:
            xt_c[:] = xT
        mask = np.zeros((128, J), f)
        mask[:, JOWN0 - 16 * c:] = 1.0
        in_maps.append({**shared, "xt": xt_c, "mask_cm": mask})
    return in_maps


def kernel(**inputs):
    in_maps = _stage_inputs(inputs)
    nc, din, outT_d, dbg = build(debug=False)
    res = run_bass_kernel_spmd(nc, in_maps, core_ids=list(range(NCORE)))
    outs = [np.asarray(res.results[c]["outT"]) for c in range(NCORE)]
    full = np.concatenate(outs, axis=1)
    return np.ascontiguousarray(full.T).reshape(1, L, D).astype(np.float32)
